# version 02
# baseline (speedup 1.0000x reference)
import contextlib
import numpy as np
import concourse.bass as bass
import concourse.mybir as mybir
from concourse.bass_utils import run_bass_kernel_spmd

F32 = mybir.dt.float32
BF16 = mybir.dt.bfloat16
AF = mybir.ActivationFunctionType
ALU = mybir.AluOpType

DM = 2048
HD = 128
NH = 16
HPC = 2
NCORES = 8
DFF = 8192
FC = DM // 128
TS = 256
QB = 512
EPS = 1e-5


def _split2(v):
    a = np.float64(np.float32(v))
    b = float(np.float32(np.float64(v) - a))
    return float(a), b


def _split3(v):
    a = np.float64(np.float32(v))
    r = np.float64(v) - a
    b = np.float64(np.float32(r))
    c = float(np.float32(r - b))
    return float(a), float(b), c


PI1, PI2, PI3 = _split3(np.pi)
H1, H2 = _split2(np.pi / 2)
E1, E2 = _split2(EPS)
S1, S2 = _split2(1.0 / np.sqrt(128.0))
C71, C72 = _split2(0.7)

ENGS = ("sync", "scalar", "vector", "tensor", "gpsimd")


class Sem:
    __slots__ = ("h", "n")

    def __init__(self, h):
        self.h = h
        self.n = 0


class P:

    def __init__(self, nc, es):
        self.nc = nc
        self.es = es
        self.q = {e: [] for e in ENGS}
        self._sems = {}

    def sem(self, name):
        if name.startswith("bar_"):
            name = "bar"
        elif not name.startswith("d") and name != "ccA":
            name = name[0] + "P"
        if name not in self._sems:
            self._sems[name] = Sem(self.es.enter_context(self.nc.semaphore(name)))
        return self._sems[name]

    def wait(self, eng, sem, v=None):
        val = sem.n if v is None else v
        self.q[eng].append(lambda e, s=sem.h, w=val: e.wait_ge(s, w))

    def op(self, eng, fn, inc=None, incn=1, drain=False):
        if inc is not None:
            inc.n += incn

        def run(e, fn=fn, inc=inc, incn=incn, drain=drain):
            instr = fn(e)
            if inc is not None:
                instr.then_inc(inc.h, incn)
            if drain:
                e.drain()

        self.q[eng].append(run)

    def dma(self, eng, out, in_, inc, transpose=False):
        inc.n += 16

        def run(e, out=out, in_=in_, inc=inc, transpose=transpose):
            if transpose:
                e.dma_start(out=out, in_=in_, transpose=True).then_inc(inc.h, 16)
            else:
                e.dma_start(out=out, in_=in_).then_inc(inc.h, 16)

        self.q[eng].append(run)

    def barrier(self, name):
        b = self.sem("bar_" + name)
        target = b.n + len(ENGS)
        for e in ENGS:
            self.op(e, lambda eng: eng.nop(), inc=b, incn=1)
        for e in ENGS:
            self.wait(e, b, target)

    def emit(self, block):
        for name in ENGS:
            fns = self.q[name]

            def body(eng, fns=fns):
                for f in fns:
                    f(eng)

            getattr(block, name)(body)


def build(seq=8192, taps=False):
    nts = seq // TS
    nqb = seq // QB
    rows = seq // NCORES
    tbn = rows // QB
    nkc = seq // 128
    nA = seq // 128
    w1r = DM // NCORES
    w2r = DFF // NCORES
    nfo = DFF // 128

    nc = bass.Bass("TRN2", target_bir_lowering=False)
    dram = lambda n, s, d, k: nc.dram_tensor(n, s, d, kind=k).ap()
    drams = lambda n, s, d: nc.dram_tensor(n, s, d, kind="Internal", addr_space="Shared").ap()
    IK = "ExternalOutput" if taps else "Internal"
    x = dram("x", [seq, DM], F32, "ExternalInput")
    xT = dram("xT", [DM, rows], F32, "ExternalInput")
    wproj = dram("wproj", [HPC, 4, DM, HD], F32, "ExternalInput")
    wos_d = dram("wos_d", [2 * HD, DM], F32, "ExternalInput")
    w1s = dram("w1s", [w1r, DFF], F32, "ExternalInput")
    w2s = dram("w2s", [w2r, DM], F32, "ExternalInput")
    b1 = dram("b1", [128, DFF // 128], F32, "ExternalInput")
    b2 = dram("b2", [128, FC], F32, "ExternalInput")
    nw = dram("nw", [HD, 1], F32, "ExternalInput")
    k2s = dram("k2s", [HD, seq], F32, "ExternalInput")
    sgs = dram("sgs", [HD, seq], F32, "ExternalInput")
    k2c = dram("k2c", [HD, seq], F32, "ExternalInput")
    sgc = dram("sgc", [HD, seq], F32, "ExternalInput")
    omh = dram("omh", [HD, 1], F32, "ExternalInput")
    oml = dram("oml", [HD, 1], F32, "ExternalInput")
    y = dram("y", [rows, DM], F32, "ExternalOutput")

    lnsp = dram("lnsp", [2, seq, DM], BF16, IK)
    qsp = dram("qsp", [HPC, 2, HD, seq], BF16, IK)
    csp = dram("csp", [HPC, HD, seq], F32, IK)
    ctab = dram("ctab", [HD, seq], F32, "Internal")
    stab = dram("stab", [HD, seq], F32, "Internal")
    hsp = dram("hsp", [2, nfo, 128, rows], BF16, IK)
    gsph = dram("gsph", [HPC * HD, seq], BF16, IK)
    gspl = dram("gspl", [HPC * HD, seq], BF16, IK)
    rs_in = [dram(f"rs_in{k}", [NCORES, 2 * HD, rows], F32, "Internal") for k in range(NCORES)]
    rs_out = [dram(f"rs_out{k}", [2 * HD, rows], F32, "Internal") for k in range(NCORES)]
    rs_cp = [dram(f"rs_cp{k}", [2 * HD, rows], F32, "Internal") for k in range(NCORES)]
    q1_in = dram("q1_in", [w1r, DFF], BF16, "Internal")
    q1tap = dram("q1tap", [w1r, DFF], BF16, IK) if taps else None
    q1otap = dram("q1otap", [DM, DFF], BF16, IK) if taps else None
    q1_out = drams("q1_out", [DM, DFF], BF16)
    q2_in = dram("q2_in", [w2r, DM], BF16, "Internal")
    q2_out = drams("q2_out", [DFF, DM], BF16)
    x2tap = dram("x2tap", [DM, rows], F32, IK) if taps else None
    dbgq = dram("dbgq", [1, 8], F32, IK) if taps else None
    ar8tap = dram("ar8tap", [8, 128], F32, IK) if taps else None
    rstap = dram("rstap", [NCORES, 2 * HD, rows], F32, IK) if taps else None
    rsotap = dram("rsotap", [2 * HD, rows], F32, IK) if taps else None
    rsused = dram("rsused", [FC, 128, rows], F32, IK) if taps else None
    avtap = dram("avtap", [HD, seq], F32, IK) if taps else None
    fence_in = dram("fence_in", [1, 128], F32, "Internal")
    fence_out = dram("fence_out", [8, 128], F32, "Internal")
    ar_in = dram("ar_in", [1, 128], F32, "Internal")
    ar_out = dram("ar_out", [8, 128], F32, "Internal")
    RG = [list(range(NCORES))]

    es = contextlib.ExitStack()
    with es:
        def sb(name, shape, dt=F32, st=None):
            stk = st if st is not None else es
            return stk.enter_context(nc.sbuf_tensor(name, shape, dt))

        def ps(name, shape, st=None):
            stk = st if st is not None else es
            return stk.enter_context(nc.psum_tensor(name, shape, F32))

        p = P(nc, es)
        S = p.sem

        onesb = sb("onesb", [128, 1], BF16)
        onesf = sb("onesf", [128, 1], F32)
        ones1f = sb("ones1f", [1, 128], F32)
        ident = sb("ident", [128, 128], F32)
        maskb = sb("maskb", [128, 896], F32)
        nwc = sb("nwc", [HD, 1], F32)
        omhc = sb("omhc", [HD, 1], F32)
        omlc = sb("omlc", [HD, 1], F32)
        b2c = sb("b2c", [128, FC], F32)
        c20 = sb("c20", [128, 1], F32)
        b1c = sb("b1c", [128, DFF // 128], F32)

        block = es.enter_context(nc.Block())

        dW = S("dW")
        dX = S("dX")
        dLT = S("dLT")
        dSW = S("dSW")
        dG = S("dG")
        dSY = S("dSY")
        vZ = S("vZ")


        p.op("gpsimd", lambda e: e.iota(maskb[:], pattern=[[1, 896]], base=-384,
                                        channel_multiplier=-1, allow_small_or_imprecise_dtypes=True), inc=vZ)
        p.op("gpsimd", lambda e: e.iota(ident[:], pattern=[[1, 128]], base=0,
                                        channel_multiplier=-1, allow_small_or_imprecise_dtypes=True), inc=vZ)
        p.op("vector", lambda e: e.memset(onesb[:], 1.0))
        p.op("vector", lambda e: e.memset(onesf[:], 1.0))
        p.op("vector", lambda e: e.memset(ones1f[:], 1.0))
        p.op("vector", lambda e: e.memset(c20[:], -20.0))
        p.wait("vector", vZ)
        p.op("vector", lambda e: e.tensor_single_scalar(maskb[:], maskb[:], 0.0, ALU.is_ge), drain=True)
        p.op("vector", lambda e: e.tensor_single_scalar(ident[:], ident[:], 0.0, ALU.is_equal), inc=vZ, drain=True)

        p.dma("scalar", nwc[:], nw[:, :], dW)
        p.dma("scalar", omhc[:], omh[:, :], dW)
        p.dma("scalar", omlc[:], oml[:, :], dW)
        p.dma("scalar", b2c[:], b2[:, :], dW)
        p.dma("scalar", b1c[:], b1[:, :], dW)

        vWs = S("vWs")

        stA = contextlib.ExitStack()
        xa = [sb(f"xa{i}", [128, DM], F32, stA) for i in range(2)]
        lnhb = [sb(f"lnhb{i}", [128, DM], BF16, stA) for i in range(2)]
        lnlb = [sb(f"lnlb{i}", [128, DM], BF16, stA) for i in range(2)]
        lfb = sb("lfb", [128, DM], F32, stA)
        sqd = sb("sqd", [128, DM], F32, stA)
        s1c = sb("s1c", [128, 1], F32, stA); s2c = sb("s2c", [128, 1], F32, stA)
        mcl = sb("mcl", [128, 1], F32, stA); nmc = sb("nmc", [128, 1], F32, stA)
        msc = sb("msc", [128, 1], F32, stA); vvc = sb("vvc", [128, 1], F32, stA)
        sdc = sb("sdc", [128, 1], F32, stA); rsc = sb("rsc", [128, 1], F32, stA)
        nbc = sb("nbc", [128, 1], F32, stA)
        sqA = S("sqA"); vstA = S("vstA"); sdA = S("sdA"); vrsA = S("vrsA")
        shiA = S("shiA"); vfinA = S("vfinA"); vaffA = S("vaffA")
        dXp = [S("dX0"), S("dX1")]
        dGp = [S("dGa0"), S("dGa1")]

        for i in range(nA):
            b = i % 2
            if i >= 2:
                p.wait("sync", vfinA)
                p.wait("sync", dGp[b])
            p.dma("sync", xa[b][:], x[128 * i:128 * (i + 1), :], dXp[b])
            p.wait("scalar", dXp[b])
            p.op("scalar", lambda e, b=b: e.activation(sqd[:], xa[b][:], AF.Square, accum_out=s2c[:]), inc=sqA)
            p.wait("vector", dXp[b])
            p.op("vector", lambda e, b=b: e.tensor_reduce(s1c[:], xa[b][:], axis=mybir.AxisListType.X, op=ALU.add), drain=True)
            p.wait("vector", sqA)
            p.wait("vector", shiA)
            p.op("vector", lambda e: e.tensor_single_scalar(mcl[:], s1c[:], 1.0 / DM, ALU.mult), drain=True)
            p.op("vector", lambda e: e.tensor_single_scalar(nmc[:], mcl[:], -1.0, ALU.mult), drain=True)
            p.op("vector", lambda e: e.tensor_single_scalar(msc[:], s2c[:], 1.0 / DM, ALU.mult), drain=True)
            p.op("vector", lambda e: e.scalar_tensor_tensor(vvc[:], mcl[:], nmc[:, 0:1], msc[:], ALU.mult, ALU.add), drain=True)
            p.op("vector", lambda e: e.tensor_scalar(vvc[:], vvc[:], E1, E2, ALU.add, ALU.add), inc=vstA, drain=True)
            p.wait("scalar", vstA)
            p.op("scalar", lambda e: e.activation(sdc[:], vvc[:], AF.Sqrt), inc=sdA)
            p.wait("vector", sdA)
            p.op("vector", lambda e: e.reciprocal(rsc[:], sdc[:]), drain=True)
            p.op("vector", lambda e: e.tensor_tensor(nbc[:], nmc[:], rsc[:], ALU.mult), inc=vrsA, drain=True)
            p.wait("vector", vrsA, vrsA.n)
            p.op("vector", lambda e, b=b: e.tensor_scalar(lfb[:], xa[b][:], rsc[:, 0:1], nbc[:, 0:1],
                                                          ALU.mult, ALU.add), inc=vaffA, drain=True)
            p.wait("scalar", vaffA)
            p.op("scalar", lambda e, b=b: e.copy(lnhb[b][:], lfb[:]), inc=shiA)
            p.wait("vector", shiA)
            p.op("vector", lambda e, b=b: e.tensor_tensor(lnlb[b][:], lfb[:], lnhb[b][:], ALU.subtract),
                 inc=vfinA, drain=True)
            p.wait("gpsimd", vfinA)
            p.dma("gpsimd", lnsp[0, 128 * i:128 * (i + 1), :], lnhb[b][:], dGp[b])
            p.dma("gpsimd", lnsp[1, 128 * i:128 * (i + 1), :], lnlb[b][:], dGp[b])

        p.wait("gpsimd", dGp[0])
        p.wait("gpsimd", dGp[1])
        p.barrier("A")
        stA.close()

        stB = contextlib.ExitStack()
        wqkvg = [[sb(f"w{t}{i}", [128, FC, 128], BF16, stB) for i in range(2)]
                 for t in "qkvg"]
        Kr = [sb(f"Kr{i}", [128, nkc, 128], BF16, stB) for i in range(2)]
        Vr = [sb(f"Vr{i}", [128, nkc, 128], BF16, stB) for i in range(2)]
        qhiA = [sb(f"qhiA{i}", [128, QB], BF16, stB) for i in range(2)]
        qloA = [sb(f"qloA{i}", [128, QB], BF16, stB) for i in range(2)]
        cmbA = [sb(f"cmbA{i}", [128, QB], F32, stB) for i in range(2)]
        pexp = [sb(f"pexp{i}", [128, QB], F32, stB) for i in range(2)]
        phi = [sb(f"phi{i}", [128, QB], BF16, stB) for i in range(2)]
        plo = [sb(f"plo{i}", [128, QB], BF16, stB) for i in range(2)]
        zrow = sb("zrow", [1, QB], F32, stB)
        rz = sb("rz", [1, QB], F32, stB)
        bc1 = sb("bc1", [128, QB], F32, stB)
        avn = sb("avn", [128, QB], F32, stB)
        sqb = sb("sqb", [128, QB], F32, stB)
        msqr = sb("msqr", [1, QB], F32, stB)
        rr_ = sb("rr_", [1, QB], F32, stB)
        bc2 = sb("bc2", [128, QB], F32, stB)
        gat = sb("gat", [128, QB], F32, stB)
        gth = sb("gth", [128, QB], BF16, stB)
        gtl = sb("gtl", [128, QB], BF16, stB)
        stSw = contextlib.ExitStack()
        lnT = sb("lnT", [128, 2, 2, FC, TS], BF16, stSw)
        wraw = sb("wraw", [128, FC, 128], F32, stSw)
        wscr = sb("wscr", [128, FC, 128], F32, stSw)
        tio = sb("tio", [128, TS], F32, stSw)
        ph_ = sb("ph_", [128, TS], F32, stSw)
        rA = sb("rA", [128, TS], F32, stSw)
        rB = sb("rB", [128, TS], F32, stSw)
        snb = sb("snb", [128, TS], F32, stSw)
        costile = sb("costile", [128, TS], F32, stSw)
        sintile = sb("sintile", [128, TS], F32, stSw)
        k2sT = sb("k2sT", [128, TS], F32, stSw); sgsT = sb("sgsT", [128, TS], F32, stSw)
        k2cT = sb("k2cT", [128, TS], F32, stSw); sgcT = sb("sgcT", [128, TS], F32, stSw)
        qf = sb("qf", [128, TS], F32, stSw)
        qsw = sb("qsw", [128, TS], F32, stSw)
        qr1 = sb("qr1", [128, TS], F32, stSw)
        qr2 = sb("qr2", [128, TS], F32, stSw)
        qhiT = sb("qhiT", [128, TS], BF16, stSw)
        qloT = sb("qloT", [128, TS], BF16, stSw)
        kf = sb("kf", [128, TS], F32, stSw)
        ksw = sb("ksw", [128, TS], F32, stSw)
        gsb = sb("gsb", [128, TS], F32, stSw)
        cmb = sb("cmb", [128, TS], F32, stSw)
        stBp = contextlib.ExitStack()
        P6 = [ps(f"P6_{i}", [128, QB], stBp) for i in range(6)]
        PQ = P6[0][:, 0:TS]
        PK = P6[1][:, 0:TS]
        PG = P6[2][:, 0:TS]
        PV = P6[3][:, 0:TS]

        dR = S("dR")
        dLTp = [S("dLT0"), S("dLT1")]
        dSWq = S("dSWq"); dSWk = S("dSWk")
        dGq = S("dGq")
        vTab = S("vTab"); sTab = S("sTab")
        tQd = S("tQd"); tKd = S("tKd"); tGd = S("tGd"); tVd = S("tVd")
        sQe = S("sQe"); sKe = S("sKe"); sGe = S("sGe")
        vQe = S("vQe"); vKe = S("vKe"); vGe = S("vGe"); vVe = S("vVe")


        tSc = S("tSc"); sEx = S("sEx"); vPs = S("vPs"); tPe = S("tPe")
        dQp = [S("dQA0"), S("dQA1")]
        sZe = S("sZe"); vRz = S("vRz"); tB1 = S("tB1"); sB1 = S("sB1")
        vAvn = S("vAvn"); sSq = S("sSq"); tM = S("tM"); sMs = S("sMs")
        vM1 = S("vM1"); sS2 = S("sS2"); vRr = S("vRr"); tB2 = S("tB2"); sB2 = S("sB2")
        vGa = S("vGa"); dQA = S("dQA")

        def load_weights(h):
            for pj in range(4):
                p.wait("scalar", vWs, vWs.n)
                p.dma("scalar", wraw[:, :, :],
                      wproj[h, pj, :, :].rearrange("(a c) d -> c a d", c=128), dW)
                p.wait("vector", dW)
                dhi, dlo = wqkvg[pj]
                if pj == 0:
                    p.op("vector", lambda e: e.tensor_single_scalar(wscr[:], wraw[:], S1, ALU.mult), drain=True)
                    p.op("vector", lambda e: e.scalar_tensor_tensor(wscr[:], wraw[:], S2, wscr[:], ALU.mult, ALU.add), drain=True)
                    s_ = wscr
                else:
                    s_ = wraw
                p.op("vector", lambda e, d=dhi, s=s_: e.tensor_copy(d[:], s[:]), drain=True)
                p.op("vector", lambda e, dh_=dhi, dl=dlo, s=s_:
                     e.tensor_tensor(dl[:], s[:], dh_[:], ALU.subtract), inc=vWs, drain=True)

        def sweep(h):
            load_weights(h)
            for tt in range(nts):
                b = tt % 2
                T = slice(TS * tt, TS * (tt + 1))
                if tt >= 2:
                    p.wait("sync", tQd)
                    p.wait("sync", tVd)
                for half in range(2):
                    for fc in range(FC):
                        p.dma("sync", lnT[:, b, half, fc, :],
                              lnsp[half, T, 128 * fc:128 * (fc + 1)], dLTp[b], transpose=True)
                if h == 0:
                    p.wait("gpsimd", vTab)
                    p.wait("gpsimd", dGq)
                    p.dma("gpsimd", k2sT[:], k2s[:, T], dR)
                    p.dma("gpsimd", sgsT[:], sgs[:, T], dR)
                    p.dma("gpsimd", k2cT[:], k2c[:, T], dR)
                    p.dma("gpsimd", sgcT[:], sgc[:, T], dR)
                    p.op("gpsimd", lambda e, tt=tt: e.iota(tio[:], pattern=[[1, TS]], base=TS * tt,
                                                           channel_multiplier=0,
                                                           allow_small_or_imprecise_dtypes=True), inc=dR)
                    p.wait("vector", dR)
                    V_ = "vector"
                    p.op(V_, lambda e: e.tensor_single_scalar(ph_[:], tio[:], omhc[:, 0:1], ALU.mult), drain=True)
                    p.op(V_, lambda e: e.scalar_tensor_tensor(ph_[:], tio[:], omlc[:, 0:1], ph_[:], ALU.mult, ALU.add), drain=True)
                    p.op(V_, lambda e: e.scalar_tensor_tensor(rA[:], k2sT[:], -PI1, ph_[:], ALU.mult, ALU.add), drain=True)
                    p.op(V_, lambda e: e.scalar_tensor_tensor(rB[:], k2sT[:], -PI2, rA[:], ALU.mult, ALU.add), drain=True)
                    p.op(V_, lambda e: e.scalar_tensor_tensor(rA[:], k2sT[:], -PI3, rB[:], ALU.mult, ALU.add), inc=vTab, drain=True)
                    p.wait("scalar", vTab)
                    p.op("scalar", lambda e: e.activation(snb[:], rA[:], AF.Sin), inc=sTab)
                    p.wait(V_, sTab)
                    p.op(V_, lambda e: e.tensor_tensor(sintile[:], snb[:], sgsT[:], ALU.mult), inc=vTab, drain=True)
                    p.op(V_, lambda e: e.tensor_scalar(rB[:], ph_[:], H1, H2, ALU.add, ALU.add), drain=True)
                    p.op(V_, lambda e: e.scalar_tensor_tensor(rA[:], k2cT[:], -PI1, rB[:], ALU.mult, ALU.add), drain=True)
                    p.op(V_, lambda e: e.scalar_tensor_tensor(rB[:], k2cT[:], -PI2, rA[:], ALU.mult, ALU.add), drain=True)
                    p.op(V_, lambda e: e.scalar_tensor_tensor(rA[:], k2cT[:], -PI3, rB[:], ALU.mult, ALU.add), inc=vTab, drain=True)
                    p.wait("scalar", vTab)
                    p.op("scalar", lambda e: e.activation(snb[:], rA[:], AF.Sin), inc=sTab)
                    p.wait(V_, sTab)
                    p.op(V_, lambda e: e.tensor_tensor(costile[:], snb[:], sgcT[:], ALU.mult), inc=vTab, drain=True)
                    p.wait("gpsimd", vTab)
                    p.dma("gpsimd", ctab[:, T], costile[:], dGq)
                    p.dma("gpsimd", stab[:, T], sintile[:], dGq)
                else:
                    p.wait("gpsimd", vQe)
                    p.wait("gpsimd", vKe)
                    p.wait("gpsimd", dGq)
                    p.dma("gpsimd", costile[:], ctab[:, T], dR)
                    p.dma("gpsimd", sintile[:], stab[:, T], dR)

                p.wait("tensor", dLTp[b])
                if tt == 0:
                    p.wait("tensor", vWs)
                if tt >= 1:
                    p.wait("tensor", sQe); p.wait("tensor", sKe); p.wait("tensor", sGe)
                    p.wait("tensor", vVe)
                wqh, wql = wqkvg[0]; wkh, wkl = wqkvg[1]
                wvh, wvl = wqkvg[2]; wgh, wgl = wqkvg[3]
                for (PP, whi_, wlo_, tsem) in ((PQ, wqh, wql, tQd), (PK, wkh, wkl, tKd), (PG, wgh, wgl, tGd)):
                    for fc in range(FC):
                        p.op("tensor", lambda e, PP=PP, w=whi_, fc=fc, b=b:
                             e.matmul(PP, w[:, fc], lnT[:, b, 0, fc], start=(fc == 0), stop=False))
                        p.op("tensor", lambda e, PP=PP, w=whi_, fc=fc, b=b:
                             e.matmul(PP, w[:, fc], lnT[:, b, 1, fc], start=False, stop=False))
                        p.op("tensor", lambda e, PP=PP, w=wlo_, fc=fc, b=b:
                             e.matmul(PP, w[:, fc], lnT[:, b, 0, fc], start=False, stop=(fc == FC - 1)),
                             inc=(tsem if fc == FC - 1 else None))
                for s4 in range(TS // 128):
                    for fc in range(FC):
                        sl = slice(128 * s4, 128 * (s4 + 1))
                        p.op("tensor", lambda e, s4=s4, fc=fc, b=b, sl=sl, h=h:
                             e.matmul(P6[3][:, sl], lnT[:, b, 0, fc, sl], wvh[:, fc], start=(fc == 0), stop=False))
                        p.op("tensor", lambda e, s4=s4, fc=fc, b=b, sl=sl, h=h:
                             e.matmul(P6[3][:, sl], lnT[:, b, 0, fc, sl], wvl[:, fc], start=False, stop=False))
                        p.op("tensor", lambda e, s4=s4, fc=fc, b=b, sl=sl, h=h:
                             e.matmul(P6[3][:, sl], lnT[:, b, 1, fc, sl], wvh[:, fc], start=False, stop=(fc == FC - 1)),
                             inc=(tVd if (fc == FC - 1 and s4 == TS // 128 - 1) else None))

                p.wait("scalar", tQd)
                p.wait("scalar", vQe, vQe.n)
                p.op("scalar", lambda e: e.copy(qf[:], PQ), inc=sQe)
                p.wait("sync", sQe)
                p.dma("sync", qsw[0:64, :], qf[64:128, :], dSWq)
                p.dma("sync", qsw[64:128, :], qf[0:64, :], dSWq)
                p.wait("vector", dSWq)
                p.wait("vector", vTab if h == 0 else dR)
                p.wait("vector", dGq)
                p.op("vector", lambda e: e.tensor_tensor(qr1[:], qf[:], costile[:], ALU.mult))
                p.op("vector", lambda e: e.tensor_tensor(qr2[:], qsw[:], sintile[:], ALU.mult), drain=True)
                p.op("vector", lambda e: e.tensor_tensor(qr1[:], qr1[:], qr2[:], ALU.add), drain=True)
                p.op("vector", lambda e: e.tensor_copy(qhiT[:], qr1[:]), drain=True)
                p.op("vector", lambda e: e.tensor_tensor(qloT[:], qr1[:], qhiT[:], ALU.subtract), inc=vQe, drain=True)
                p.wait("gpsimd", vQe)
                p.dma("gpsimd", qsp[h, 0, :, T], qhiT[:], dGq)
                p.dma("gpsimd", qsp[h, 1, :, T], qloT[:], dGq)

                p.wait("scalar", tKd)
                p.wait("scalar", vKe, vKe.n)
                p.op("scalar", lambda e: e.copy(kf[:], PK), inc=sKe)
                p.wait("sync", sKe)
                p.dma("sync", ksw[0:64, :], kf[64:128, :], dSWk)
                p.dma("sync", ksw[64:128, :], kf[0:64, :], dSWk)
                p.wait("vector", dSWk)
                kcs = slice(2 * tt, 2 * (tt + 1))
                p.op("vector", lambda e: e.tensor_tensor(qr1[:], kf[:], costile[:], ALU.mult))
                p.op("vector", lambda e: e.tensor_tensor(qr2[:], ksw[:], sintile[:], ALU.mult), drain=True)
                p.op("vector", lambda e: e.tensor_tensor(qr1[:], qr1[:], qr2[:], ALU.add), drain=True)
                p.op("vector", lambda e, kcs=kcs: e.tensor_copy(Kr[0][:, kcs, :], qr1[:]), drain=True)
                p.op("vector", lambda e, kcs=kcs: e.tensor_tensor(Kr[1][:, kcs, :], qr1[:], Kr[0][:, kcs, :], ALU.subtract), inc=vKe, drain=True)

                p.wait("scalar", tGd)
                p.wait("scalar", vGe, vGe.n)
                p.op("scalar", lambda e: e.activation(gsb[:], PG, AF.Silu), inc=sGe)
                p.wait("vector", sGe)
                p.op("vector", lambda e: e.tensor_single_scalar(cmb[:], gsb[:], nwc[:, 0:1], ALU.mult), inc=vGe, drain=True)
                p.wait("gpsimd", vGe)
                p.dma("gpsimd", csp[h, :, T], cmb[:], dGq)

                p.wait("vector", tVd)
                p.op("vector", lambda e, kcs=kcs: e.tensor_copy(Vr[0][:, kcs, :], PV), drain=True)
                p.op("vector", lambda e, kcs=kcs: e.tensor_tensor(Vr[1][:, kcs, :], PV, Vr[0][:, kcs, :], ALU.subtract), inc=vVe, drain=True)

        def attention(h):
            qld = "gpsimd" if h == 0 else "scalar"
            qst = "gpsimd" if h == 0 else "sync"
            dST = dG if h == 0 else dSY
            p.wait(qld, dGq)
            for qb in range(nqb):
                ab = qb % 2
                dQ = dQp[ab]
                QT = slice(QB * qb, QB * (qb + 1))
                if qb >= 2:
                    p.wait(qld, tSc)
                    p.wait(qld, vGa, vGa.n)
                p.dma(qld, qhiA[ab][:], qsp[h, 0, :, QT], dQ)
                p.dma(qld, qloA[ab][:], qsp[h, 1, :, QT], dQ)
                p.dma(qld, cmbA[ab][:], csp[h, :, QT], dQ)
                nkq = 4 * (qb + 1)
                for kc in range(nkq):
                    pb = kc % 2
                    PSb = PS2[pb]
                    p.wait("tensor", dQ)
                    if kc >= 2:
                        p.wait("tensor", vPs)
                    p.op("tensor", lambda e, PSb=PSb, kc=kc, ab=ab:
                         e.matmul(PSb[:], Kr[0][:, kc, :], qhiA[ab][:], start=True, stop=False))
                    p.op("tensor", lambda e, PSb=PSb, kc=kc, ab=ab:
                         e.matmul(PSb[:], Kr[0][:, kc, :], qloA[ab][:], start=False, stop=False))
                    p.op("tensor", lambda e, PSb=PSb, kc=kc, ab=ab:
                         e.matmul(PSb[:], Kr[1][:, kc, :], qhiA[ab][:], start=False, stop=True), inc=tSc)
                    p.wait("scalar", tSc)
                    if kc >= 2:
                        p.wait("scalar", vPs, vPs.n)
                    p.op("scalar", lambda e, pb=pb, PSb=PSb:
                         e.activation(pexp[pb][:], PSb[:], AF.Exp, bias=c20[:, 0:1]), inc=sEx)
                    p.wait("vector", sEx)
                    if kc >= 4 * qb:
                        j = kc - 4 * qb
                        off = 384 - 128 * j
                        p.op("vector", lambda e, pb=pb, off=off:
                             e.tensor_tensor(pexp[pb][:], pexp[pb][:], maskb[:, off:off + QB], ALU.mult), drain=True)
                    if kc >= 2:
                        p.wait("vector", tPe)
                    p.op("vector", lambda e, pb=pb: e.tensor_copy(phi[pb][:], pexp[pb][:]), drain=True)
                    p.op("vector", lambda e, pb=pb: e.tensor_tensor(plo[pb][:], pexp[pb][:], phi[pb][:], ALU.subtract),
                         inc=vPs, drain=True)
                    p.wait("tensor", vPs)
                    last = kc == nkq - 1
                    p.op("tensor", lambda e, kc=kc, pb=pb, qb=qb:
                         e.matmul(PAV[:], Vr[0][:, kc, :], phi[pb][:], start=(kc == 0), stop=False))
                    p.op("tensor", lambda e, kc=kc, pb=pb:
                         e.matmul(PAV[:], Vr[0][:, kc, :], plo[pb][:], start=False, stop=False))
                    p.op("tensor", lambda e, kc=kc, pb=pb, last=last:
                         e.matmul(PAV[:], Vr[1][:, kc, :], phi[pb][:], start=False, stop=last))
                    p.op("tensor", lambda e, kc=kc, pb=pb:
                         e.matmul(PZ, onesb[:], phi[pb][:], start=(kc == 0), stop=False))
                    p.op("tensor", lambda e, kc=kc, pb=pb, last=last:
                         e.matmul(PZ, onesb[:], plo[pb][:], start=False, stop=last), inc=tPe)
                p.wait("scalar", tPe)
                p.op("scalar", lambda e: e.copy(zrow[:], PZ), inc=sZe)
                p.wait("vector", sZe)
                p.op("vector", lambda e: e.reciprocal(rz[:], zrow[:]), inc=vRz, drain=True)
                p.wait("tensor", vRz)
                p.op("tensor", lambda e: e.matmul(PB1[:], ones1f[:], rz[:], start=True, stop=True), inc=tB1)
                p.wait("scalar", tB1)
                p.op("scalar", lambda e: e.copy(bc1[:], PB1[:]), inc=sB1)
                p.wait("vector", sB1)
                p.op("vector", lambda e: e.tensor_tensor(avn[:], PAV[:], bc1[:], ALU.mult), inc=vAvn, drain=True)
                if taps and h == 0:
                    p.wait("gpsimd" if h == 0 else "sync", vAvn)
                    p.dma("gpsimd" if h == 0 else "sync", avtap[:, QT], avn[:], dST)
                p.wait("scalar", vAvn)
                p.op("scalar", lambda e: e.activation(sqb[:], avn[:], AF.Square), inc=sSq)
                p.wait("tensor", sSq)
                p.wait("tensor", vAvn)
                p.op("tensor", lambda e: e.matmul(PM, onesf[:], sqb[:], start=True, stop=True), inc=tM)
                p.wait("scalar", tM)
                p.op("scalar", lambda e: e.copy(msqr[:], PM), inc=sMs)
                p.wait("vector", sMs)
                p.op("vector", lambda e: e.tensor_scalar(msqr[:], msqr[:], 1.0 / 128.0, E1, ALU.mult, ALU.add), drain=True)
                p.op("vector", lambda e: e.tensor_single_scalar(msqr[:], msqr[:], E2, ALU.add), inc=vM1, drain=True)
                p.wait("scalar", vM1)
                p.op("scalar", lambda e: e.activation(msqr[:], msqr[:], AF.Sqrt), inc=sS2)
                p.wait("vector", sS2)
                p.op("vector", lambda e: e.reciprocal(rr_[:], msqr[:]), inc=vRr, drain=True)
                p.wait("tensor", vRr)
                p.op("tensor", lambda e: e.matmul(PB2[:], ones1f[:], rr_[:], start=True, stop=True), inc=tB2)
                p.wait("scalar", tB2)
                p.op("scalar", lambda e: e.copy(bc2[:], PB2[:]), inc=sB2)
                p.wait("vector", sB2)
                p.wait("vector", dQ)
                p.op("vector", lambda e: e.tensor_tensor(avn[:], avn[:], bc2[:], ALU.mult), drain=True)
                p.wait("vector", dST, dST.n)
                p.op("vector", lambda e, ab=ab: e.tensor_tensor(gat[:], avn[:], cmbA[ab][:], ALU.mult), drain=True)
                p.op("vector", lambda e: e.tensor_copy(gth[:], gat[:]), drain=True)
                p.op("vector", lambda e: e.tensor_tensor(gtl[:], gat[:], gth[:], ALU.subtract), inc=vGa, drain=True)
                p.wait(qst, vGa)
                p.dma(qst, gsph[HD * h:HD * (h + 1), QT], gth[:], dST)
                p.dma(qst, gspl[HD * h:HD * (h + 1), QT], gtl[:], dST)

        PS2 = [P6[0], P6[1]]
        PAV = P6[2]
        PZ = P6[3][0:1, :]
        PB1 = P6[4]
        PM = P6[5][0:1, :]
        PB2 = P6[4]

        sweep(0)
        p.wait("gpsimd", dGq)
        p.barrier("s0")
        attention(0)
        p.barrier("a0")
        sweep(1)
        p.wait("gpsimd", dGq)
        p.barrier("s1")
        stSw.close()

        stQ = contextlib.ExitStack()
        wslc = sb("wslc", [128, 1024], F32, stQ)
        racc1 = sb("racc1", [128, 1], F32, stQ)
        racc2 = sb("racc2", [128, 1], F32, stQ)
        rtmp = sb("rtmp", [128, 1], F32, stQ)
        ppb = sb("ppb", [128, 1024], F32, stQ)
        nnb = sb("nnb", [128, 1024], F32, stQ)
        qtb = sb("qtb", [128, 1024], BF16, stQ)
        s12 = sb("s12", [1, 128], F32, stQ)
        ar8 = sb("ar8", [8, 128], F32, stQ)
        ones8 = sb("ones8", [8, 1], F32, stQ)
        fin_sb = sb("fin_sb", [1, 128], F32, stQ)
        arb = sb("arb", [1, 2], F32, stQ)
        thb = sb("thb", [1, 2], F32, stQ)
        nthb = sb("nthb", [1, 2], F32, stQ)
        thbc = sb("thbc", [128, 2], F32, stQ)
        nthbc = sb("nthbc", [128, 2], F32, stQ)
        PQm = P6[5][0:1, 0:2]
        dQz = S("dQz"); vQz = S("vQz"); tQz = S("tQz"); sQz = S("sQz"); gQz = S("gQz")

        p.op("vector", lambda e: e.memset(ones8[:], 1.0))
        p.op("vector", lambda e: e.memset(s12[:], 0.0))
        p.op("vector", lambda e: e.memset(fin_sb[:], 0.0), drain=True)
        p.op("vector", lambda e: e.memset(racc1[:], 0.0))
        p.op("vector", lambda e: e.memset(racc2[:], 0.0), drain=True)
        npc1 = (w1r * DFF) // (128 * 1024)
        npc2 = (w2r * DM) // (128 * 1024)
        def wpiece(mat, rowsz, i):
            wch = int(mat.tensor.shape[1]) // 1024
            rj, cj = divmod(i, wch)
            return mat[128 * rj:128 * (rj + 1), 1024 * cj:1024 * (cj + 1)]
        for mat, npc, racc in ((w1s, npc1, racc1), (w2s, npc2, racc2)):
            for i in range(npc):
                p.wait("scalar", vQz)
                p.dma("scalar", wslc[:], wpiece(mat, None, i), dQz)
                p.wait("vector", dQz)
                p.op("vector", lambda e: e.tensor_reduce(rtmp[:], wslc[:], axis=mybir.AxisListType.X,
                                                         op=ALU.add, apply_absolute_value=True), drain=True)
                p.op("vector", lambda e, racc=racc: e.tensor_tensor(racc[:], racc[:], rtmp[:], ALU.add),
                     inc=vQz, drain=True)
        p.wait("tensor", vQz)
        p.op("tensor", lambda e: e.matmul(P6[5][0:1, 0:1], onesf[:], racc1[:], start=True, stop=True))
        p.op("tensor", lambda e: e.matmul(P6[5][0:1, 1:2], onesf[:], racc2[:], start=True, stop=True), inc=tQz)
        p.wait("scalar", tQz)
        p.op("scalar", lambda e: e.copy(s12[0:1, 0:2], PQm), inc=sQz)
        p.wait("gpsimd", sQz)
        p.dma("gpsimd", ar_in[:, :], s12[:], dQz)
        p.dma("gpsimd", fence_in[:, :], fin_sb[:], dQz)
        p.wait("gpsimd", dQz)
        p.op("gpsimd", lambda e: e.collective_compute("AllGather", ALU.bypass, replica_groups=RG,
                                                      ins=[ar_in[:, :]], outs=[ar_out[:, :]]), inc=gQz)
        p.op("gpsimd", lambda e: e.collective_compute("AllGather", ALU.bypass, replica_groups=RG,
                                                      ins=[fence_in[:, :]], outs=[fence_out[:, :]]), inc=gQz)

        attention(1)
        p.wait("gpsimd", dGq)
        p.wait("sync", dSY, dSY.n)
        p.barrier("a1")

        p.dma("gpsimd", ar8[:], ar_out[:, :], dQz)
        p.wait("tensor", dQz)
        p.op("tensor", lambda e: e.matmul(P6[5][0:1, 4:6], ones8[:], ar8[:, 0:2], start=True, stop=True), inc=tQz)
        p.wait("scalar", tQz)
        p.op("scalar", lambda e: e.copy(arb[:], P6[5][0:1, 4:6]), inc=sQz)
        p.wait("vector", sQz)
        p.op("vector", lambda e: e.tensor_single_scalar(arb[:], arb[:], 1.0 / (DFF * DM), ALU.mult), drain=True)
        p.op("vector", lambda e: e.tensor_single_scalar(thb[:], arb[:], C71, ALU.mult), drain=True)
        p.op("vector", lambda e: e.scalar_tensor_tensor(thb[:], arb[:], C72, thb[:], ALU.mult, ALU.add), drain=True)
        p.op("vector", lambda e: e.tensor_single_scalar(nthb[:], thb[:], -1.0, ALU.mult), inc=vQz, drain=True)
        if taps:
            p.wait("gpsimd", vQz)
            p.dma("gpsimd", dbgq[0:1, 0:2], s12[0:1, 0:2], dG)
            p.dma("gpsimd", dbgq[0:1, 2:4], arb[:], dG)
            p.dma("gpsimd", dbgq[0:1, 4:6], thb[:], dG)
            p.dma("gpsimd", ar8tap[:, :], ar8[:], dG)
        p.wait("tensor", vQz)
        p.op("tensor", lambda e: e.matmul(P6[4][:, 0:2], ones1f[:], thb[:], start=True, stop=True))
        p.op("tensor", lambda e: e.matmul(P6[4][:, 2:4], ones1f[:], nthb[:], start=True, stop=True), inc=tQz)
        p.wait("scalar", tQz)
        p.op("scalar", lambda e: e.copy(thbc[:], P6[4][:, 0:2]))
        p.op("scalar", lambda e: e.copy(nthbc[:], P6[4][:, 2:4]), inc=gQz)
        for mat, npc, qdst, col in ((w1s, npc1, q1_in, 0), (w2s, npc2, q2_in, 1)):
            for i in range(npc):
                p.wait("scalar", vQz)
                p.wait("scalar", dG, dG.n)
                p.dma("scalar", wslc[:], wpiece(mat, None, i), dQz)
                p.wait("vector", dQz)
                p.wait("vector", gQz)
                p.op("vector", lambda e, col=col: e.tensor_single_scalar(ppb[:], wslc[:], thbc[:, col:col + 1], ALU.is_ge))
                p.op("vector", lambda e, col=col: e.tensor_single_scalar(nnb[:], wslc[:], nthbc[:, col:col + 1], ALU.is_le), drain=True)
                p.op("vector", lambda e: e.tensor_tensor(qtb[:], ppb[:], nnb[:], ALU.subtract), inc=vQz, drain=True)
                p.wait("gpsimd", vQz)
                p.dma("gpsimd", wpiece(qdst, None, i), qtb[:], dG)
                if taps and qdst is q1_in:
                    p.dma("gpsimd", wpiece(q1tap, None, i), qtb[:], dG)
        p.wait("gpsimd", dG, dG.n)
        p.op("gpsimd", lambda e: e.collective_compute("AllGather", ALU.bypass, replica_groups=RG,
                                                      ins=[q1_in[:, :]], outs=[q1_out[:, :]]), inc=gQz)
        p.op("gpsimd", lambda e: e.collective_compute("AllGather", ALU.bypass, replica_groups=RG,
                                                      ins=[q2_in[:, :]], outs=[q2_out[:, :]]), inc=gQz)
        p.op("gpsimd", lambda e: e.collective_compute("AllGather", ALU.bypass, replica_groups=RG,
                                                      ins=[fence_in[:, :]], outs=[fence_out[:, :]]), inc=gQz)
        gQzFinal = gQz.n

        attention(1)
        p.wait("gpsimd", dG, dG.n)
        p.wait("sync", dSY, dSY.n)
        p.barrier("a1")
        stQ.close()
        stBp.close()
        stB.close()

        ccA = S("ccA")
        stB3 = contextlib.ExitStack()
        wosr = sb("wosr", [128, 2, DM], F32, stB3)
        woshh = sb("woshh", [128, 2, DM], BF16, stB3)
        wosll = sb("wosll", [128, 2, DM], BF16, stB3)
        gh3 = [sb(f"gh3{i}", [128, 2, QB], BF16, stB3) for i in range(2)]
        gl3 = [sb(f"gl3{i}", [128, 2, QB], BF16, stB3) for i in range(2)]
        pev = [sb(f"pev{i}", [128, QB], F32, stB3) for i in range(4)]
        PWp = [P6[i] for i in range(2)]
        dB3 = S("dB3"); vB3 = S("vB3"); tB3 = S("tB3"); sB3 = S("sB3")
        dB3p = [S("dB30"), S("dB31")]
        dGrs = [S("dGrs0"), S("dGrs1")]
        p.dma("scalar", wosr[:, 0, :], wos_d[0:128, :], dB3)
        p.dma("scalar", wosr[:, 1, :], wos_d[128:256, :], dB3)
        p.wait("vector", dB3)
        p.op("vector", lambda e: e.tensor_copy(woshh[:], wosr[:]), drain=True)
        p.op("vector", lambda e: e.tensor_tensor(wosll[:], wosr[:], woshh[:], ALU.subtract), inc=vB3, drain=True)
        for k in range(NCORES):
            for tt in range(nqb):
                QT = slice(QB * tt, QB * (tt + 1))
                pb3 = tt % 2
                if k + tt > 0:
                    p.wait("scalar", tB3)
                for hh in range(2):
                    p.dma("scalar", gh3[pb3][:, hh, :], gsph[HD * hh:HD * (hh + 1), QT], dB3p[pb3])
                    p.dma("scalar", gl3[pb3][:, hh, :], gspl[HD * hh:HD * (hh + 1), QT], dB3p[pb3])
                p.wait("tensor", vB3)
                p.wait("tensor", dB3)
                p.wait("tensor", dB3p[pb3])
                if k + tt > 0:
                    p.wait("tensor", sB3)
                for d in range(2):
                    dout0 = 256 * k + 128 * d
                    for hc in range(2):
                        st_ = hc == 0
                        sp_ = hc == 1
                        p.op("tensor", lambda e, d=d, hc=hc, dout0=dout0, pb3=pb3, st_=st_:
                             e.matmul(PWp[d][:], woshh[:, hc, dout0:dout0 + 128], gh3[pb3][:, hc, :], start=st_, stop=False))
                        p.op("tensor", lambda e, d=d, hc=hc, dout0=dout0, pb3=pb3:
                             e.matmul(PWp[d][:], woshh[:, hc, dout0:dout0 + 128], gl3[pb3][:, hc, :], start=False, stop=False))
                        p.op("tensor", lambda e, d=d, hc=hc, dout0=dout0, pb3=pb3, sp_=sp_:
                             e.matmul(PWp[d][:], wosll[:, hc, dout0:dout0 + 128], gh3[pb3][:, hc, :], start=False, stop=sp_),
                             inc=(tB3 if (d == 1 and hc == 1) else None))
                p.wait("scalar", tB3)
                p.wait("scalar", dGrs[pb3])
                p.op("scalar", lambda e, pb3=pb3: e.copy(pev[2 * pb3][:], PWp[0][:]))
                p.op("scalar", lambda e, pb3=pb3: e.copy(pev[2 * pb3 + 1][:], PWp[1][:]), inc=sB3)
                p.wait("sync", sB3)
                jj = (QB * tt) // rows
                col = (QB * tt) % rows
                p.dma("sync", rs_in[k][jj, 0:128, col:col + QB], pev[2 * pb3][:], dGrs[pb3])
                p.dma("sync", rs_in[k][jj, 128:256, col:col + QB], pev[2 * pb3 + 1][:], dGrs[pb3])
                if taps and k == 0:
                    p.dma("sync", rstap[jj, 0:128, col:col + QB], pev[2 * pb3][:], dGrs[pb3])
                    p.dma("sync", rstap[jj, 128:256, col:col + QB], pev[2 * pb3 + 1][:], dGrs[pb3])
            p.wait("gpsimd", dGrs[0])
            p.wait("gpsimd", dGrs[1])
            p.op("gpsimd", lambda e, k=k: e.collective_compute(
                "ReduceScatter", ALU.add, replica_groups=RG,
                ins=[rs_in[k][:, :, :]],
                outs=[rs_out[k][:, :]]), inc=ccA)
        p.op("gpsimd", lambda e: e.collective_compute("AllGather", ALU.bypass, replica_groups=RG,
                                                      ins=[fence_in[:, :]], outs=[fence_out[:, :]]), inc=ccA)
        p.wait("gpsimd", ccA)
        for k in range(NCORES):
            p.dma("gpsimd", rs_cp[k][:, :], rs_out[k][:, :], dG)
        if taps:
            p.dma("gpsimd", rsotap[:, :], rs_out[0][:, :], dG)
        stB3.close()
        dGcopy_mark = dG.n

        stC = contextlib.ExitStack()
        x2Tb = sb("x2Tb", [128, FC, rows], F32, stC)
        ln2h = sb("ln2h", [128, FC, rows], BF16, stC)
        ln2l = sb("ln2l", [128, FC, rows], BF16, stC)
        rst = [sb(f"rst{i}", [128, rows], F32, stC) for i in range(2)]
        sq2 = [sb(f"sq2{i}", [128, QB], F32, stC) for i in range(2)]
        ev1 = sb("ev1", [1, QB], F32, stC)
        ev2 = sb("ev2", [1, QB], F32, stC)
        mrow = sb("mrow", [1, QB], F32, stC)
        nmrow = sb("nmrow", [1, QB], F32, stC)
        msrow = sb("msrow", [1, QB], F32, stC)
        varow = sb("varow", [1, QB], F32, stC)
        rsrow = sb("rsrow", [1, QB], F32, stC)
        nbrow = sb("nbrow", [1, QB], F32, stC)
        bcr = sb("bcr", [128, QB], F32, stC)
        bcb = sb("bcb", [128, QB], F32, stC)
        lf2 = sb("lf2", [128, QB], F32, stC)
        stCp = contextlib.ExitStack()
        PW = [ps(f"PW{i}", [128, QB], stCp) for i in range(8)]

        dC = S("dC"); vCx = S("vCx")
        dCx = S("dCx")
        dCr = [S("dCr0"), S("dCr1")]
        sCq = S("sCq"); tCm = S("tCm"); sCe = S("sCe"); vCst = S("vCst")
        sCsq = S("sCsq"); vCrs = S("vCrs"); tCb = S("tCb"); sCb = S("sCb"); vCln = S("vCln")

        for fc in range(FC):
            p.dma("scalar", x2Tb[:, fc, :], xT[128 * fc:128 * (fc + 1), :], dCx)
        for fc in range(FC):
            rb = fc % 2
            k, dsub = divmod(fc, 2)
            p.wait("gpsimd", ccA)
            p.wait("gpsimd", dG, dGcopy_mark)
            if fc >= 2:
                p.wait("gpsimd", vCx)
            p.dma("gpsimd", rst[rb][:], rs_cp[k][128 * dsub:128 * (dsub + 1), :], dCr[rb])
            if fc == 0:
                p.wait("vector", dCx)
            p.wait("vector", dCr[rb])
            p.op("vector", lambda e, fc=fc, rb=rb:
                 e.tensor_tensor(x2Tb[:, fc, :], x2Tb[:, fc, :], rst[rb][:], ALU.add), inc=vCx, drain=True)

        for tb in range(tbn):
            TB = slice(QB * tb, QB * (tb + 1))
            if taps:
                p.wait("gpsimd", vCx)
                for fc in range(FC):
                    p.dma("gpsimd", x2tap[128 * fc:128 * (fc + 1), TB], x2Tb[:, fc, TB], dG)
            for fc in range(FC):
                qb2 = fc % 2
                p.wait("scalar", vCx)
                if fc >= 2:
                    p.wait("scalar", tCm)
                p.op("scalar", lambda e, fc=fc, qb2=qb2, TB=TB: e.activation(sq2[qb2][:], x2Tb[:, fc, TB], AF.Square), inc=sCq)
                p.wait("tensor", sCq)
                p.op("tensor", lambda e, fc=fc, TB=TB:
                     e.matmul(PW[0][0:1, :], onesf[:], x2Tb[:, fc, TB], start=(fc == 0), stop=(fc == FC - 1)))
                p.op("tensor", lambda e, fc=fc, qb2=qb2:
                     e.matmul(PW[1][0:1, :], onesf[:], sq2[qb2][:], start=(fc == 0), stop=(fc == FC - 1)),
                     inc=tCm)
            p.wait("scalar", tCm)
            p.op("scalar", lambda e: e.copy(ev1[:], PW[0][0:1, :]))
            p.op("scalar", lambda e: e.copy(ev2[:], PW[1][0:1, :]), inc=sCe)
            p.wait("vector", sCe)
            p.op("vector", lambda e: e.tensor_single_scalar(mrow[:], ev1[:], 1.0 / DM, ALU.mult), drain=True)
            p.op("vector", lambda e: e.tensor_single_scalar(nmrow[:], mrow[:], -1.0, ALU.mult), drain=True)
            p.op("vector", lambda e: e.tensor_single_scalar(msrow[:], ev2[:], 1.0 / DM, ALU.mult), drain=True)
            p.op("vector", lambda e: e.tensor_tensor(varow[:], mrow[:], nmrow[:], ALU.mult), drain=True)
            p.op("vector", lambda e: e.tensor_tensor(varow[:], msrow[:], varow[:], ALU.add), drain=True)
            p.op("vector", lambda e: e.tensor_scalar(varow[:], varow[:], E1, E2, ALU.add, ALU.add), inc=vCst, drain=True)
            p.wait("scalar", vCst)
            p.op("scalar", lambda e: e.activation(varow[:], varow[:], AF.Sqrt), inc=sCsq)
            p.wait("vector", sCsq)
            p.op("vector", lambda e: e.reciprocal(rsrow[:], varow[:]), drain=True)
            p.op("vector", lambda e: e.tensor_tensor(nbrow[:], nmrow[:], rsrow[:], ALU.mult), inc=vCrs, drain=True)
            p.wait("tensor", vCrs)
            p.op("tensor", lambda e: e.matmul(PW[2][:], ones1f[:], rsrow[:], start=True, stop=True))
            p.op("tensor", lambda e: e.matmul(PW[3][:], ones1f[:], nbrow[:], start=True, stop=True), inc=tCb)
            p.wait("scalar", tCb)
            p.op("scalar", lambda e: e.copy(bcr[:], PW[2][:]))
            p.op("scalar", lambda e: e.copy(bcb[:], PW[3][:]), inc=sCb)
            p.wait("vector", sCb)
            for fc in range(FC):
                p.op("vector", lambda e, fc=fc, TB=TB: e.tensor_tensor(lf2[:], x2Tb[:, fc, TB], bcr[:], ALU.mult), drain=True)
                p.op("vector", lambda e, fc=fc: e.tensor_tensor(lf2[:], lf2[:], bcb[:], ALU.add), drain=True)
                p.op("vector", lambda e, fc=fc, TB=TB: e.tensor_copy(ln2h[:, fc, TB], lf2[:]), drain=True)
                p.op("vector", lambda e, fc=fc, TB=TB: e.tensor_tensor(ln2l[:, fc, TB], lf2[:], ln2h[:, fc, TB], ALU.subtract),
                     inc=vCln, drain=True)

        p.barrier("C")
        stCp.close()

        stD = contextlib.ExitStack()
        w1t = [sb(f"w1t{i}", [128, FC, 128], BF16, stD) for i in range(2)]
        hfb = [sb(f"hfb{i}", [128, rows], F32, stD) for i in range(2)]
        hhb = [sb(f"hhb{i}", [128, rows], BF16, stD) for i in range(2)]
        hlb = [sb(f"hlb{i}", [128, rows], BF16, stD) for i in range(2)]
        stD1p = contextlib.ExitStack()
        PH = [ps(f"PH{i}", [128, rows], stD1p) for i in range(2)]

        dD = S("dD"); tDh = S("tDh"); sDg = S("sDg"); vDs = S("vDs")
        dD1p = [S("dD10"), S("dD11")]
        p.wait("scalar", gQz, gQzFinal)
        if taps:
            p.wait("gpsimd", gQz, gQzFinal)
            p.dma("gpsimd", q1otap[:, :], q1_out[:, :], dG)
        for fo in range(nfo):
            fb = fo % 2
            if fo >= 2:
                p.wait("scalar", tDh)
            p.dma("scalar", w1t[fb][:, :, :],
                  q1_out[:, 128 * fo:128 * (fo + 1)].rearrange("(a c) d -> c a d", c=128), dD1p[fb])
            p.wait("tensor", dD1p[fb])
            p.wait("tensor", vCln, vCln.n)
            if fo >= 2:
                p.wait("tensor", sDg, sDg.n)
            for fc in range(FC):
                for tb in range(tbn):
                    TB = slice(QB * tb, QB * (tb + 1))
                    p.op("tensor", lambda e, fb=fb, fc=fc, TB=TB:
                         e.matmul(PH[fb][:, TB], w1t[fb][:, fc], ln2h[:, fc, TB], start=(fc == 0), stop=False))
                    p.op("tensor", lambda e, fb=fb, fc=fc, TB=TB:
                         e.matmul(PH[fb][:, TB], w1t[fb][:, fc], ln2l[:, fc, TB], start=False, stop=(fc == FC - 1)),
                         inc=(tDh if (fc == FC - 1 and tb == tbn - 1) else None))
            p.wait("scalar", tDh)
            p.wait("scalar", vDs, vDs.n)
            p.op("scalar", lambda e, fb=fb, fo=fo:
                 e.activation(hfb[fb][:], PH[fb][:], AF.Gelu, bias=b1c[:, fo:fo + 1]), inc=sDg)
            p.wait("vector", sDg)
            p.wait("vector", dG, dG.n)
            p.op("vector", lambda e, fb=fb: e.tensor_copy(hhb[fb][:], hfb[fb][:]), drain=True)
            p.op("vector", lambda e, fb=fb: e.tensor_tensor(hlb[fb][:], hfb[fb][:], hhb[fb][:], ALU.subtract),
                 inc=vDs, drain=True)
            p.wait("gpsimd", vDs)
            p.dma("gpsimd", hsp[0, fo, :, :], hhb[fb][:], dG)
            p.dma("gpsimd", hsp[1, fo, :, :], hlb[fb][:], dG)

        p.wait("gpsimd", dG, dG.n)
        p.barrier("D1")
        stD1p.close()
        w2t = [sb(f"w2t{i}", [128, 1024], BF16, stD) for i in range(2)]
        hht = [sb(f"hht{i}", [128, QB], BF16, stD) for i in range(2)]
        hlt = [sb(f"hlt{i}", [128, QB], BF16, stD) for i in range(2)]
        tfb = sb("tfb", [128, QB], F32, stD)
        ytt = sb("ytt", [128, QB], F32, stD)
        yot = [sb(f"yot{i}", [128, 128], F32, stD) for i in range(2)]
        dout_hist = []
        stD2p = contextlib.ExitStack()
        PF = [ps(f"PF{i}", [128, QB], stD2p) for i in range(8)]

        tDf = S("tDf"); sDe = S("sDe"); vDy = S("vDy"); tDt = S("tDt"); sDt = S("sDt"); dOut = S("dOut")
        dDhp = [S("dDh0"), S("dDh1")]
        dDwp = [S("dDw0"), S("dDw1")]
        dOutp = [S("dOut0"), S("dOut1")]
        for tb2 in range(tbn):
            TB = slice(QB * tb2, QB * (tb2 + 1))
            for dh in range(2):
                for fo in range(nfo):
                    hb = fo % 2
                    if fo >= 2:
                        p.wait("gpsimd", tDf)
                    p.dma("gpsimd", hht[hb][:], hsp[0, fo, :, TB], dDhp[hb])
                    p.dma("gpsimd", hlt[hb][:], hsp[1, fo, :, TB], dDhp[hb])
                    if fo >= 2:
                        p.wait("scalar", tDf)
                    p.dma("scalar", w2t[hb][:], q2_out[128 * fo:128 * (fo + 1), 1024 * dh:1024 * (dh + 1)], dDwp[hb])
                    p.wait("tensor", dDhp[hb])
                    p.wait("tensor", dDwp[hb])
                    if fo == 0:
                        p.wait("tensor", sDt, sDt.n)
                    for d in range(8):
                        p.op("tensor", lambda e, d=d, hb=hb, fo=fo:
                             e.matmul(PF[d][:], w2t[hb][:, 128 * d:128 * (d + 1)], hht[hb][:], start=(fo == 0), stop=False))
                        p.op("tensor", lambda e, d=d, hb=hb, fo=fo:
                             e.matmul(PF[d][:], w2t[hb][:, 128 * d:128 * (d + 1)], hlt[hb][:], start=False, stop=(fo == nfo - 1)),
                             inc=(tDf if d == 7 else None))
                for d in range(8):
                    dd = 8 * dh + d
                    p.wait("vector", tDf)
                    p.op("vector", lambda e, d=d, dd=dd, TB=TB:
                         e.scalar_tensor_tensor(ytt[:], PF[d][:], b2c[:, dd:dd + 1], x2Tb[:, dd, TB], ALU.add, ALU.add),
                         inc=vDy, drain=True)
                    for s4 in range(QB // 128):
                        yb = len(dout_hist) % 2
                        p.wait("tensor", vDy)
                        if dout_hist:
                            p.wait("tensor", sDt)
                        p.op("tensor", lambda e, d=d, s4=s4:
                             e.transpose(PF[d][:, 0:128], ytt[:, 128 * s4:128 * (s4 + 1)], ident[:]), inc=tDt)
                        p.wait("scalar", tDt)
                        p.wait("scalar", dOutp[yb])
                        p.op("scalar", lambda e, d=d, yb=yb: e.copy(yot[yb][:], PF[d][:, 0:128]), inc=sDt)
                        p.wait("gpsimd", sDt)
                        p.dma("gpsimd", y[QB * tb2 + 128 * s4:QB * tb2 + 128 * (s4 + 1), 128 * dd:128 * (dd + 1)],
                              yot[yb][:], dOutp[yb])
                        dout_hist.append(0)
        p.wait("gpsimd", dOutp[0])
        p.wait("gpsimd", dOutp[1])
        stD2p.close()
        stD.close()
        stC.close()

        p.emit(block)
    return nc


def _rope_tables(seq):
    d = np.arange(128)
    i = d % 64
    om = np.exp(-np.log(10000.0) * i / 64.0)
    omp = om * np.where(d < 64, -1.0, 1.0)
    omh_ = omp.astype(np.float32)
    oml_ = (omp - omh_.astype(np.float64)).astype(np.float32)
    t = np.arange(seq, dtype=np.float64)
    phase = omp[:, None] * t[None, :]
    k2s_ = np.round(phase / np.pi)
    sgs_ = np.where(np.abs(k2s_) % 2 == 1, -1.0, 1.0).astype(np.float32)
    k2c_ = np.round((phase + np.pi / 2) / np.pi)
    sgc_ = np.where(np.abs(k2c_) % 2 == 1, -1.0, 1.0).astype(np.float32)
    return (k2s_.astype(np.float32), sgs_, k2c_.astype(np.float32), sgc_,
            omh_.reshape(128, 1), oml_.reshape(128, 1))


def kernel(L0_attn_hidden_states, L0_attn_wq, L0_attn_wk, L0_attn_wv, L0_attn_wg,
           L0_attn_wo, L0_attn_norm_w, L0_ffn_w1, L0_ffn_b1, L0_ffn_w2, L0_ffn_b2):
    seq = L0_attn_hidden_states.shape[1]
    rows = seq // NCORES
    xfull = np.ascontiguousarray(np.asarray(L0_attn_hidden_states, dtype=np.float32).reshape(seq, DM))
    xTf = np.ascontiguousarray(xfull.T)
    wq = np.ascontiguousarray(np.asarray(L0_attn_wq, np.float32).T)
    wk = np.ascontiguousarray(np.asarray(L0_attn_wk, np.float32).T)
    wv = np.ascontiguousarray(np.asarray(L0_attn_wv, np.float32).T)
    wg = np.ascontiguousarray(np.asarray(L0_attn_wg, np.float32).T)
    woT = np.ascontiguousarray(np.asarray(L0_attn_wo, np.float32).T)
    w1T = np.ascontiguousarray(np.asarray(L0_ffn_w1, np.float32).T)
    w2T = np.ascontiguousarray(np.asarray(L0_ffn_w2, np.float32).T)
    b1v = np.ascontiguousarray(np.asarray(L0_ffn_b1, np.float32).reshape(DFF // 128, 128).T)
    b2v = np.ascontiguousarray(np.asarray(L0_ffn_b2, np.float32).reshape(FC, 128).T)
    nwv = np.ascontiguousarray(np.asarray(L0_attn_norm_w, np.float32).reshape(HD, 1))
    k2s_, sgs_, k2c_, sgc_, omh_, oml_ = _rope_tables(seq)

    nc = build(seq=seq)
    in_maps = []
    for c in range(NCORES):
        h0 = HPC * c
        wp = np.empty((HPC, 4, DM, HD), np.float32)
        for hh in range(HPC):
            sl = slice(HD * (h0 + hh), HD * (h0 + hh + 1))
            wp[hh, 0] = wq[:, sl]
            wp[hh, 1] = wk[:, sl]
            wp[hh, 2] = wv[:, sl]
            wp[hh, 3] = wg[:, sl]
        in_maps.append({
            "x": xfull,
            "xT": np.ascontiguousarray(xTf[:, rows * c:rows * (c + 1)]),
            "wproj": np.ascontiguousarray(wp),
            "wos_d": np.ascontiguousarray(woT[2 * HD * c:2 * HD * (c + 1), :]),
            "w1s": np.ascontiguousarray(w1T[(DM // NCORES) * c:(DM // NCORES) * (c + 1), :]),
            "w2s": np.ascontiguousarray(w2T[(DFF // NCORES) * c:(DFF // NCORES) * (c + 1), :]),
            "b1": b1v, "b2": b2v, "nw": nwv,
            "k2s": k2s_, "sgs": sgs_, "k2c": k2c_, "sgc": sgc_,
            "omh": omh_, "oml": oml_,
        })
    res = run_bass_kernel_spmd(nc, in_maps, core_ids=list(range(NCORES)))
    out = np.empty((1, seq, DM), np.float32)
    for c in range(NCORES):
        out[0, rows * c:rows * (c + 1), :] = res.results[c]["y"]
    return out


# version 03
# speedup vs baseline: 1.0695x; 1.0695x over previous
import contextlib
import numpy as np
import concourse.bass as bass
import concourse.mybir as mybir
from concourse.bass_utils import run_bass_kernel_spmd

F32 = mybir.dt.float32
BF16 = mybir.dt.bfloat16
AF = mybir.ActivationFunctionType
ALU = mybir.AluOpType

DM = 2048
HD = 128
NH = 16
HPC = 2
NCORES = 8
DFF = 8192
FC = DM // 128
TS = 256
QB = 512
EPS = 1e-5


def _split2(v):
    a = np.float64(np.float32(v))
    b = float(np.float32(np.float64(v) - a))
    return float(a), b


def _split3(v):
    a = np.float64(np.float32(v))
    r = np.float64(v) - a
    b = np.float64(np.float32(r))
    c = float(np.float32(r - b))
    return float(a), float(b), c


PI1, PI2, PI3 = _split3(np.pi)
H1, H2 = _split2(np.pi / 2)
E1, E2 = _split2(EPS)
S1, S2 = _split2(1.0 / np.sqrt(128.0))
C71, C72 = _split2(0.7)

ENGS = ("sync", "scalar", "vector", "tensor", "gpsimd")


class Sem:
    __slots__ = ("h", "n")

    def __init__(self, h):
        self.h = h
        self.n = 0


class P:

    def __init__(self, nc, es):
        self.nc = nc
        self.es = es
        self.q = {e: [] for e in ENGS}
        self._sems = {}

    def sem(self, name):
        if name.startswith("bar_"):
            name = "bar"
        elif not name.startswith("d") and name != "ccA":
            name = name[0] + "P"
        if name not in self._sems:
            self._sems[name] = Sem(self.es.enter_context(self.nc.semaphore(name)))
        return self._sems[name]

    def wait(self, eng, sem, v=None):
        val = sem.n if v is None else v
        self.q[eng].append(lambda e, s=sem.h, w=val: e.wait_ge(s, w))

    def op(self, eng, fn, inc=None, incn=1, drain=False):
        if inc is not None:
            inc.n += incn

        def run(e, fn=fn, inc=inc, incn=incn, drain=drain):
            instr = fn(e)
            if inc is not None:
                instr.then_inc(inc.h, incn)
            if drain:
                e.drain()

        self.q[eng].append(run)

    def dma(self, eng, out, in_, inc, transpose=False):
        inc.n += 16

        def run(e, out=out, in_=in_, inc=inc, transpose=transpose):
            if transpose:
                e.dma_start(out=out, in_=in_, transpose=True).then_inc(inc.h, 16)
            else:
                e.dma_start(out=out, in_=in_).then_inc(inc.h, 16)

        self.q[eng].append(run)

    def barrier(self, name):
        b = self.sem("bar_" + name)
        target = b.n + len(ENGS)
        for e in ENGS:
            self.op(e, lambda eng: eng.nop(), inc=b, incn=1)
        for e in ENGS:
            self.wait(e, b, target)

    def emit(self, block):
        for name in ENGS:
            fns = self.q[name]

            def body(eng, fns=fns):
                for f in fns:
                    f(eng)

            getattr(block, name)(body)


def build(seq=8192, taps=False):
    nts = seq // TS
    nqb = seq // QB
    rows = seq // NCORES
    tbn = rows // QB
    nkc = seq // 128
    nA = seq // 128
    w1r = DM // NCORES
    w2r = DFF // NCORES
    nfo = DFF // 128

    nc = bass.Bass("TRN2", target_bir_lowering=False)
    dram = lambda n, s, d, k: nc.dram_tensor(n, s, d, kind=k).ap()
    drams = lambda n, s, d: nc.dram_tensor(n, s, d, kind="Internal", addr_space="Shared").ap()
    IK = "ExternalOutput" if taps else "Internal"
    x = dram("x", [seq, DM], F32, "ExternalInput")
    xT = dram("xT", [DM, rows], F32, "ExternalInput")
    wproj = dram("wproj", [HPC, 4, DM, HD], F32, "ExternalInput")
    wos_d = dram("wos_d", [2 * HD, DM], F32, "ExternalInput")
    w1s = dram("w1s", [w1r, DFF], F32, "ExternalInput")
    w2s = dram("w2s", [w2r, DM], F32, "ExternalInput")
    b1 = dram("b1", [128, DFF // 128], F32, "ExternalInput")
    b2 = dram("b2", [128, FC], F32, "ExternalInput")
    nw = dram("nw", [HD, 1], F32, "ExternalInput")
    k2s = dram("k2s", [HD, seq], F32, "ExternalInput")
    sgs = dram("sgs", [HD, seq], F32, "ExternalInput")
    k2c = dram("k2c", [HD, seq], F32, "ExternalInput")
    sgc = dram("sgc", [HD, seq], F32, "ExternalInput")
    omh = dram("omh", [HD, 1], F32, "ExternalInput")
    oml = dram("oml", [HD, 1], F32, "ExternalInput")
    y = dram("y", [rows, DM], F32, "ExternalOutput")

    lnsp = dram("lnsp", [2, seq, DM], BF16, IK)
    qsp = dram("qsp", [HPC, 2, HD, seq], BF16, IK)
    csp = dram("csp", [HPC, HD, seq], F32, IK)
    ctab = dram("ctab", [HD, seq], F32, "Internal")
    stab = dram("stab", [HD, seq], F32, "Internal")
    hsp = dram("hsp", [2, nfo, 128, rows], BF16, IK)
    gsph = dram("gsph", [HPC * HD, seq], BF16, IK)
    gspl = dram("gspl", [HPC * HD, seq], BF16, IK)
    rs_in = [dram(f"rs_in{k}", [NCORES, 2 * HD, rows], F32, "Internal") for k in range(NCORES)]
    rs_out = [dram(f"rs_out{k}", [2 * HD, rows], F32, "Internal") for k in range(NCORES)]
    rs_cp = [dram(f"rs_cp{k}", [2 * HD, rows], F32, "Internal") for k in range(NCORES)]
    q1_in = dram("q1_in", [w1r, DFF], BF16, "Internal")
    q1tap = dram("q1tap", [w1r, DFF], BF16, IK) if taps else None
    q1otap = dram("q1otap", [DM, DFF], BF16, IK) if taps else None
    q1_out = drams("q1_out", [DM, DFF], BF16)
    q2_in = dram("q2_in", [w2r, DM], BF16, "Internal")
    q2_out = drams("q2_out", [DFF, DM], BF16)
    x2tap = dram("x2tap", [DM, rows], F32, IK) if taps else None
    dbgq = dram("dbgq", [1, 8], F32, IK) if taps else None
    ar8tap = dram("ar8tap", [8, 128], F32, IK) if taps else None
    rstap = dram("rstap", [NCORES, 2 * HD, rows], F32, IK) if taps else None
    rsotap = dram("rsotap", [2 * HD, rows], F32, IK) if taps else None
    rsused = dram("rsused", [FC, 128, rows], F32, IK) if taps else None
    avtap = dram("avtap", [HD, seq], F32, IK) if taps else None
    fence_in = dram("fence_in", [1, 128], F32, "Internal")
    fence_out = dram("fence_out", [8, 128], F32, "Internal")
    ar_in = dram("ar_in", [1, 128], F32, "Internal")
    ar_out = dram("ar_out", [8, 128], F32, "Internal")
    RG = [list(range(NCORES))]

    es = contextlib.ExitStack()
    with es:
        def sb(name, shape, dt=F32, st=None):
            stk = st if st is not None else es
            return stk.enter_context(nc.sbuf_tensor(name, shape, dt))

        def ps(name, shape, st=None):
            stk = st if st is not None else es
            return stk.enter_context(nc.psum_tensor(name, shape, F32))

        p = P(nc, es)
        S = p.sem

        onesb = sb("onesb", [128, 1], BF16)
        onesf = sb("onesf", [128, 1], F32)
        ones1f = sb("ones1f", [1, 128], F32)
        ident = sb("ident", [128, 128], F32)
        maskb = sb("maskb", [128, 896], F32)
        nwc = sb("nwc", [HD, 1], F32)
        omhc = sb("omhc", [HD, 1], F32)
        omlc = sb("omlc", [HD, 1], F32)
        b2c = sb("b2c", [128, FC], F32)
        c20 = sb("c20", [128, 1], F32)
        b1c = sb("b1c", [128, DFF // 128], F32)

        block = es.enter_context(nc.Block())

        dW = S("dW")
        dX = S("dX")
        dLT = S("dLT")
        dSW = S("dSW")
        dG = S("dG")
        dSY = S("dSY")
        vZ = S("vZ")


        p.op("gpsimd", lambda e: e.iota(maskb[:], pattern=[[1, 896]], base=-384,
                                        channel_multiplier=-1, allow_small_or_imprecise_dtypes=True), inc=vZ)
        p.op("gpsimd", lambda e: e.iota(ident[:], pattern=[[1, 128]], base=0,
                                        channel_multiplier=-1, allow_small_or_imprecise_dtypes=True), inc=vZ)
        p.op("vector", lambda e: e.memset(onesb[:], 1.0))
        p.op("vector", lambda e: e.memset(onesf[:], 1.0))
        p.op("vector", lambda e: e.memset(ones1f[:], 1.0))
        p.op("vector", lambda e: e.memset(c20[:], -20.0))
        p.wait("vector", vZ)
        p.op("vector", lambda e: e.tensor_single_scalar(maskb[:], maskb[:], 0.0, ALU.is_ge), drain=True)
        p.op("vector", lambda e: e.tensor_single_scalar(ident[:], ident[:], 0.0, ALU.is_equal), inc=vZ, drain=True)

        p.dma("scalar", nwc[:], nw[:, :], dW)
        p.dma("scalar", omhc[:], omh[:, :], dW)
        p.dma("scalar", omlc[:], oml[:, :], dW)
        p.dma("scalar", b2c[:], b2[:, :], dW)
        p.dma("scalar", b1c[:], b1[:, :], dW)

        vWs = S("vWs")

        stA = contextlib.ExitStack()
        xa = [sb(f"xa{i}", [128, DM], F32, stA) for i in range(2)]
        lnhb = [sb(f"lnhb{i}", [128, DM], BF16, stA) for i in range(2)]
        lnlb = [sb(f"lnlb{i}", [128, DM], BF16, stA) for i in range(2)]
        lfb = sb("lfb", [128, DM], F32, stA)
        sqd = sb("sqd", [128, DM], F32, stA)
        s1c = sb("s1c", [128, 1], F32, stA); s2c = sb("s2c", [128, 1], F32, stA)
        mcl = sb("mcl", [128, 1], F32, stA); nmc = sb("nmc", [128, 1], F32, stA)
        msc = sb("msc", [128, 1], F32, stA); vvc = sb("vvc", [128, 1], F32, stA)
        sdc = sb("sdc", [128, 1], F32, stA); rsc = sb("rsc", [128, 1], F32, stA)
        nbc = sb("nbc", [128, 1], F32, stA)
        sqA = S("sqA"); vstA = S("vstA"); sdA = S("sdA"); vrsA = S("vrsA")
        shiA = S("shiA"); vfinA = S("vfinA"); vaffA = S("vaffA")
        dXp = [S("dX0"), S("dX1")]
        dGp = [S("dGa0"), S("dGa1")]

        for i in range(nA):
            b = i % 2
            if i >= 2:
                p.wait("sync", vfinA)
                p.wait("sync", dGp[b])
            p.dma("sync", xa[b][:], x[128 * i:128 * (i + 1), :], dXp[b])
            p.wait("scalar", dXp[b])
            p.op("scalar", lambda e, b=b: e.activation(sqd[:], xa[b][:], AF.Square, accum_out=s2c[:]), inc=sqA)
            p.wait("vector", dXp[b])
            p.op("vector", lambda e, b=b: e.tensor_reduce(s1c[:], xa[b][:], axis=mybir.AxisListType.X, op=ALU.add), drain=True)
            p.wait("vector", sqA)
            p.wait("vector", shiA)
            p.op("vector", lambda e: e.tensor_single_scalar(mcl[:], s1c[:], 1.0 / DM, ALU.mult), drain=True)
            p.op("vector", lambda e: e.tensor_single_scalar(nmc[:], mcl[:], -1.0, ALU.mult), drain=True)
            p.op("vector", lambda e: e.tensor_single_scalar(msc[:], s2c[:], 1.0 / DM, ALU.mult), drain=True)
            p.op("vector", lambda e: e.scalar_tensor_tensor(vvc[:], mcl[:], nmc[:, 0:1], msc[:], ALU.mult, ALU.add), drain=True)
            p.op("vector", lambda e: e.tensor_scalar(vvc[:], vvc[:], E1, E2, ALU.add, ALU.add), inc=vstA, drain=True)
            p.wait("scalar", vstA)
            p.op("scalar", lambda e: e.activation(sdc[:], vvc[:], AF.Sqrt), inc=sdA)
            p.wait("vector", sdA)
            p.op("vector", lambda e: e.reciprocal(rsc[:], sdc[:]), drain=True)
            p.op("vector", lambda e: e.tensor_tensor(nbc[:], nmc[:], rsc[:], ALU.mult), inc=vrsA, drain=True)
            p.wait("vector", vrsA, vrsA.n)
            p.op("vector", lambda e, b=b: e.tensor_scalar(lfb[:], xa[b][:], rsc[:, 0:1], nbc[:, 0:1],
                                                          ALU.mult, ALU.add), inc=vaffA, drain=True)
            p.wait("scalar", vaffA)
            p.op("scalar", lambda e, b=b: e.copy(lnhb[b][:], lfb[:]), inc=shiA)
            p.wait("vector", shiA)
            p.op("vector", lambda e, b=b: e.tensor_tensor(lnlb[b][:], lfb[:], lnhb[b][:], ALU.subtract),
                 inc=vfinA, drain=True)
            p.wait("gpsimd", vfinA)
            p.dma("gpsimd", lnsp[0, 128 * i:128 * (i + 1), :], lnhb[b][:], dGp[b])
            p.dma("gpsimd", lnsp[1, 128 * i:128 * (i + 1), :], lnlb[b][:], dGp[b])

        p.wait("gpsimd", dGp[0])
        p.wait("gpsimd", dGp[1])
        p.barrier("A")
        stA.close()

        stB = contextlib.ExitStack()
        wqkvg = [[sb(f"w{t}{i}", [128, FC, 128], BF16, stB) for i in range(2)]
                 for t in "qkvg"]
        Kr = [sb(f"Kr{i}", [128, nkc, 128], BF16, stB) for i in range(2)]
        Vr = [sb(f"Vr{i}", [128, nkc, 128], BF16, stB) for i in range(2)]
        qhiA = [sb(f"qhiA{i}", [128, QB], BF16, stB) for i in range(2)]
        qloA = [sb(f"qloA{i}", [128, QB], BF16, stB) for i in range(2)]
        cmbA = [sb(f"cmbA{i}", [128, QB], F32, stB) for i in range(2)]
        pexp = [sb(f"pexp{i}", [128, QB], F32, stB) for i in range(2)]
        phi = [sb(f"phi{i}", [128, QB], BF16, stB) for i in range(2)]
        plo = [sb(f"plo{i}", [128, QB], BF16, stB) for i in range(2)]
        zrow = sb("zrow", [1, QB], F32, stB)
        rz = sb("rz", [1, QB], F32, stB)
        bc1 = sb("bc1", [128, QB], F32, stB)
        avn = sb("avn", [128, QB], F32, stB)
        sqb = sb("sqb", [128, QB], F32, stB)
        msqr = sb("msqr", [1, QB], F32, stB)
        rr_ = sb("rr_", [1, QB], F32, stB)
        bc2 = sb("bc2", [128, QB], F32, stB)
        gat = sb("gat", [128, QB], F32, stB)
        gth = sb("gth", [128, QB], BF16, stB)
        gtl = sb("gtl", [128, QB], BF16, stB)
        stSw = contextlib.ExitStack()
        lnT = sb("lnT", [128, 2, 2, FC, TS], BF16, stSw)
        wraw = sb("wraw", [128, FC, 128], F32, stSw)
        wscr = sb("wscr", [128, FC, 128], F32, stSw)
        tio = sb("tio", [128, TS], F32, stSw)
        ph_ = sb("ph_", [128, TS], F32, stSw)
        rA = sb("rA", [128, TS], F32, stSw)
        rB = sb("rB", [128, TS], F32, stSw)
        snb = sb("snb", [128, TS], F32, stSw)
        costile = sb("costile", [128, TS], F32, stSw)
        sintile = sb("sintile", [128, TS], F32, stSw)
        k2sT = sb("k2sT", [128, TS], F32, stSw); sgsT = sb("sgsT", [128, TS], F32, stSw)
        k2cT = sb("k2cT", [128, TS], F32, stSw); sgcT = sb("sgcT", [128, TS], F32, stSw)
        qf = sb("qf", [128, TS], F32, stSw)
        qsw = sb("qsw", [128, TS], F32, stSw)
        qr1 = sb("qr1", [128, TS], F32, stSw)
        qr2 = sb("qr2", [128, TS], F32, stSw)
        qhiT = sb("qhiT", [128, TS], BF16, stSw)
        qloT = sb("qloT", [128, TS], BF16, stSw)
        kf = sb("kf", [128, TS], F32, stSw)
        ksw = sb("ksw", [128, TS], F32, stSw)
        gsb = sb("gsb", [128, TS], F32, stSw)
        cmb = sb("cmb", [128, TS], F32, stSw)
        stBp = contextlib.ExitStack()
        P6 = [ps(f"P6_{i}", [128, QB], stBp) for i in range(6)]
        P7 = ps("P7", [128, QB], stBp)
        P8 = ps("P8", [128, QB], stBp)
        PQs = [P6[0][:, 0:TS], P6[4][:, 0:TS]]
        PKs = [P6[1][:, 0:TS], P6[5][:, 0:TS]]
        PGs = [P6[2][:, 0:TS], P7[:, 0:TS]]
        PVs = [P6[3][:, 0:TS], P8[:, 0:TS]]

        dR = S("dR")
        dLTp = [S("dLT0"), S("dLT1")]
        dSWq = S("dSWq"); dSWk = S("dSWk")
        dGq = S("dGq")
        vTab = S("vTab"); sTab = S("sTab")
        tQd = S("tQd"); tKd = S("tKd"); tGd = S("tGd"); tVd = S("tVd")
        sQe = S("sQe"); sKe = S("sKe"); sGe = S("sGe")
        vQe = S("vQe"); vKe = S("vKe"); vGe = S("vGe"); vVe = S("vVe")


        tSc = S("tSc"); sEx = S("sEx"); vPs = S("vPs"); tPe = S("tPe")
        dQp = [S("dQA0"), S("dQA1")]
        sZe = S("sZe"); vRz = S("vRz"); tB1 = S("tB1"); sB1 = S("sB1")
        vAvn = S("vAvn"); sSq = S("sSq"); tM = S("tM"); sMs = S("sMs")
        vM1 = S("vM1"); sS2 = S("sS2"); vRr = S("vRr"); tB2 = S("tB2"); sB2 = S("sB2")
        vGa = S("vGa"); dQA = S("dQA")

        def load_weights(h):
            for pj in range(4):
                p.wait("scalar", vWs, vWs.n)
                p.dma("scalar", wraw[:, :, :],
                      wproj[h, pj, :, :].rearrange("(a c) d -> c a d", c=128), dW)
                p.wait("vector", dW)
                dhi, dlo = wqkvg[pj]
                if pj == 0:
                    p.op("vector", lambda e: e.tensor_single_scalar(wscr[:], wraw[:], S1, ALU.mult), drain=True)
                    p.op("vector", lambda e: e.scalar_tensor_tensor(wscr[:], wraw[:], S2, wscr[:], ALU.mult, ALU.add), drain=True)
                    s_ = wscr
                else:
                    s_ = wraw
                p.op("vector", lambda e, d=dhi, s=s_: e.tensor_copy(d[:], s[:]), drain=True)
                p.op("vector", lambda e, dh_=dhi, dl=dlo, s=s_:
                     e.tensor_tensor(dl[:], s[:], dh_[:], ALU.subtract), inc=vWs, drain=True)

        def sweep(h):
            load_weights(h)
            tQd_h = []; tVd_h = []; sQe_h = []; sKe_h = []; sGe_h = []; vVe_h = []
            for tt in range(nts):
                b = tt % 2
                T = slice(TS * tt, TS * (tt + 1))
                PQ = PQs[b]; PK = PKs[b]; PG = PGs[b]; PV = PVs[b]
                PVfull = (P6[3] if b == 0 else P8)
                if tt >= 2:
                    p.wait("sync", tQd, tQd_h[tt - 2])
                    p.wait("sync", tVd, tVd_h[tt - 2])
                for half in range(2):
                    for fc in range(FC):
                        p.dma("sync", lnT[:, b, half, fc, :],
                              lnsp[half, T, 128 * fc:128 * (fc + 1)], dLTp[b], transpose=True)
                if h == 0:
                    p.wait("gpsimd", vTab)
                    p.wait("gpsimd", dGq)
                    p.dma("gpsimd", k2sT[:], k2s[:, T], dR)
                    p.dma("gpsimd", sgsT[:], sgs[:, T], dR)
                    p.dma("gpsimd", k2cT[:], k2c[:, T], dR)
                    p.dma("gpsimd", sgcT[:], sgc[:, T], dR)
                    p.op("gpsimd", lambda e, tt=tt: e.iota(tio[:], pattern=[[1, TS]], base=TS * tt,
                                                           channel_multiplier=0,
                                                           allow_small_or_imprecise_dtypes=True), inc=dR)
                    p.wait("vector", dR)
                    V_ = "vector"
                    p.op(V_, lambda e: e.tensor_single_scalar(ph_[:], tio[:], omhc[:, 0:1], ALU.mult), drain=True)
                    p.op(V_, lambda e: e.scalar_tensor_tensor(ph_[:], tio[:], omlc[:, 0:1], ph_[:], ALU.mult, ALU.add), drain=True)
                    p.op(V_, lambda e: e.scalar_tensor_tensor(rA[:], k2sT[:], -PI1, ph_[:], ALU.mult, ALU.add), drain=True)
                    p.op(V_, lambda e: e.scalar_tensor_tensor(rB[:], k2sT[:], -PI2, rA[:], ALU.mult, ALU.add), drain=True)
                    p.op(V_, lambda e: e.scalar_tensor_tensor(rA[:], k2sT[:], -PI3, rB[:], ALU.mult, ALU.add), inc=vTab, drain=True)
                    p.wait("scalar", vTab)
                    p.op("scalar", lambda e: e.activation(snb[:], rA[:], AF.Sin), inc=sTab)
                    p.wait(V_, sTab)
                    p.op(V_, lambda e: e.tensor_tensor(sintile[:], snb[:], sgsT[:], ALU.mult), inc=vTab, drain=True)
                    p.op(V_, lambda e: e.tensor_scalar(rB[:], ph_[:], H1, H2, ALU.add, ALU.add), drain=True)
                    p.op(V_, lambda e: e.scalar_tensor_tensor(rA[:], k2cT[:], -PI1, rB[:], ALU.mult, ALU.add), drain=True)
                    p.op(V_, lambda e: e.scalar_tensor_tensor(rB[:], k2cT[:], -PI2, rA[:], ALU.mult, ALU.add), drain=True)
                    p.op(V_, lambda e: e.scalar_tensor_tensor(rA[:], k2cT[:], -PI3, rB[:], ALU.mult, ALU.add), inc=vTab, drain=True)
                    p.wait("scalar", vTab)
                    p.op("scalar", lambda e: e.activation(snb[:], rA[:], AF.Sin), inc=sTab)
                    p.wait(V_, sTab)
                    p.op(V_, lambda e: e.tensor_tensor(costile[:], snb[:], sgcT[:], ALU.mult), inc=vTab, drain=True)
                    p.wait("gpsimd", vTab)
                    p.dma("gpsimd", ctab[:, T], costile[:], dGq)
                    p.dma("gpsimd", stab[:, T], sintile[:], dGq)
                else:
                    p.wait("gpsimd", vQe)
                    p.wait("gpsimd", vKe)
                    p.wait("gpsimd", dGq)
                    p.dma("gpsimd", costile[:], ctab[:, T], dR)
                    p.dma("gpsimd", sintile[:], stab[:, T], dR)

                p.wait("tensor", dLTp[b])
                if tt == 0:
                    p.wait("tensor", vWs)
                if tt >= 2:
                    p.wait("tensor", sQe, sQe_h[tt - 2]); p.wait("tensor", sKe, sKe_h[tt - 2])
                    p.wait("tensor", sGe, sGe_h[tt - 2]); p.wait("tensor", vVe, vVe_h[tt - 2])
                wqh, wql = wqkvg[0]; wkh, wkl = wqkvg[1]
                wvh, wvl = wqkvg[2]; wgh, wgl = wqkvg[3]
                for (PP, whi_, wlo_, tsem) in ((PQ, wqh, wql, tQd), (PK, wkh, wkl, tKd), (PG, wgh, wgl, tGd)):
                    for fc in range(FC):
                        p.op("tensor", lambda e, PP=PP, w=whi_, fc=fc, b=b:
                             e.matmul(PP, w[:, fc], lnT[:, b, 0, fc], start=(fc == 0), stop=False))
                        p.op("tensor", lambda e, PP=PP, w=whi_, fc=fc, b=b:
                             e.matmul(PP, w[:, fc], lnT[:, b, 1, fc], start=False, stop=False))
                        p.op("tensor", lambda e, PP=PP, w=wlo_, fc=fc, b=b:
                             e.matmul(PP, w[:, fc], lnT[:, b, 0, fc], start=False, stop=(fc == FC - 1)),
                             inc=(tsem if fc == FC - 1 else None))
                for s4 in range(TS // 128):
                    for fc in range(FC):
                        sl = slice(128 * s4, 128 * (s4 + 1))
                        p.op("tensor", lambda e, PVf=PVfull, fc=fc, b=b, sl=sl:
                             e.matmul(PVf[:, sl], lnT[:, b, 0, fc, sl], wvh[:, fc], start=(fc == 0), stop=False))
                        p.op("tensor", lambda e, PVf=PVfull, fc=fc, b=b, sl=sl:
                             e.matmul(PVf[:, sl], lnT[:, b, 0, fc, sl], wvl[:, fc], start=False, stop=False))
                        p.op("tensor", lambda e, PVf=PVfull, fc=fc, b=b, sl=sl:
                             e.matmul(PVf[:, sl], lnT[:, b, 1, fc, sl], wvh[:, fc], start=False, stop=(fc == FC - 1)),
                             inc=(tVd if (fc == FC - 1 and s4 == TS // 128 - 1) else None))

                p.wait("scalar", tQd)
                p.wait("scalar", vQe, vQe.n)
                tQd_h.append(tQd.n); tKd_h = None
                p.op("scalar", lambda e, PQ=PQ: e.copy(qf[:], PQ), inc=sQe)
                p.wait("sync", sQe)
                p.dma("sync", qsw[0:64, :], qf[64:128, :], dSWq)
                p.dma("sync", qsw[64:128, :], qf[0:64, :], dSWq)
                p.wait("vector", dSWq)
                p.wait("vector", vTab if h == 0 else dR)
                p.wait("vector", dGq)
                p.op("vector", lambda e: e.tensor_tensor(qr1[:], qf[:], costile[:], ALU.mult))
                p.op("vector", lambda e: e.tensor_tensor(qr2[:], qsw[:], sintile[:], ALU.mult), drain=True)
                p.op("vector", lambda e: e.tensor_tensor(qr1[:], qr1[:], qr2[:], ALU.add), drain=True)
                p.op("vector", lambda e: e.tensor_copy(qhiT[:], qr1[:]), drain=True)
                p.op("vector", lambda e: e.tensor_tensor(qloT[:], qr1[:], qhiT[:], ALU.subtract), inc=vQe, drain=True)
                p.wait("gpsimd", vQe)
                p.dma("gpsimd", qsp[h, 0, :, T], qhiT[:], dGq)
                p.dma("gpsimd", qsp[h, 1, :, T], qloT[:], dGq)

                p.wait("scalar", tKd)
                p.wait("scalar", vKe, vKe.n)
                p.op("scalar", lambda e, PK=PK: e.copy(kf[:], PK), inc=sKe)
                p.wait("sync", sKe)
                p.dma("sync", ksw[0:64, :], kf[64:128, :], dSWk)
                p.dma("sync", ksw[64:128, :], kf[0:64, :], dSWk)
                p.wait("vector", dSWk)
                kcs = slice(2 * tt, 2 * (tt + 1))
                p.op("vector", lambda e: e.tensor_tensor(qr1[:], kf[:], costile[:], ALU.mult))
                p.op("vector", lambda e: e.tensor_tensor(qr2[:], ksw[:], sintile[:], ALU.mult), drain=True)
                p.op("vector", lambda e: e.tensor_tensor(qr1[:], qr1[:], qr2[:], ALU.add), drain=True)
                p.op("vector", lambda e, kcs=kcs: e.tensor_copy(Kr[0][:, kcs, :], qr1[:]), drain=True)
                p.op("vector", lambda e, kcs=kcs: e.tensor_tensor(Kr[1][:, kcs, :], qr1[:], Kr[0][:, kcs, :], ALU.subtract), inc=vKe, drain=True)

                p.wait("scalar", tGd)
                p.wait("scalar", vGe, vGe.n)
                p.op("scalar", lambda e, PG=PG: e.activation(gsb[:], PG, AF.Silu), inc=sGe)
                p.wait("vector", sGe)
                p.op("vector", lambda e: e.tensor_single_scalar(cmb[:], gsb[:], nwc[:, 0:1], ALU.mult), inc=vGe, drain=True)
                p.wait("gpsimd", vGe)
                p.dma("gpsimd", csp[h, :, T], cmb[:], dGq)

                p.wait("vector", tVd)
                p.op("vector", lambda e, kcs=kcs, PV=PV: e.tensor_copy(Vr[0][:, kcs, :], PV), drain=True)
                p.op("vector", lambda e, kcs=kcs, PV=PV: e.tensor_tensor(Vr[1][:, kcs, :], PV, Vr[0][:, kcs, :], ALU.subtract), inc=vVe, drain=True)
                sQe_h.append(sQe.n); sKe_h.append(sKe.n); sGe_h.append(sGe.n)
                vVe_h.append(vVe.n); tVd_h.append(tVd.n)

        def attention(h):
            qld = "gpsimd" if h == 0 else "scalar"
            qst = "gpsimd" if h == 0 else "sync"
            dST = dG if h == 0 else dSY
            p.wait(qld, dGq)
            for qb in range(nqb):
                ab = qb % 2
                dQ = dQp[ab]
                QT = slice(QB * qb, QB * (qb + 1))
                if qb >= 2:
                    p.wait(qld, tSc)
                    p.wait(qld, vGa, vGa.n)
                p.dma(qld, qhiA[ab][:], qsp[h, 0, :, QT], dQ)
                p.dma(qld, qloA[ab][:], qsp[h, 1, :, QT], dQ)
                p.dma(qld, cmbA[ab][:], csp[h, :, QT], dQ)
                nkq = 4 * (qb + 1)
                vPs_h = []
                tPe_h = []
                for kc in range(nkq):
                    pb = kc % 2
                    PSb = PS2[pb]
                    if kc == 0:
                        p.wait("tensor", dQ)
                    if kc >= 2:
                        p.wait("tensor", vPs, vPs_h[kc - 2])
                    p.op("tensor", lambda e, PSb=PSb, kc=kc, ab=ab:
                         e.matmul(PSb[:], Kr[0][:, kc, :], qhiA[ab][:], start=True, stop=False))
                    p.op("tensor", lambda e, PSb=PSb, kc=kc, ab=ab:
                         e.matmul(PSb[:], Kr[0][:, kc, :], qloA[ab][:], start=False, stop=False))
                    p.op("tensor", lambda e, PSb=PSb, kc=kc, ab=ab:
                         e.matmul(PSb[:], Kr[1][:, kc, :], qhiA[ab][:], start=False, stop=True), inc=tSc)
                    p.wait("scalar", tSc)
                    if kc >= 2:
                        p.wait("scalar", vPs, vPs_h[kc - 2])
                    p.op("scalar", lambda e, pb=pb, PSb=PSb:
                         e.activation(pexp[pb][:], PSb[:], AF.Exp, bias=c20[:, 0:1]), inc=sEx)
                    p.wait("vector", sEx)
                    if kc >= 4 * qb:
                        j = kc - 4 * qb
                        off = 384 - 128 * j
                        p.op("vector", lambda e, pb=pb, off=off:
                             e.tensor_tensor(pexp[pb][:], pexp[pb][:], maskb[:, off:off + QB], ALU.mult), drain=True)
                    if kc >= 2:
                        p.wait("vector", tPe, tPe_h[kc - 2])
                    p.op("vector", lambda e, pb=pb: e.tensor_copy(phi[pb][:], pexp[pb][:]), drain=True)
                    p.op("vector", lambda e, pb=pb: e.tensor_tensor(plo[pb][:], pexp[pb][:], phi[pb][:], ALU.subtract),
                         inc=vPs, drain=True)
                    vPs_h.append(vPs.n)
                    p.wait("tensor", vPs)
                    last = kc == nkq - 1
                    p.op("tensor", lambda e, kc=kc, pb=pb, qb=qb:
                         e.matmul(PAV[:], Vr[0][:, kc, :], phi[pb][:], start=(kc == 0), stop=False))
                    p.op("tensor", lambda e, kc=kc, pb=pb:
                         e.matmul(PAV[:], Vr[0][:, kc, :], plo[pb][:], start=False, stop=False))
                    p.op("tensor", lambda e, kc=kc, pb=pb, last=last:
                         e.matmul(PAV[:], Vr[1][:, kc, :], phi[pb][:], start=False, stop=last))
                    p.op("tensor", lambda e, kc=kc, pb=pb:
                         e.matmul(PZ, onesb[:], phi[pb][:], start=(kc == 0), stop=False))
                    p.op("tensor", lambda e, kc=kc, pb=pb, last=last:
                         e.matmul(PZ, onesb[:], plo[pb][:], start=False, stop=last), inc=tPe)
                    tPe_h.append(tPe.n)
                p.wait("scalar", tPe)
                p.op("scalar", lambda e: e.copy(zrow[:], PZ), inc=sZe)
                p.wait("vector", sZe)
                p.op("vector", lambda e: e.reciprocal(rz[:], zrow[:]), inc=vRz, drain=True)
                p.wait("tensor", vRz)
                p.op("tensor", lambda e: e.matmul(PB1[:], ones1f[:], rz[:], start=True, stop=True), inc=tB1)
                p.wait("scalar", tB1)
                p.op("scalar", lambda e: e.copy(bc1[:], PB1[:]), inc=sB1)
                p.wait("vector", sB1)
                p.op("vector", lambda e: e.tensor_tensor(avn[:], PAV[:], bc1[:], ALU.mult), inc=vAvn, drain=True)
                if taps and h == 0:
                    p.wait("gpsimd" if h == 0 else "sync", vAvn)
                    p.dma("gpsimd" if h == 0 else "sync", avtap[:, QT], avn[:], dST)
                p.wait("scalar", vAvn)
                p.op("scalar", lambda e: e.activation(sqb[:], avn[:], AF.Square), inc=sSq)
                p.wait("tensor", sSq)
                p.wait("tensor", vAvn)
                p.op("tensor", lambda e: e.matmul(PM, onesf[:], sqb[:], start=True, stop=True), inc=tM)
                p.wait("scalar", tM)
                p.op("scalar", lambda e: e.copy(msqr[:], PM), inc=sMs)
                p.wait("vector", sMs)
                p.op("vector", lambda e: e.tensor_scalar(msqr[:], msqr[:], 1.0 / 128.0, E1, ALU.mult, ALU.add), drain=True)
                p.op("vector", lambda e: e.tensor_single_scalar(msqr[:], msqr[:], E2, ALU.add), inc=vM1, drain=True)
                p.wait("scalar", vM1)
                p.op("scalar", lambda e: e.activation(msqr[:], msqr[:], AF.Sqrt), inc=sS2)
                p.wait("vector", sS2)
                p.op("vector", lambda e: e.reciprocal(rr_[:], msqr[:]), inc=vRr, drain=True)
                p.wait("tensor", vRr)
                p.op("tensor", lambda e: e.matmul(PB2[:], ones1f[:], rr_[:], start=True, stop=True), inc=tB2)
                p.wait("scalar", tB2)
                p.op("scalar", lambda e: e.copy(bc2[:], PB2[:]), inc=sB2)
                p.wait("vector", sB2)
                p.wait("vector", dQ)
                p.op("vector", lambda e: e.tensor_tensor(avn[:], avn[:], bc2[:], ALU.mult), drain=True)
                p.wait("vector", dST, dST.n)
                p.op("vector", lambda e, ab=ab: e.tensor_tensor(gat[:], avn[:], cmbA[ab][:], ALU.mult), drain=True)
                p.op("vector", lambda e: e.tensor_copy(gth[:], gat[:]), drain=True)
                p.op("vector", lambda e: e.tensor_tensor(gtl[:], gat[:], gth[:], ALU.subtract), inc=vGa, drain=True)
                p.wait(qst, vGa)
                p.dma(qst, gsph[HD * h:HD * (h + 1), QT], gth[:], dST)
                p.dma(qst, gspl[HD * h:HD * (h + 1), QT], gtl[:], dST)

        PS2 = [P6[0], P6[1]]
        PAV = P6[2]
        PZ = P6[3][0:1, :]
        PB1 = P6[4]
        PM = P6[5][0:1, :]
        PB2 = P6[4]

        sweep(0)
        p.wait("gpsimd", dGq)
        p.barrier("s0")
        attention(0)
        p.barrier("a0")
        sweep(1)
        p.wait("gpsimd", dGq)
        p.barrier("s1")
        stSw.close()

        stQ = contextlib.ExitStack()
        wslc = sb("wslc", [128, 1024], F32, stQ)
        racc1 = sb("racc1", [128, 1], F32, stQ)
        racc2 = sb("racc2", [128, 1], F32, stQ)
        rtmp = sb("rtmp", [128, 1], F32, stQ)
        ppb = sb("ppb", [128, 1024], F32, stQ)
        nnb = sb("nnb", [128, 1024], F32, stQ)
        qtb = sb("qtb", [128, 1024], BF16, stQ)
        s12 = sb("s12", [1, 128], F32, stQ)
        ar8 = sb("ar8", [8, 128], F32, stQ)
        ones8 = sb("ones8", [8, 1], F32, stQ)
        fin_sb = sb("fin_sb", [1, 128], F32, stQ)
        arb = sb("arb", [1, 2], F32, stQ)
        thb = sb("thb", [1, 2], F32, stQ)
        nthb = sb("nthb", [1, 2], F32, stQ)
        thbc = sb("thbc", [128, 2], F32, stQ)
        nthbc = sb("nthbc", [128, 2], F32, stQ)
        PQm = P6[5][0:1, 0:2]
        dQz = S("dQz"); vQz = S("vQz"); tQz = S("tQz"); sQz = S("sQz"); gQz = S("gQz")

        p.op("vector", lambda e: e.memset(ones8[:], 1.0))
        p.op("vector", lambda e: e.memset(s12[:], 0.0))
        p.op("vector", lambda e: e.memset(fin_sb[:], 0.0), drain=True)
        p.op("vector", lambda e: e.memset(racc1[:], 0.0))
        p.op("vector", lambda e: e.memset(racc2[:], 0.0), drain=True)
        npc1 = (w1r * DFF) // (128 * 1024)
        npc2 = (w2r * DM) // (128 * 1024)
        def wpiece(mat, rowsz, i):
            wch = int(mat.tensor.shape[1]) // 1024
            rj, cj = divmod(i, wch)
            return mat[128 * rj:128 * (rj + 1), 1024 * cj:1024 * (cj + 1)]
        for mat, npc, racc in ((w1s, npc1, racc1), (w2s, npc2, racc2)):
            for i in range(npc):
                p.wait("scalar", vQz)
                p.dma("scalar", wslc[:], wpiece(mat, None, i), dQz)
                p.wait("vector", dQz)
                p.op("vector", lambda e: e.tensor_reduce(rtmp[:], wslc[:], axis=mybir.AxisListType.X,
                                                         op=ALU.add, apply_absolute_value=True), drain=True)
                p.op("vector", lambda e, racc=racc: e.tensor_tensor(racc[:], racc[:], rtmp[:], ALU.add),
                     inc=vQz, drain=True)
        p.wait("tensor", vQz)
        p.op("tensor", lambda e: e.matmul(P6[5][0:1, 0:1], onesf[:], racc1[:], start=True, stop=True))
        p.op("tensor", lambda e: e.matmul(P6[5][0:1, 1:2], onesf[:], racc2[:], start=True, stop=True), inc=tQz)
        p.wait("scalar", tQz)
        p.op("scalar", lambda e: e.copy(s12[0:1, 0:2], PQm), inc=sQz)
        p.wait("gpsimd", sQz)
        p.dma("gpsimd", ar_in[:, :], s12[:], dQz)
        p.dma("gpsimd", fence_in[:, :], fin_sb[:], dQz)
        p.wait("gpsimd", dQz)
        p.op("gpsimd", lambda e: e.collective_compute("AllGather", ALU.bypass, replica_groups=RG,
                                                      ins=[ar_in[:, :]], outs=[ar_out[:, :]]), inc=gQz)
        p.op("gpsimd", lambda e: e.collective_compute("AllGather", ALU.bypass, replica_groups=RG,
                                                      ins=[fence_in[:, :]], outs=[fence_out[:, :]]), inc=gQz)

        attention(1)
        p.wait("gpsimd", dGq)
        p.wait("sync", dSY, dSY.n)
        p.barrier("a1")

        p.dma("gpsimd", ar8[:], ar_out[:, :], dQz)
        p.wait("tensor", dQz)
        p.op("tensor", lambda e: e.matmul(P6[5][0:1, 4:6], ones8[:], ar8[:, 0:2], start=True, stop=True), inc=tQz)
        p.wait("scalar", tQz)
        p.op("scalar", lambda e: e.copy(arb[:], P6[5][0:1, 4:6]), inc=sQz)
        p.wait("vector", sQz)
        p.op("vector", lambda e: e.tensor_single_scalar(arb[:], arb[:], 1.0 / (DFF * DM), ALU.mult), drain=True)
        p.op("vector", lambda e: e.tensor_single_scalar(thb[:], arb[:], C71, ALU.mult), drain=True)
        p.op("vector", lambda e: e.scalar_tensor_tensor(thb[:], arb[:], C72, thb[:], ALU.mult, ALU.add), drain=True)
        p.op("vector", lambda e: e.tensor_single_scalar(nthb[:], thb[:], -1.0, ALU.mult), inc=vQz, drain=True)
        if taps:
            p.wait("gpsimd", vQz)
            p.dma("gpsimd", dbgq[0:1, 0:2], s12[0:1, 0:2], dG)
            p.dma("gpsimd", dbgq[0:1, 2:4], arb[:], dG)
            p.dma("gpsimd", dbgq[0:1, 4:6], thb[:], dG)
            p.dma("gpsimd", ar8tap[:, :], ar8[:], dG)
        p.wait("tensor", vQz)
        p.op("tensor", lambda e: e.matmul(P6[4][:, 0:2], ones1f[:], thb[:], start=True, stop=True))
        p.op("tensor", lambda e: e.matmul(P6[4][:, 2:4], ones1f[:], nthb[:], start=True, stop=True), inc=tQz)
        p.wait("scalar", tQz)
        p.op("scalar", lambda e: e.copy(thbc[:], P6[4][:, 0:2]))
        p.op("scalar", lambda e: e.copy(nthbc[:], P6[4][:, 2:4]), inc=gQz)
        for mat, npc, qdst, col in ((w1s, npc1, q1_in, 0), (w2s, npc2, q2_in, 1)):
            for i in range(npc):
                p.wait("scalar", vQz)
                p.wait("scalar", dG, dG.n)
                p.dma("scalar", wslc[:], wpiece(mat, None, i), dQz)
                p.wait("vector", dQz)
                p.wait("vector", gQz)
                p.op("vector", lambda e, col=col: e.tensor_single_scalar(ppb[:], wslc[:], thbc[:, col:col + 1], ALU.is_ge))
                p.op("vector", lambda e, col=col: e.tensor_single_scalar(nnb[:], wslc[:], nthbc[:, col:col + 1], ALU.is_le), drain=True)
                p.op("vector", lambda e: e.tensor_tensor(qtb[:], ppb[:], nnb[:], ALU.subtract), inc=vQz, drain=True)
                p.wait("gpsimd", vQz)
                p.dma("gpsimd", wpiece(qdst, None, i), qtb[:], dG)
                if taps and qdst is q1_in:
                    p.dma("gpsimd", wpiece(q1tap, None, i), qtb[:], dG)
        p.wait("gpsimd", dG, dG.n)
        p.op("gpsimd", lambda e: e.collective_compute("AllGather", ALU.bypass, replica_groups=RG,
                                                      ins=[q1_in[:, :]], outs=[q1_out[:, :]]), inc=gQz)
        p.op("gpsimd", lambda e: e.collective_compute("AllGather", ALU.bypass, replica_groups=RG,
                                                      ins=[q2_in[:, :]], outs=[q2_out[:, :]]), inc=gQz)
        p.op("gpsimd", lambda e: e.collective_compute("AllGather", ALU.bypass, replica_groups=RG,
                                                      ins=[fence_in[:, :]], outs=[fence_out[:, :]]), inc=gQz)
        gQzFinal = gQz.n

        attention(1)
        p.wait("gpsimd", dG, dG.n)
        p.wait("sync", dSY, dSY.n)
        p.barrier("a1")
        stQ.close()
        stBp.close()
        stB.close()

        ccA = S("ccA")
        stB3 = contextlib.ExitStack()
        wosr = sb("wosr", [128, 2, DM], F32, stB3)
        woshh = sb("woshh", [128, 2, DM], BF16, stB3)
        wosll = sb("wosll", [128, 2, DM], BF16, stB3)
        gh3 = [sb(f"gh3{i}", [128, 2, QB], BF16, stB3) for i in range(2)]
        gl3 = [sb(f"gl3{i}", [128, 2, QB], BF16, stB3) for i in range(2)]
        pev = [sb(f"pev{i}", [128, QB], F32, stB3) for i in range(4)]
        PWp = [P6[i] for i in range(2)]
        dB3 = S("dB3"); vB3 = S("vB3"); tB3 = S("tB3"); sB3 = S("sB3")
        dB3p = [S("dB30"), S("dB31")]
        dGrs = [S("dGrs0"), S("dGrs1")]
        p.dma("scalar", wosr[:, 0, :], wos_d[0:128, :], dB3)
        p.dma("scalar", wosr[:, 1, :], wos_d[128:256, :], dB3)
        p.wait("vector", dB3)
        p.op("vector", lambda e: e.tensor_copy(woshh[:], wosr[:]), drain=True)
        p.op("vector", lambda e: e.tensor_tensor(wosll[:], wosr[:], woshh[:], ALU.subtract), inc=vB3, drain=True)
        for k in range(NCORES):
            for tt in range(nqb):
                QT = slice(QB * tt, QB * (tt + 1))
                pb3 = tt % 2
                if k + tt > 0:
                    p.wait("scalar", tB3)
                for hh in range(2):
                    p.dma("scalar", gh3[pb3][:, hh, :], gsph[HD * hh:HD * (hh + 1), QT], dB3p[pb3])
                    p.dma("scalar", gl3[pb3][:, hh, :], gspl[HD * hh:HD * (hh + 1), QT], dB3p[pb3])
                p.wait("tensor", vB3)
                p.wait("tensor", dB3)
                p.wait("tensor", dB3p[pb3])
                if k + tt > 0:
                    p.wait("tensor", sB3)
                for d in range(2):
                    dout0 = 256 * k + 128 * d
                    for hc in range(2):
                        st_ = hc == 0
                        sp_ = hc == 1
                        p.op("tensor", lambda e, d=d, hc=hc, dout0=dout0, pb3=pb3, st_=st_:
                             e.matmul(PWp[d][:], woshh[:, hc, dout0:dout0 + 128], gh3[pb3][:, hc, :], start=st_, stop=False))
                        p.op("tensor", lambda e, d=d, hc=hc, dout0=dout0, pb3=pb3:
                             e.matmul(PWp[d][:], woshh[:, hc, dout0:dout0 + 128], gl3[pb3][:, hc, :], start=False, stop=False))
                        p.op("tensor", lambda e, d=d, hc=hc, dout0=dout0, pb3=pb3, sp_=sp_:
                             e.matmul(PWp[d][:], wosll[:, hc, dout0:dout0 + 128], gh3[pb3][:, hc, :], start=False, stop=sp_),
                             inc=(tB3 if (d == 1 and hc == 1) else None))
                p.wait("scalar", tB3)
                p.wait("scalar", dGrs[pb3])
                p.op("scalar", lambda e, pb3=pb3: e.copy(pev[2 * pb3][:], PWp[0][:]))
                p.op("scalar", lambda e, pb3=pb3: e.copy(pev[2 * pb3 + 1][:], PWp[1][:]), inc=sB3)
                p.wait("sync", sB3)
                jj = (QB * tt) // rows
                col = (QB * tt) % rows
                p.dma("sync", rs_in[k][jj, 0:128, col:col + QB], pev[2 * pb3][:], dGrs[pb3])
                p.dma("sync", rs_in[k][jj, 128:256, col:col + QB], pev[2 * pb3 + 1][:], dGrs[pb3])
                if taps and k == 0:
                    p.dma("sync", rstap[jj, 0:128, col:col + QB], pev[2 * pb3][:], dGrs[pb3])
                    p.dma("sync", rstap[jj, 128:256, col:col + QB], pev[2 * pb3 + 1][:], dGrs[pb3])
            p.wait("gpsimd", dGrs[0])
            p.wait("gpsimd", dGrs[1])
            p.op("gpsimd", lambda e, k=k: e.collective_compute(
                "ReduceScatter", ALU.add, replica_groups=RG,
                ins=[rs_in[k][:, :, :]],
                outs=[rs_out[k][:, :]]), inc=ccA)
        p.op("gpsimd", lambda e: e.collective_compute("AllGather", ALU.bypass, replica_groups=RG,
                                                      ins=[fence_in[:, :]], outs=[fence_out[:, :]]), inc=ccA)
        p.wait("gpsimd", ccA)
        for k in range(NCORES):
            p.dma("gpsimd", rs_cp[k][:, :], rs_out[k][:, :], dG)
        if taps:
            p.dma("gpsimd", rsotap[:, :], rs_out[0][:, :], dG)
        stB3.close()
        dGcopy_mark = dG.n

        stC = contextlib.ExitStack()
        x2Tb = sb("x2Tb", [128, FC, rows], F32, stC)
        ln2h = sb("ln2h", [128, FC, rows], BF16, stC)
        ln2l = sb("ln2l", [128, FC, rows], BF16, stC)
        rst = [sb(f"rst{i}", [128, rows], F32, stC) for i in range(2)]
        sq2 = [sb(f"sq2{i}", [128, QB], F32, stC) for i in range(2)]
        ev1 = sb("ev1", [1, QB], F32, stC)
        ev2 = sb("ev2", [1, QB], F32, stC)
        mrow = sb("mrow", [1, QB], F32, stC)
        nmrow = sb("nmrow", [1, QB], F32, stC)
        msrow = sb("msrow", [1, QB], F32, stC)
        varow = sb("varow", [1, QB], F32, stC)
        rsrow = sb("rsrow", [1, QB], F32, stC)
        nbrow = sb("nbrow", [1, QB], F32, stC)
        bcr = sb("bcr", [128, QB], F32, stC)
        bcb = sb("bcb", [128, QB], F32, stC)
        lf2 = sb("lf2", [128, QB], F32, stC)
        stCp = contextlib.ExitStack()
        PW = [ps(f"PW{i}", [128, QB], stCp) for i in range(8)]

        dC = S("dC"); vCx = S("vCx")
        dCx = S("dCx")
        dCr = [S("dCr0"), S("dCr1")]
        sCq = S("sCq"); tCm = S("tCm"); sCe = S("sCe"); vCst = S("vCst")
        sCsq = S("sCsq"); vCrs = S("vCrs"); tCb = S("tCb"); sCb = S("sCb"); vCln = S("vCln")

        for fc in range(FC):
            p.dma("scalar", x2Tb[:, fc, :], xT[128 * fc:128 * (fc + 1), :], dCx)
        for fc in range(FC):
            rb = fc % 2
            k, dsub = divmod(fc, 2)
            p.wait("gpsimd", ccA)
            p.wait("gpsimd", dG, dGcopy_mark)
            if fc >= 2:
                p.wait("gpsimd", vCx)
            p.dma("gpsimd", rst[rb][:], rs_cp[k][128 * dsub:128 * (dsub + 1), :], dCr[rb])
            if fc == 0:
                p.wait("vector", dCx)
            p.wait("vector", dCr[rb])
            p.op("vector", lambda e, fc=fc, rb=rb:
                 e.tensor_tensor(x2Tb[:, fc, :], x2Tb[:, fc, :], rst[rb][:], ALU.add), inc=vCx, drain=True)

        for tb in range(tbn):
            TB = slice(QB * tb, QB * (tb + 1))
            if taps:
                p.wait("gpsimd", vCx)
                for fc in range(FC):
                    p.dma("gpsimd", x2tap[128 * fc:128 * (fc + 1), TB], x2Tb[:, fc, TB], dG)
            for fc in range(FC):
                qb2 = fc % 2
                p.wait("scalar", vCx)
                if fc >= 2:
                    p.wait("scalar", tCm)
                p.op("scalar", lambda e, fc=fc, qb2=qb2, TB=TB: e.activation(sq2[qb2][:], x2Tb[:, fc, TB], AF.Square), inc=sCq)
                p.wait("tensor", sCq)
                p.op("tensor", lambda e, fc=fc, TB=TB:
                     e.matmul(PW[0][0:1, :], onesf[:], x2Tb[:, fc, TB], start=(fc == 0), stop=(fc == FC - 1)))
                p.op("tensor", lambda e, fc=fc, qb2=qb2:
                     e.matmul(PW[1][0:1, :], onesf[:], sq2[qb2][:], start=(fc == 0), stop=(fc == FC - 1)),
                     inc=tCm)
            p.wait("scalar", tCm)
            p.op("scalar", lambda e: e.copy(ev1[:], PW[0][0:1, :]))
            p.op("scalar", lambda e: e.copy(ev2[:], PW[1][0:1, :]), inc=sCe)
            p.wait("vector", sCe)
            p.op("vector", lambda e: e.tensor_single_scalar(mrow[:], ev1[:], 1.0 / DM, ALU.mult), drain=True)
            p.op("vector", lambda e: e.tensor_single_scalar(nmrow[:], mrow[:], -1.0, ALU.mult), drain=True)
            p.op("vector", lambda e: e.tensor_single_scalar(msrow[:], ev2[:], 1.0 / DM, ALU.mult), drain=True)
            p.op("vector", lambda e: e.tensor_tensor(varow[:], mrow[:], nmrow[:], ALU.mult), drain=True)
            p.op("vector", lambda e: e.tensor_tensor(varow[:], msrow[:], varow[:], ALU.add), drain=True)
            p.op("vector", lambda e: e.tensor_scalar(varow[:], varow[:], E1, E2, ALU.add, ALU.add), inc=vCst, drain=True)
            p.wait("scalar", vCst)
            p.op("scalar", lambda e: e.activation(varow[:], varow[:], AF.Sqrt), inc=sCsq)
            p.wait("vector", sCsq)
            p.op("vector", lambda e: e.reciprocal(rsrow[:], varow[:]), drain=True)
            p.op("vector", lambda e: e.tensor_tensor(nbrow[:], nmrow[:], rsrow[:], ALU.mult), inc=vCrs, drain=True)
            p.wait("tensor", vCrs)
            p.op("tensor", lambda e: e.matmul(PW[2][:], ones1f[:], rsrow[:], start=True, stop=True))
            p.op("tensor", lambda e: e.matmul(PW[3][:], ones1f[:], nbrow[:], start=True, stop=True), inc=tCb)
            p.wait("scalar", tCb)
            p.op("scalar", lambda e: e.copy(bcr[:], PW[2][:]))
            p.op("scalar", lambda e: e.copy(bcb[:], PW[3][:]), inc=sCb)
            p.wait("vector", sCb)
            for fc in range(FC):
                p.op("vector", lambda e, fc=fc, TB=TB: e.tensor_tensor(lf2[:], x2Tb[:, fc, TB], bcr[:], ALU.mult), drain=True)
                p.op("vector", lambda e, fc=fc: e.tensor_tensor(lf2[:], lf2[:], bcb[:], ALU.add), drain=True)
                p.op("vector", lambda e, fc=fc, TB=TB: e.tensor_copy(ln2h[:, fc, TB], lf2[:]), drain=True)
                p.op("vector", lambda e, fc=fc, TB=TB: e.tensor_tensor(ln2l[:, fc, TB], lf2[:], ln2h[:, fc, TB], ALU.subtract),
                     inc=vCln, drain=True)

        p.barrier("C")
        stCp.close()

        stD = contextlib.ExitStack()
        w1t = [sb(f"w1t{i}", [128, FC, 128], BF16, stD) for i in range(2)]
        hfb = [sb(f"hfb{i}", [128, rows], F32, stD) for i in range(2)]
        hhb = [sb(f"hhb{i}", [128, rows], BF16, stD) for i in range(2)]
        hlb = [sb(f"hlb{i}", [128, rows], BF16, stD) for i in range(2)]
        stD1p = contextlib.ExitStack()
        PH = [ps(f"PH{i}", [128, rows], stD1p) for i in range(2)]

        dD = S("dD"); tDh = S("tDh"); sDg = S("sDg"); vDs = S("vDs")
        dD1p = [S("dD10"), S("dD11")]
        tDh_h = []; sDg_h = []; vDs_h = []
        p.wait("scalar", gQz, gQzFinal)
        if taps:
            p.wait("gpsimd", gQz, gQzFinal)
            p.dma("gpsimd", q1otap[:, :], q1_out[:, :], dG)
        for fo in range(nfo):
            fb = fo % 2
            if fo >= 2:
                p.wait("scalar", tDh, tDh_h[fo - 2])
            p.dma("scalar", w1t[fb][:, :, :],
                  q1_out[:, 128 * fo:128 * (fo + 1)].rearrange("(a c) d -> c a d", c=128), dD1p[fb])
            p.wait("tensor", dD1p[fb])
            if fo == 0:
                p.wait("tensor", vCln, vCln.n)
            if fo >= 2:
                p.wait("tensor", sDg, sDg_h[fo - 2])
            for fc in range(FC):
                for tb in range(tbn):
                    TB = slice(QB * tb, QB * (tb + 1))
                    p.op("tensor", lambda e, fb=fb, fc=fc, TB=TB:
                         e.matmul(PH[fb][:, TB], w1t[fb][:, fc], ln2h[:, fc, TB], start=(fc == 0), stop=False))
                    p.op("tensor", lambda e, fb=fb, fc=fc, TB=TB:
                         e.matmul(PH[fb][:, TB], w1t[fb][:, fc], ln2l[:, fc, TB], start=False, stop=(fc == FC - 1)),
                         inc=(tDh if (fc == FC - 1 and tb == tbn - 1) else None))
            tDh_h.append(tDh.n)
            p.wait("scalar", tDh)
            if fo >= 2:
                p.wait("scalar", vDs, vDs_h[fo - 2])
            p.op("scalar", lambda e, fb=fb, fo=fo:
                 e.activation(hfb[fb][:], PH[fb][:], AF.Gelu, bias=b1c[:, fo:fo + 1]), inc=sDg)
            sDg_h.append(sDg.n)
            p.wait("vector", sDg)
            p.wait("vector", dG, dG.n)
            p.op("vector", lambda e, fb=fb: e.tensor_copy(hhb[fb][:], hfb[fb][:]), drain=True)
            p.op("vector", lambda e, fb=fb: e.tensor_tensor(hlb[fb][:], hfb[fb][:], hhb[fb][:], ALU.subtract),
                 inc=vDs, drain=True)
            vDs_h.append(vDs.n)
            p.wait("gpsimd", vDs)
            p.dma("gpsimd", hsp[0, fo, :, :], hhb[fb][:], dG)
            p.dma("gpsimd", hsp[1, fo, :, :], hlb[fb][:], dG)

        p.wait("gpsimd", dG, dG.n)
        p.barrier("D1")
        stD1p.close()
        w2t = [sb(f"w2t{i}", [128, 1024], BF16, stD) for i in range(2)]
        hht = [sb(f"hht{i}", [128, QB], BF16, stD) for i in range(2)]
        hlt = [sb(f"hlt{i}", [128, QB], BF16, stD) for i in range(2)]
        tfb = sb("tfb", [128, QB], F32, stD)
        ytt = sb("ytt", [128, QB], F32, stD)
        yot = [sb(f"yot{i}", [128, 128], F32, stD) for i in range(2)]
        dout_hist = []
        stD2p = contextlib.ExitStack()
        PF = [ps(f"PF{i}", [128, QB], stD2p) for i in range(8)]

        tDf = S("tDf"); sDe = S("sDe"); vDy = S("vDy"); tDt = S("tDt"); sDt = S("sDt"); dOut = S("dOut")
        dDhp = [S("dDh0"), S("dDh1")]
        dDwp = [S("dDw0"), S("dDw1")]
        dOutp = [S("dOut0"), S("dOut1")]
        for tb2 in range(tbn):
            TB = slice(QB * tb2, QB * (tb2 + 1))
            for dh in range(2):
                tDf_h = []
                for fo in range(nfo):
                    hb = fo % 2
                    if fo >= 2:
                        p.wait("gpsimd", tDf, tDf_h[fo - 2])
                    p.dma("gpsimd", hht[hb][:], hsp[0, fo, :, TB], dDhp[hb])
                    p.dma("gpsimd", hlt[hb][:], hsp[1, fo, :, TB], dDhp[hb])
                    if fo >= 2:
                        p.wait("scalar", tDf, tDf_h[fo - 2])
                    p.dma("scalar", w2t[hb][:], q2_out[128 * fo:128 * (fo + 1), 1024 * dh:1024 * (dh + 1)], dDwp[hb])
                    p.wait("tensor", dDhp[hb])
                    p.wait("tensor", dDwp[hb])
                    if fo == 0:
                        p.wait("tensor", sDt, sDt.n)
                    for d in range(8):
                        p.op("tensor", lambda e, d=d, hb=hb, fo=fo:
                             e.matmul(PF[d][:], w2t[hb][:, 128 * d:128 * (d + 1)], hht[hb][:], start=(fo == 0), stop=False))
                        p.op("tensor", lambda e, d=d, hb=hb, fo=fo:
                             e.matmul(PF[d][:], w2t[hb][:, 128 * d:128 * (d + 1)], hlt[hb][:], start=False, stop=(fo == nfo - 1)),
                             inc=(tDf if d == 7 else None))
                    tDf_h.append(tDf.n)
                for d in range(8):
                    dd = 8 * dh + d
                    p.wait("vector", tDf)
                    p.op("vector", lambda e, d=d, dd=dd, TB=TB:
                         e.scalar_tensor_tensor(ytt[:], PF[d][:], b2c[:, dd:dd + 1], x2Tb[:, dd, TB], ALU.add, ALU.add),
                         inc=vDy, drain=True)
                    for s4 in range(QB // 128):
                        yb = len(dout_hist) % 2
                        p.wait("tensor", vDy)
                        if dout_hist:
                            p.wait("tensor", sDt)
                        p.op("tensor", lambda e, d=d, s4=s4:
                             e.transpose(PF[d][:, 0:128], ytt[:, 128 * s4:128 * (s4 + 1)], ident[:]), inc=tDt)
                        p.wait("scalar", tDt)
                        p.wait("scalar", dOutp[yb])
                        p.op("scalar", lambda e, d=d, yb=yb: e.copy(yot[yb][:], PF[d][:, 0:128]), inc=sDt)
                        p.wait("gpsimd", sDt)
                        p.dma("gpsimd", y[QB * tb2 + 128 * s4:QB * tb2 + 128 * (s4 + 1), 128 * dd:128 * (dd + 1)],
                              yot[yb][:], dOutp[yb])
                        dout_hist.append(0)
        p.wait("gpsimd", dOutp[0])
        p.wait("gpsimd", dOutp[1])
        stD2p.close()
        stD.close()
        stC.close()

        p.emit(block)
    return nc


def _rope_tables(seq):
    d = np.arange(128)
    i = d % 64
    om = np.exp(-np.log(10000.0) * i / 64.0)
    omp = om * np.where(d < 64, -1.0, 1.0)
    omh_ = omp.astype(np.float32)
    oml_ = (omp - omh_.astype(np.float64)).astype(np.float32)
    t = np.arange(seq, dtype=np.float64)
    phase = omp[:, None] * t[None, :]
    k2s_ = np.round(phase / np.pi)
    sgs_ = np.where(np.abs(k2s_) % 2 == 1, -1.0, 1.0).astype(np.float32)
    k2c_ = np.round((phase + np.pi / 2) / np.pi)
    sgc_ = np.where(np.abs(k2c_) % 2 == 1, -1.0, 1.0).astype(np.float32)
    return (k2s_.astype(np.float32), sgs_, k2c_.astype(np.float32), sgc_,
            omh_.reshape(128, 1), oml_.reshape(128, 1))


def kernel(L0_attn_hidden_states, L0_attn_wq, L0_attn_wk, L0_attn_wv, L0_attn_wg,
           L0_attn_wo, L0_attn_norm_w, L0_ffn_w1, L0_ffn_b1, L0_ffn_w2, L0_ffn_b2):
    seq = L0_attn_hidden_states.shape[1]
    rows = seq // NCORES
    xfull = np.ascontiguousarray(np.asarray(L0_attn_hidden_states, dtype=np.float32).reshape(seq, DM))
    xTf = np.ascontiguousarray(xfull.T)
    wq = np.ascontiguousarray(np.asarray(L0_attn_wq, np.float32).T)
    wk = np.ascontiguousarray(np.asarray(L0_attn_wk, np.float32).T)
    wv = np.ascontiguousarray(np.asarray(L0_attn_wv, np.float32).T)
    wg = np.ascontiguousarray(np.asarray(L0_attn_wg, np.float32).T)
    woT = np.ascontiguousarray(np.asarray(L0_attn_wo, np.float32).T)
    w1T = np.ascontiguousarray(np.asarray(L0_ffn_w1, np.float32).T)
    w2T = np.ascontiguousarray(np.asarray(L0_ffn_w2, np.float32).T)
    b1v = np.ascontiguousarray(np.asarray(L0_ffn_b1, np.float32).reshape(DFF // 128, 128).T)
    b2v = np.ascontiguousarray(np.asarray(L0_ffn_b2, np.float32).reshape(FC, 128).T)
    nwv = np.ascontiguousarray(np.asarray(L0_attn_norm_w, np.float32).reshape(HD, 1))
    k2s_, sgs_, k2c_, sgc_, omh_, oml_ = _rope_tables(seq)

    nc = build(seq=seq)
    in_maps = []
    for c in range(NCORES):
        h0 = HPC * c
        wp = np.empty((HPC, 4, DM, HD), np.float32)
        for hh in range(HPC):
            sl = slice(HD * (h0 + hh), HD * (h0 + hh + 1))
            wp[hh, 0] = wq[:, sl]
            wp[hh, 1] = wk[:, sl]
            wp[hh, 2] = wv[:, sl]
            wp[hh, 3] = wg[:, sl]
        in_maps.append({
            "x": xfull,
            "xT": np.ascontiguousarray(xTf[:, rows * c:rows * (c + 1)]),
            "wproj": np.ascontiguousarray(wp),
            "wos_d": np.ascontiguousarray(woT[2 * HD * c:2 * HD * (c + 1), :]),
            "w1s": np.ascontiguousarray(w1T[(DM // NCORES) * c:(DM // NCORES) * (c + 1), :]),
            "w2s": np.ascontiguousarray(w2T[(DFF // NCORES) * c:(DFF // NCORES) * (c + 1), :]),
            "b1": b1v, "b2": b2v, "nw": nwv,
            "k2s": k2s_, "sgs": sgs_, "k2c": k2c_, "sgc": sgc_,
            "omh": omh_, "oml": oml_,
        })
    res = run_bass_kernel_spmd(nc, in_maps, core_ids=list(range(NCORES)))
    out = np.empty((1, seq, DM), np.float32)
    for c in range(NCORES):
        out[0, rows * c:rows * (c + 1), :] = res.results[c]["y"]
    return out


# version 04
# speedup vs baseline: 1.5853x; 1.4824x over previous
import contextlib
import numpy as np
import concourse.bass as bass
import concourse.mybir as mybir
from concourse.bass_utils import run_bass_kernel_spmd

F32 = mybir.dt.float32
BF16 = mybir.dt.bfloat16
AF = mybir.ActivationFunctionType
ALU = mybir.AluOpType

DM = 2048
HD = 128
NH = 16
HPC = 2
NCORES = 8
DFF = 8192
FC = DM // 128
TS = 256
QB = 512
EPS = 1e-5


def _split2(v):
    a = np.float64(np.float32(v))
    b = float(np.float32(np.float64(v) - a))
    return float(a), b


def _split3(v):
    a = np.float64(np.float32(v))
    r = np.float64(v) - a
    b = np.float64(np.float32(r))
    c = float(np.float32(r - b))
    return float(a), float(b), c


PI1, PI2, PI3 = _split3(np.pi)
H1, H2 = _split2(np.pi / 2)
E1, E2 = _split2(EPS)
S1, S2 = _split2(1.0 / np.sqrt(128.0))
C71, C72 = _split2(0.7)

ENGS = ("sync", "scalar", "vector", "tensor", "gpsimd")


class Sem:
    __slots__ = ("h", "n")

    def __init__(self, h):
        self.h = h
        self.n = 0


class P:

    def __init__(self, nc, es):
        self.nc = nc
        self.es = es
        self.q = {e: [] for e in ENGS}
        self._sems = {}

    def sem(self, name):
        if name.startswith("bar_"):
            name = "bar"
        elif not name.startswith("d") and name != "ccA":
            name = name[0] + "P"
        if name not in self._sems:
            self._sems[name] = Sem(self.es.enter_context(self.nc.semaphore(name)))
        return self._sems[name]

    def wait(self, eng, sem, v=None):
        val = sem.n if v is None else v
        self.q[eng].append(lambda e, s=sem.h, w=val: e.wait_ge(s, w))

    def op(self, eng, fn, inc=None, incn=1, drain=False):
        if inc is not None:
            inc.n += incn

        def run(e, fn=fn, inc=inc, incn=incn, drain=drain):
            instr = fn(e)
            if inc is not None:
                instr.then_inc(inc.h, incn)
            if drain:
                e.drain()

        self.q[eng].append(run)

    def dma(self, eng, out, in_, inc, transpose=False):
        inc.n += 16

        def run(e, out=out, in_=in_, inc=inc, transpose=transpose):
            if transpose:
                e.dma_start(out=out, in_=in_, transpose=True).then_inc(inc.h, 16)
            else:
                e.dma_start(out=out, in_=in_).then_inc(inc.h, 16)

        self.q[eng].append(run)

    def barrier(self, name):
        b = self.sem("bar_" + name)
        target = b.n + len(ENGS)
        for e in ENGS:
            self.op(e, lambda eng: eng.nop(), inc=b, incn=1)
        for e in ENGS:
            self.wait(e, b, target)

    def emit(self, block):
        for name in ENGS:
            fns = self.q[name]

            def body(eng, fns=fns):
                for f in fns:
                    f(eng)

            getattr(block, name)(body)


def build(seq=8192, taps=False):
    nts = seq // TS
    nqb = seq // QB
    rows = seq // NCORES
    tbn = rows // QB
    nkc = seq // 128
    nA = seq // 128
    w1r = DM // NCORES
    w2r = DFF // NCORES
    nfo = DFF // 128

    nc = bass.Bass("TRN2", target_bir_lowering=False)
    dram = lambda n, s, d, k: nc.dram_tensor(n, s, d, kind=k).ap()
    drams = lambda n, s, d: nc.dram_tensor(n, s, d, kind="Internal", addr_space="Shared").ap()
    IK = "ExternalOutput" if taps else "Internal"
    x = dram("x", [seq, DM], F32, "ExternalInput")
    xT = dram("xT", [DM, rows], F32, "ExternalInput")
    wproj = dram("wproj", [HPC, 6, DM, HD], F32, "ExternalInput")
    wos_d = dram("wos_d", [2 * HD, DM], F32, "ExternalInput")
    w1s = dram("w1s", [w1r, DFF], F32, "ExternalInput")
    w2s = dram("w2s", [w2r, DM], F32, "ExternalInput")
    b1 = dram("b1", [128, DFF // 128], F32, "ExternalInput")
    b2 = dram("b2", [128, FC], F32, "ExternalInput")
    nw = dram("nw", [HD, 1], F32, "ExternalInput")
    k2s = dram("k2s", [HD, seq], F32, "ExternalInput")
    sgs = dram("sgs", [HD, seq], F32, "ExternalInput")
    k2c = dram("k2c", [HD, seq], F32, "ExternalInput")
    sgc = dram("sgc", [HD, seq], F32, "ExternalInput")
    omh = dram("omh", [HD, 1], F32, "ExternalInput")
    oml = dram("oml", [HD, 1], F32, "ExternalInput")
    y = dram("y", [rows, DM], F32, "ExternalOutput")

    lnsp = dram("lnsp", [2, seq, DM], BF16, IK)
    qsp = dram("qsp", [HPC, 2, HD, seq], BF16, IK)
    csp = dram("csp", [HPC, HD, seq], F32, IK)
    ctab = dram("ctab", [HD, seq], F32, "Internal")
    stab = dram("stab", [HD, seq], F32, "Internal")
    hsp = dram("hsp", [2, nfo, 128, rows], BF16, IK)
    gsph = dram("gsph", [HPC * HD, seq], BF16, IK)
    gspl = dram("gspl", [HPC * HD, seq], BF16, IK)
    rs_in = [dram(f"rs_in{k}", [NCORES, 2 * HD, rows], F32, "Internal") for k in range(NCORES)]
    rs_out = [dram(f"rs_out{k}", [2 * HD, rows], F32, "Internal") for k in range(NCORES)]
    rs_cp = [dram(f"rs_cp{k}", [2 * HD, rows], F32, "Internal") for k in range(NCORES)]
    q1_in = dram("q1_in", [w1r, DFF], BF16, "Internal")
    q1tap = dram("q1tap", [w1r, DFF], BF16, IK) if taps else None
    q1otap = dram("q1otap", [DM, DFF], BF16, IK) if taps else None
    q1_out = drams("q1_out", [DM, DFF], BF16)
    q2_in = dram("q2_in", [w2r, DM], BF16, "Internal")
    q2_out = drams("q2_out", [DFF, DM], BF16)
    x2tap = dram("x2tap", [DM, rows], F32, IK) if taps else None
    dbgq = dram("dbgq", [1, 8], F32, IK) if taps else None
    ar8tap = dram("ar8tap", [8, 128], F32, IK) if taps else None
    rstap = dram("rstap", [NCORES, 2 * HD, rows], F32, IK) if taps else None
    rsotap = dram("rsotap", [2 * HD, rows], F32, IK) if taps else None
    rsused = dram("rsused", [FC, 128, rows], F32, IK) if taps else None
    avtap = dram("avtap", [HD, seq], F32, IK) if taps else None
    fence_in = dram("fence_in", [1, 128], F32, "Internal")
    fence_out = dram("fence_out", [8, 128], F32, "Internal")
    ar_in = dram("ar_in", [1, 128], F32, "Internal")
    ar_out = dram("ar_out", [8, 128], F32, "Internal")
    RG = [list(range(NCORES))]

    es = contextlib.ExitStack()
    with es:
        def sb(name, shape, dt=F32, st=None):
            stk = st if st is not None else es
            return stk.enter_context(nc.sbuf_tensor(name, shape, dt))

        def ps(name, shape, st=None):
            stk = st if st is not None else es
            return stk.enter_context(nc.psum_tensor(name, shape, F32))

        p = P(nc, es)
        S = p.sem

        onesb = sb("onesb", [128, 1], BF16)
        onesf = sb("onesf", [128, 1], F32)
        ones1f = sb("ones1f", [1, 128], F32)
        maskb = sb("maskb", [128, 896], BF16)
        nwc = sb("nwc", [HD, 1], F32)
        omhc = sb("omhc", [HD, 1], F32)
        omlc = sb("omlc", [HD, 1], F32)
        b2c = sb("b2c", [128, FC], F32)
        c20 = sb("c20", [128, 1], F32)
        b1c = sb("b1c", [128, DFF // 128], F32)

        block = es.enter_context(nc.Block())

        dW = S("dW")
        dX = S("dX")
        dLT = S("dLT")
        dSW = S("dSW")
        dG = S("dG")
        dSY = S("dSY")
        vZ = S("vZ")


        st00 = contextlib.ExitStack()
        mkf = sb("mkf", [128, 896], F32, st00)
        p.op("gpsimd", lambda e: e.iota(mkf[:], pattern=[[1, 896]], base=-384,
                                        channel_multiplier=-1, allow_small_or_imprecise_dtypes=True), inc=vZ)
        p.op("vector", lambda e: e.memset(onesb[:], 1.0))
        p.op("vector", lambda e: e.memset(onesf[:], 1.0))
        p.op("vector", lambda e: e.memset(ones1f[:], 1.0))
        p.op("vector", lambda e: e.memset(c20[:], -20.0))
        p.wait("vector", vZ)
        p.op("vector", lambda e: e.tensor_single_scalar(maskb[:], mkf[:], 0.0, ALU.is_ge), inc=vZ, drain=True)
        st00.close()

        p.dma("scalar", nwc[:], nw[:, :], dW)
        p.dma("scalar", omhc[:], omh[:, :], dW)
        p.dma("scalar", omlc[:], oml[:, :], dW)
        p.dma("scalar", b2c[:], b2[:, :], dW)
        p.dma("scalar", b1c[:], b1[:, :], dW)

        vWs = S("vWs")

        stA = contextlib.ExitStack()
        xa = [sb(f"xa{i}", [128, DM], F32, stA) for i in range(2)]
        lnhb = [sb(f"lnhb{i}", [128, DM], BF16, stA) for i in range(2)]
        lnlb = [sb(f"lnlb{i}", [128, DM], BF16, stA) for i in range(2)]
        lfb = sb("lfb", [128, DM], F32, stA)
        sqd = sb("sqd", [128, DM], F32, stA)
        s1c = sb("s1c", [128, 1], F32, stA); s2c = sb("s2c", [128, 1], F32, stA)
        mcl = sb("mcl", [128, 1], F32, stA); nmc = sb("nmc", [128, 1], F32, stA)
        msc = sb("msc", [128, 1], F32, stA); vvc = sb("vvc", [128, 1], F32, stA)
        sdc = sb("sdc", [128, 1], F32, stA); rsc = sb("rsc", [128, 1], F32, stA)
        nbc = sb("nbc", [128, 1], F32, stA)
        sqA = S("sqA"); vstA = S("vstA"); sdA = S("sdA"); vrsA = S("vrsA")
        shiA = S("shiA"); vfinA = S("vfinA"); vaffA = S("vaffA")
        dXp = [S("dX0"), S("dX1")]
        dGp = [S("dGa0"), S("dGa1")]

        p.wait("sync", vZ)
        for i in range(nA):
            b = i % 2
            if i >= 2:
                p.wait("sync", vfinA)
                p.wait("sync", dGp[b])
            p.dma("sync", xa[b][:], x[128 * i:128 * (i + 1), :], dXp[b])
            p.wait("scalar", dXp[b])
            p.op("scalar", lambda e, b=b: e.activation(sqd[:], xa[b][:], AF.Square, accum_out=s2c[:]), inc=sqA)
            p.wait("vector", dXp[b])
            p.op("vector", lambda e, b=b: e.tensor_reduce(s1c[:], xa[b][:], axis=mybir.AxisListType.X, op=ALU.add), drain=True)
            p.wait("vector", sqA)
            p.wait("vector", shiA)
            p.op("vector", lambda e: e.tensor_single_scalar(mcl[:], s1c[:], 1.0 / DM, ALU.mult), drain=True)
            p.op("vector", lambda e: e.tensor_single_scalar(nmc[:], mcl[:], -1.0, ALU.mult), drain=True)
            p.op("vector", lambda e: e.tensor_single_scalar(msc[:], s2c[:], 1.0 / DM, ALU.mult), drain=True)
            p.op("vector", lambda e: e.scalar_tensor_tensor(vvc[:], mcl[:], nmc[:, 0:1], msc[:], ALU.mult, ALU.add), drain=True)
            p.op("vector", lambda e: e.tensor_scalar(vvc[:], vvc[:], E1, E2, ALU.add, ALU.add), inc=vstA, drain=True)
            p.wait("scalar", vstA)
            p.op("scalar", lambda e: e.activation(sdc[:], vvc[:], AF.Sqrt), inc=sdA)
            p.wait("vector", sdA)
            p.op("vector", lambda e: e.reciprocal(rsc[:], sdc[:]), drain=True)
            p.op("vector", lambda e: e.tensor_tensor(nbc[:], nmc[:], rsc[:], ALU.mult), inc=vrsA, drain=True)
            p.wait("vector", vrsA, vrsA.n)
            p.op("vector", lambda e, b=b: e.tensor_scalar(lfb[:], xa[b][:], rsc[:, 0:1], nbc[:, 0:1],
                                                          ALU.mult, ALU.add), inc=vaffA, drain=True)
            p.wait("scalar", vaffA)
            p.op("scalar", lambda e, b=b: e.copy(lnhb[b][:], lfb[:]), inc=shiA)
            p.wait("vector", shiA)
            p.op("vector", lambda e, b=b: e.tensor_tensor(lnlb[b][:], lfb[:], lnhb[b][:], ALU.subtract),
                 inc=vfinA, drain=True)
            p.wait("gpsimd", vfinA)
            p.dma("gpsimd", lnsp[0, 128 * i:128 * (i + 1), :], lnhb[b][:], dGp[b])
            p.dma("gpsimd", lnsp[1, 128 * i:128 * (i + 1), :], lnlb[b][:], dGp[b])

        p.wait("gpsimd", dGp[0])
        p.wait("gpsimd", dGp[1])
        p.barrier("A")
        stA.close()

        stB = contextlib.ExitStack()
        wqkvg = [[sb(f"w{t}{i}", [128, FC, 128], BF16, stB) for i in range(2)]
                 for t in ("q", "k", "v", "g", "qp", "kp")]
        Kr = [sb(f"Kr{i}", [128, nkc, 128], BF16, stB) for i in range(2)]
        Vr = [sb(f"Vr{i}", [128, nkc, 128], BF16, stB) for i in range(2)]
        qhiA = [sb(f"qhiA{i}", [128, QB], BF16, stB) for i in range(2)]
        qloA = [sb(f"qloA{i}", [128, QB], BF16, stB) for i in range(2)]
        cmbA = [sb(f"cmbA{i}", [128, QB], F32, stB) for i in range(2)]
        pexp = [sb(f"pexp{i}", [128, QB], F32, stB) for i in range(2)]
        phi = [sb(f"phi{i}", [128, QB], BF16, stB) for i in range(2)]
        plo = [sb(f"plo{i}", [128, QB], BF16, stB) for i in range(2)]
        zrow = sb("zrow", [1, QB], F32, stB)
        rz = sb("rz", [1, QB], F32, stB)
        bc1 = sb("bc1", [128, QB], F32, stB)
        avn = sb("avn", [128, QB], F32, stB)
        sqb = sb("sqb", [128, QB], F32, stB)
        msqr = sb("msqr", [1, QB], F32, stB)
        rr_ = sb("rr_", [1, QB], F32, stB)
        bc2 = bc1
        gat = sqb
        gth = sb("gth", [128, QB], BF16, stB)
        gtl = sb("gtl", [128, QB], BF16, stB)
        stSw = contextlib.ExitStack()
        lnT = sb("lnT", [128, 2, 2, FC, TS], BF16, stSw)
        wraw = sb("wraw", [128, FC, 128], F32, stSw)
        cosQ = sb("cosQ", [128, TS], F32, stSw)
        sinQ = sb("sinQ", [128, TS], F32, stSw)
        tio = sb("tio", [128, TS], F32, stSw)
        ph_ = sb("ph_", [128, TS], F32, stSw)
        rA = sb("rA", [128, TS], F32, stSw)
        rB = sb("rB", [128, TS], F32, stSw)
        snb = sb("snb", [128, TS], F32, stSw)
        costile = sb("costile", [128, TS], F32, stSw)
        sintile = sb("sintile", [128, TS], F32, stSw)
        k2sT = sb("k2sT", [128, TS], F32, stSw); sgsT = sb("sgsT", [128, TS], F32, stSw)
        k2cT = sb("k2cT", [128, TS], F32, stSw); sgcT = sb("sgcT", [128, TS], F32, stSw)
        qf = sb("qf", [128, TS], F32, stSw)
        qsw = sb("qsw", [128, TS], F32, stSw)
        qr1 = sb("qr1", [128, TS], F32, stSw)
        qr2 = snb
        qhiT = sb("qhiT", [128, TS], BF16, stSw)
        qloT = sb("qloT", [128, TS], BF16, stSw)
        kf = sb("kf", [128, TS], F32, stSw)
        ksw = sb("ksw", [128, TS], F32, stSw)
        gsb = qr1
        cmb = snb
        stBp = contextlib.ExitStack()
        P6 = [ps(f"P6_{i}", [128, QB], stBp) for i in range(6)]
        P7 = ps("P7", [128, QB], stBp)
        P8 = ps("P8", [128, QB], stBp)
        PQs = [P6[0][:, 0:TS], P6[4][:, 0:TS]]
        PKs = [P6[1][:, 0:TS], P6[5][:, 0:TS]]
        PGs = [P6[2][:, 0:TS], P7[:, 0:TS]]
        PVs = [P6[3][:, 0:TS], P8[:, 0:TS]]

        dR = S("dR")
        dLTp = [S("dLT0"), S("dLT1")]
        dSWq = S("dSWq"); dSWk = S("dSWk")
        dGq = S("dGq")
        vTab = S("vTab"); sTab = S("sTab")
        tQd = S("tQd"); tKd = S("tKd"); tGd = S("tGd"); tVd = S("tVd")
        sQe = S("sQe"); sKe = S("sKe"); sGe = S("sGe")
        vQe = S("vQe"); vKe = S("vKe"); vGe = S("vGe"); vVe = S("vVe")


        tSc = S("tSc"); sEx = S("sEx"); vPs = S("vPs"); tPe = S("tPe")
        dQph = [[S("dQA0"), S("dQA1")], [S("dQB0"), S("dQB1")]]
        sZe = S("sZe"); vRz = S("vRz"); tB1 = S("tB1"); sB1 = S("sB1")
        vAvn = S("vAvn"); sSq = S("sSq"); tM = S("tM"); sMs = S("sMs")
        vM1 = S("vM1"); sS2 = S("sS2"); vRr = S("vRr"); tB2 = S("tB2"); sB2 = S("sB2")
        vGa = S("vGa"); dQA = S("dQA")

        def load_weights(h):
            for pj in range(6):
                p.wait("scalar", vWs, vWs.n)
                p.dma("scalar", wraw[:, :, :],
                      wproj[h, pj, :, :].rearrange("(a c) d -> c a d", c=128), dW)
                p.wait("vector", dW)
                dhi, dlo = wqkvg[pj]
                p.op("vector", lambda e, d=dhi: e.tensor_copy(d[:], wraw[:]), drain=True)
                p.op("vector", lambda e, dh_=dhi, dl=dlo:
                     e.tensor_tensor(dl[:], wraw[:], dh_[:], ALU.subtract), inc=vWs, drain=True)

        def sweep(h):
            load_weights(h)
            tQd_h = []; tVd_h = []; tKd_h2 = []; sQe_h = []; sKe_h = []; sGe_h = []; vVe_h = []
            for tt in range(nts):
                b = tt % 2
                T = slice(TS * tt, TS * (tt + 1))
                PQ = PQs[b]; PK = PKs[b]; PG = PGs[b]; PV = PVs[b]
                PVfull = (P6[3] if b == 0 else P8)
                if tt >= 2:
                    p.wait("sync", tKd, tKd_h2[tt - 2])
                    p.wait("sync", tVd, tVd_h[tt - 2])
                if tt >= 2:
                    p.wait("scalar", tKd, tKd_h2[tt - 2])
                    p.wait("scalar", tVd, tVd_h[tt - 2])
                for half in range(2):
                    for fc in range(FC):
                        p.dma("sync", lnT[:, b, half, fc, :],
                              lnsp[half, T, 128 * fc:128 * (fc + 1)], dLTp[b], transpose=True)
                if h == 0:
                    p.wait("gpsimd", vTab)
                    p.wait("gpsimd", dGq)
                    p.dma("gpsimd", k2sT[:], k2s[:, T], dR)
                    p.dma("gpsimd", sgsT[:], sgs[:, T], dR)
                    p.dma("gpsimd", k2cT[:], k2c[:, T], dR)
                    p.dma("gpsimd", sgcT[:], sgc[:, T], dR)
                    gIo = S("gIo")
                    p.op("gpsimd", lambda e, tt=tt: e.iota(tio[:], pattern=[[1, TS]], base=TS * tt,
                                                           channel_multiplier=0,
                                                           allow_small_or_imprecise_dtypes=True), inc=gIo)
                    p.wait("vector", dR)
                    p.wait("vector", gIo)
                    V_ = "vector"
                    p.op(V_, lambda e: e.tensor_single_scalar(ph_[:], tio[:], omhc[:, 0:1], ALU.mult), drain=True)
                    p.op(V_, lambda e: e.scalar_tensor_tensor(ph_[:], tio[:], omlc[:, 0:1], ph_[:], ALU.mult, ALU.add), drain=True)
                    p.op(V_, lambda e: e.scalar_tensor_tensor(rA[:], k2sT[:], -PI1, ph_[:], ALU.mult, ALU.add), drain=True)
                    p.op(V_, lambda e: e.scalar_tensor_tensor(rB[:], k2sT[:], -PI2, rA[:], ALU.mult, ALU.add), drain=True)
                    p.op(V_, lambda e: e.scalar_tensor_tensor(rA[:], k2sT[:], -PI3, rB[:], ALU.mult, ALU.add), inc=vTab, drain=True)
                    p.wait("scalar", vTab)
                    p.wait("scalar", dGq)
                    p.op("scalar", lambda e: e.activation(snb[:], rA[:], AF.Sin), inc=sTab)
                    p.wait(V_, sTab)
                    p.op(V_, lambda e: e.tensor_tensor(sintile[:], snb[:], sgsT[:], ALU.mult), inc=vTab, drain=True)
                    p.op(V_, lambda e: e.tensor_scalar(rB[:], ph_[:], H1, H2, ALU.add, ALU.add), drain=True)
                    p.op(V_, lambda e: e.scalar_tensor_tensor(rA[:], k2cT[:], -PI1, rB[:], ALU.mult, ALU.add), drain=True)
                    p.op(V_, lambda e: e.scalar_tensor_tensor(rB[:], k2cT[:], -PI2, rA[:], ALU.mult, ALU.add), drain=True)
                    p.op(V_, lambda e: e.scalar_tensor_tensor(rA[:], k2cT[:], -PI3, rB[:], ALU.mult, ALU.add), inc=vTab, drain=True)
                    p.wait("scalar", vTab)
                    p.op("scalar", lambda e: e.activation(snb[:], rA[:], AF.Sin), inc=sTab)
                    p.wait(V_, sTab)
                    p.op(V_, lambda e: e.tensor_tensor(costile[:], snb[:], sgcT[:], ALU.mult), drain=True)
                    p.op(V_, lambda e: e.tensor_single_scalar(cosQ[:], costile[:], S1, ALU.mult), drain=True)
                    p.op(V_, lambda e: e.scalar_tensor_tensor(cosQ[:], costile[:], S2, cosQ[:], ALU.mult, ALU.add), drain=True)
                    p.op(V_, lambda e: e.tensor_single_scalar(sinQ[:], sintile[:], S1, ALU.mult), drain=True)
                    p.op(V_, lambda e: e.scalar_tensor_tensor(sinQ[:], sintile[:], S2, sinQ[:], ALU.mult, ALU.add), inc=vTab, drain=True)
                    p.wait("gpsimd", vTab)
                    p.dma("gpsimd", ctab[:, T], costile[:], dGq)
                    p.dma("gpsimd", stab[:, T], sintile[:], dGq)
                else:
                    p.wait("gpsimd", vQe)
                    p.wait("gpsimd", vKe)
                    p.wait("gpsimd", dGq)
                    p.dma("gpsimd", costile[:], ctab[:, T], dR)
                    p.dma("gpsimd", sintile[:], stab[:, T], dR)
                    p.wait("vector", dR)
                    p.op("vector", lambda e: e.tensor_single_scalar(cosQ[:], costile[:], S1, ALU.mult), drain=True)
                    p.op("vector", lambda e: e.scalar_tensor_tensor(cosQ[:], costile[:], S2, cosQ[:], ALU.mult, ALU.add), drain=True)
                    p.op("vector", lambda e: e.tensor_single_scalar(sinQ[:], sintile[:], S1, ALU.mult), drain=True)
                    p.op("vector", lambda e: e.scalar_tensor_tensor(sinQ[:], sintile[:], S2, sinQ[:], ALU.mult, ALU.add), inc=vTab, drain=True)

                p.wait("tensor", dLTp[b])
                if tt == 0:
                    p.wait("tensor", vWs)
                if tt >= 2:
                    p.wait("tensor", sQe, sQe_h[tt - 2]); p.wait("tensor", sKe, sKe_h[tt - 2])
                    p.wait("tensor", sGe, sGe_h[tt - 2]); p.wait("tensor", vVe, vVe_h[tt - 2])
                wqh, wql = wqkvg[0]; wkh, wkl = wqkvg[1]
                wvh, wvl = wqkvg[2]; wgh, wgl = wqkvg[3]
                wqph, wqpl = wqkvg[4]; wkph, wkpl = wqkvg[5]
                for (PP, whi_, wlo_, tsem) in ((PQ, wqh, wql, tQd), (PK, wkh, wkl, tKd), (PG, wgh, wgl, tGd)):
                    for fc in range(FC):
                        p.op("tensor", lambda e, PP=PP, w=whi_, fc=fc, b=b:
                             e.matmul(PP, w[:, fc], lnT[:, b, 0, fc], start=(fc == 0), stop=False))
                        p.op("tensor", lambda e, PP=PP, w=whi_, fc=fc, b=b:
                             e.matmul(PP, w[:, fc], lnT[:, b, 1, fc], start=False, stop=False))
                        p.op("tensor", lambda e, PP=PP, w=wlo_, fc=fc, b=b:
                             e.matmul(PP, w[:, fc], lnT[:, b, 0, fc], start=False, stop=(fc == FC - 1)),
                             inc=(tsem if fc == FC - 1 else None))
                for s4 in range(TS // 128):
                    for fc in range(FC):
                        sl = slice(128 * s4, 128 * (s4 + 1))
                        p.op("tensor", lambda e, PVf=PVfull, fc=fc, b=b, sl=sl:
                             e.matmul(PVf[:, sl], lnT[:, b, 0, fc, sl], wvh[:, fc], start=(fc == 0), stop=False))
                        p.op("tensor", lambda e, PVf=PVfull, fc=fc, b=b, sl=sl:
                             e.matmul(PVf[:, sl], lnT[:, b, 0, fc, sl], wvl[:, fc], start=False, stop=False))
                        p.op("tensor", lambda e, PVf=PVfull, fc=fc, b=b, sl=sl:
                             e.matmul(PVf[:, sl], lnT[:, b, 1, fc, sl], wvh[:, fc], start=False, stop=(fc == FC - 1)),
                             inc=(tVd if (fc == FC - 1 and s4 == TS // 128 - 1) else None))

                p.wait("scalar", tQd)
                p.wait("scalar", vQe, vQe.n)
                p.op("scalar", lambda e, PQ=PQ: e.copy(qf[:], PQ), inc=sQe)
                p.wait("scalar", tKd)
                p.wait("scalar", vKe, vKe.n)
                p.op("scalar", lambda e, PK=PK: e.copy(kf[:], PK), inc=sKe)
                p.wait("tensor", sQe)
                p.wait("tensor", sKe)
                for (PP, wph, wpl, tsem) in ((PQ, wqph, wqpl, tQd), (PK, wkph, wkpl, tKd)):
                    for fc in range(FC):
                        p.op("tensor", lambda e, PP=PP, w=wph, fc=fc, b=b:
                             e.matmul(PP, w[:, fc], lnT[:, b, 0, fc], start=(fc == 0), stop=False))
                        p.op("tensor", lambda e, PP=PP, w=wph, fc=fc, b=b:
                             e.matmul(PP, w[:, fc], lnT[:, b, 1, fc], start=False, stop=False))
                        p.op("tensor", lambda e, PP=PP, w=wpl, fc=fc, b=b:
                             e.matmul(PP, w[:, fc], lnT[:, b, 0, fc], start=False, stop=(fc == FC - 1)),
                             inc=(tsem if fc == FC - 1 else None))
                tQd_h.append(tQd.n)
                tKd_h2.append(tKd.n)
                p.wait("scalar", tQd)
                p.op("scalar", lambda e, PQ=PQ: e.copy(qsw[:], PQ), inc=sQe)
                p.wait("scalar", tKd)
                p.op("scalar", lambda e, PK=PK: e.copy(ksw[:], PK), inc=sKe)
                p.wait("vector", sQe)
                p.wait("vector", vTab)
                p.wait("vector", dGq)
                p.op("vector", lambda e: e.tensor_tensor(qr1[:], qf[:], cosQ[:], ALU.mult))
                p.op("vector", lambda e: e.tensor_tensor(qr2[:], qsw[:], sinQ[:], ALU.mult), drain=True)
                p.op("vector", lambda e: e.tensor_tensor(qr1[:], qr1[:], qr2[:], ALU.add), drain=True)
                p.op("vector", lambda e: e.tensor_copy(qhiT[:], qr1[:]), drain=True)
                p.op("vector", lambda e: e.tensor_tensor(qloT[:], qr1[:], qhiT[:], ALU.subtract), inc=vQe, drain=True)
                p.wait("gpsimd", vQe)
                p.dma("gpsimd", qsp[h, 0, :, T], qhiT[:], dGq)
                p.dma("gpsimd", qsp[h, 1, :, T], qloT[:], dGq)

                p.wait("vector", sKe)
                kcs = slice(2 * tt, 2 * (tt + 1))
                p.op("vector", lambda e: e.tensor_tensor(qr1[:], kf[:], costile[:], ALU.mult))
                p.op("vector", lambda e: e.tensor_tensor(qr2[:], ksw[:], sintile[:], ALU.mult), drain=True)
                p.op("vector", lambda e: e.tensor_tensor(qr1[:], qr1[:], qr2[:], ALU.add), drain=True)
                p.op("vector", lambda e, kcs=kcs: e.tensor_copy(Kr[0][:, kcs, :], qr1[:]), drain=True)
                p.op("vector", lambda e, kcs=kcs: e.tensor_tensor(Kr[1][:, kcs, :], qr1[:], Kr[0][:, kcs, :], ALU.subtract), inc=vKe, drain=True)

                p.wait("scalar", tGd)
                p.wait("scalar", vGe, vGe.n)
                p.wait("scalar", vKe)
                p.op("scalar", lambda e, PG=PG: e.activation(gsb[:], PG, AF.Silu), inc=sGe)
                p.wait("vector", sGe)
                p.op("vector", lambda e: e.tensor_single_scalar(cmb[:], gsb[:], nwc[:, 0:1], ALU.mult), inc=vGe, drain=True)
                p.wait("gpsimd", vGe)
                p.dma("gpsimd", csp[h, :, T], cmb[:], dGq)

                p.wait("vector", tVd)
                p.op("vector", lambda e, kcs=kcs, PV=PV: e.tensor_copy(Vr[0][:, kcs, :], PV), drain=True)
                p.op("vector", lambda e, kcs=kcs, PV=PV: e.tensor_tensor(Vr[1][:, kcs, :], PV, Vr[0][:, kcs, :], ALU.subtract), inc=vVe, drain=True)
                sQe_h.append(sQe.n); sKe_h.append(sKe.n); sGe_h.append(sGe.n)
                vVe_h.append(vVe.n); tVd_h.append(tVd.n)

        def attention(h):
            qld = "gpsimd" if h == 0 else "scalar"
            qst = "gpsimd" if h == 0 else "sync"
            dST = dG if h == 0 else dSY
            p.wait(qld, dGq)
            vPs_h = []
            tPe_h = []
            tSc_h = []
            for qb in range(nqb):
                ab = qb % 2
                dQ = dQph[h][ab]
                QT = slice(QB * qb, QB * (qb + 1))
                if qb >= 2:
                    p.wait(qld, tSc)
                    p.wait(qld, vGa, vGa.n)
                p.dma(qld, qhiA[ab][:], qsp[h, 0, :, QT], dQ)
                p.dma(qld, qloA[ab][:], qsp[h, 1, :, QT], dQ)
                p.dma(qld, cmbA[ab][:], csp[h, :, QT], dQ)
                nkq = 4 * (qb + 1)
                base = len(vPs_h)

                def emit_scores(kc, qb=qb, ab=ab, base=base, dQ=dQ):
                    pb = (base + kc) % 2
                    PSb = PS2[pb]
                    if kc == 0:
                        p.wait("tensor", dQ)
                    u = base + kc
                    if u >= 2:
                        p.wait("tensor", vPs, vPs_h[u - 2])
                    p.op("tensor", lambda e, PSb=PSb, kc=kc, ab=ab:
                         e.matmul(PSb[:], Kr[0][:, kc, :], qhiA[ab][:], start=True, stop=False))
                    p.op("tensor", lambda e, PSb=PSb, kc=kc, ab=ab:
                         e.matmul(PSb[:], Kr[0][:, kc, :], qloA[ab][:], start=False, stop=False))
                    p.op("tensor", lambda e, PSb=PSb, kc=kc, ab=ab:
                         e.matmul(PSb[:], Kr[1][:, kc, :], qhiA[ab][:], start=False, stop=True), inc=tSc)
                    tSc_h.append(tSc.n)

                emit_scores(0)
                if nkq > 1:
                    emit_scores(1)
                for kc in range(nkq):
                    u = base + kc
                    pb = u % 2
                    PSb = PS2[pb]
                    diag = kc >= 4 * qb
                    p.wait("scalar", tSc, tSc_h[u])
                    if u >= 2:
                        p.wait("scalar", vPs, vPs_h[u - 2])
                        p.wait("scalar", tPe, tPe_h[u - 2])
                    p.op("scalar", lambda e, pb=pb, PSb=PSb:
                         e.activation(pexp[pb][:], PSb[:], AF.Exp, bias=c20[:, 0:1]), inc=sEx)
                    if diag:
                        p.wait("vector", sEx)
                        p.op("vector", lambda e, pb=pb, off=384 - 128 * (kc - 4 * qb):
                             e.tensor_tensor(pexp[pb][:], pexp[pb][:], maskb[:, off:off + QB], ALU.mult), drain=True)
                        if u >= 2:
                            p.wait("vector", tPe, tPe_h[u - 2])
                        p.op("vector", lambda e, pb=pb: e.tensor_copy(phi[pb][:], pexp[pb][:]), drain=True)
                        p.op("vector", lambda e, pb=pb: e.tensor_tensor(plo[pb][:], pexp[pb][:], phi[pb][:], ALU.subtract),
                             inc=vPs, drain=True)
                    else:
                        p.wait("vector", sEx)
                        if u >= 2:
                            p.wait("vector", tPe, tPe_h[u - 2])
                        p.op("vector", lambda e, pb=pb: e.tensor_copy(phi[pb][:], pexp[pb][:]), drain=True)
                        p.op("vector", lambda e, pb=pb: e.tensor_tensor(plo[pb][:], pexp[pb][:], phi[pb][:], ALU.subtract),
                             inc=vPs, drain=True)
                    vPs_h.append(vPs.n)
                    p.wait("tensor", vPs)
                    last = kc == nkq - 1
                    p.op("tensor", lambda e, kc=kc, pb=pb, qb=qb:
                         e.matmul(PAV[:], Vr[0][:, kc, :], phi[pb][:], start=(kc == 0), stop=False))
                    p.op("tensor", lambda e, kc=kc, pb=pb:
                         e.matmul(PAV[:], Vr[0][:, kc, :], plo[pb][:], start=False, stop=False))
                    p.op("tensor", lambda e, kc=kc, pb=pb, last=last:
                         e.matmul(PAV[:], Vr[1][:, kc, :], phi[pb][:], start=False, stop=last))
                    p.op("tensor", lambda e, kc=kc, pb=pb:
                         e.matmul(PZ, onesb[:], phi[pb][:], start=(kc == 0), stop=False))
                    p.op("tensor", lambda e, kc=kc, pb=pb, last=last:
                         e.matmul(PZ, onesb[:], plo[pb][:], start=False, stop=last), inc=tPe)
                    tPe_h.append(tPe.n)
                    if kc + 2 < nkq:
                        emit_scores(kc + 2)
                p.wait("scalar", tPe)
                p.op("scalar", lambda e: e.copy(zrow[:], PZ), inc=sZe)
                p.wait("vector", sZe)
                p.op("vector", lambda e: e.reciprocal(rz[:], zrow[:]), inc=vRz, drain=True)
                p.wait("tensor", vRz)
                p.op("tensor", lambda e: e.matmul(PB1[:], ones1f[:], rz[:], start=True, stop=True), inc=tB1)
                p.wait("scalar", tB1)
                p.op("scalar", lambda e: e.copy(bc1[:], PB1[:]), inc=sB1)
                p.wait("vector", sB1)
                p.op("vector", lambda e: e.tensor_tensor(avn[:], PAV[:], bc1[:], ALU.mult), inc=vAvn, drain=True)
                if taps and h == 0:
                    p.wait("gpsimd" if h == 0 else "sync", vAvn)
                    p.dma("gpsimd" if h == 0 else "sync", avtap[:, QT], avn[:], dST)
                p.wait("scalar", vAvn)
                p.op("scalar", lambda e: e.activation(sqb[:], avn[:], AF.Square), inc=sSq)
                p.wait("tensor", sSq)
                p.wait("tensor", vAvn)
                p.op("tensor", lambda e: e.matmul(PM, onesf[:], sqb[:], start=True, stop=True), inc=tM)
                p.wait("scalar", tM)
                p.op("scalar", lambda e: e.copy(msqr[:], PM), inc=sMs)
                p.wait("vector", sMs)
                p.op("vector", lambda e: e.tensor_scalar(msqr[:], msqr[:], 1.0 / 128.0, E1, ALU.mult, ALU.add), drain=True)
                p.op("vector", lambda e: e.tensor_single_scalar(msqr[:], msqr[:], E2, ALU.add), inc=vM1, drain=True)
                p.wait("scalar", vM1)
                p.op("scalar", lambda e: e.activation(msqr[:], msqr[:], AF.Sqrt), inc=sS2)
                p.wait("vector", sS2)
                p.op("vector", lambda e: e.reciprocal(rr_[:], msqr[:]), inc=vRr, drain=True)
                p.wait("tensor", vRr)
                p.op("tensor", lambda e: e.matmul(PB2[:], ones1f[:], rr_[:], start=True, stop=True), inc=tB2)
                p.wait("scalar", tB2)
                p.op("scalar", lambda e: e.copy(bc2[:], PB2[:]), inc=sB2)
                p.wait("vector", sB2)
                p.wait("vector", dQ)
                p.op("vector", lambda e: e.tensor_tensor(avn[:], avn[:], bc2[:], ALU.mult), drain=True)
                p.wait("vector", dST, dST.n)
                p.op("vector", lambda e, ab=ab: e.tensor_tensor(gat[:], avn[:], cmbA[ab][:], ALU.mult), drain=True)
                p.op("vector", lambda e: e.tensor_copy(gth[:], gat[:]), drain=True)
                p.op("vector", lambda e: e.tensor_tensor(gtl[:], gat[:], gth[:], ALU.subtract), inc=vGa, drain=True)
                p.wait(qst, vGa)
                p.dma(qst, gsph[HD * h:HD * (h + 1), QT], gth[:], dST)
                p.dma(qst, gspl[HD * h:HD * (h + 1), QT], gtl[:], dST)

        PS2 = [P6[0], P6[1]]
        PAV = P6[2]
        PZ = P6[3][0:1, :]
        PB1 = P6[4]
        PM = P6[5][0:1, :]
        PB2 = P6[4]

        sweep(0)
        p.wait("gpsimd", dGq)
        p.wait("gpsimd", dG, dG.n)
        p.barrier("s0")
        attention(0)
        p.barrier("a0")
        sweep(1)
        p.wait("gpsimd", dGq)
        p.wait("gpsimd", dG, dG.n)
        p.barrier("s1")
        stSw.close()

        stQ = contextlib.ExitStack()
        wslc = sb("wslc", [128, 1024], F32, stQ)
        racc1 = sb("racc1", [128, 1], F32, stQ)
        racc2 = sb("racc2", [128, 1], F32, stQ)
        rtmp = sb("rtmp", [128, 1], F32, stQ)
        ppb = sb("ppb", [128, 1024], F32, stQ)
        nnb = sb("nnb", [128, 1024], F32, stQ)
        qtb = sb("qtb", [128, 1024], BF16, stQ)
        s12 = sb("s12", [1, 128], F32, stQ)
        ar8 = sb("ar8", [8, 128], F32, stQ)
        ones8 = sb("ones8", [8, 1], F32, stQ)
        fin_sb = sb("fin_sb", [1, 128], F32, stQ)
        arb = sb("arb", [1, 2], F32, stQ)
        thb = sb("thb", [1, 2], F32, stQ)
        nthb = sb("nthb", [1, 2], F32, stQ)
        thbc = sb("thbc", [128, 2], F32, stQ)
        nthbc = sb("nthbc", [128, 2], F32, stQ)
        PQm = P6[5][0:1, 0:2]
        dQz = S("dQz"); dQg = S("dQg"); vQz = S("vQz"); tQz = S("tQz"); sQz = S("sQz"); gQz = S("gQz")

        p.op("vector", lambda e: e.memset(ones8[:], 1.0))
        p.op("vector", lambda e: e.memset(s12[:], 0.0))
        p.op("vector", lambda e: e.memset(fin_sb[:], 0.0), drain=True)
        p.op("vector", lambda e: e.memset(racc1[:], 0.0))
        p.op("vector", lambda e: e.memset(racc2[:], 0.0), drain=True)
        npc1 = (w1r * DFF) // (128 * 1024)
        npc2 = (w2r * DM) // (128 * 1024)
        def wpiece(mat, rowsz, i):
            wch = int(mat.tensor.shape[1]) // 1024
            rj, cj = divmod(i, wch)
            return mat[128 * rj:128 * (rj + 1), 1024 * cj:1024 * (cj + 1)]
        for mat, npc, racc in ((w1s, npc1, racc1), (w2s, npc2, racc2)):
            for i in range(npc):
                p.wait("scalar", vQz)
                p.dma("scalar", wslc[:], wpiece(mat, None, i), dQz)
                p.wait("vector", dQz)
                p.op("vector", lambda e: e.tensor_reduce(rtmp[:], wslc[:], axis=mybir.AxisListType.X,
                                                         op=ALU.add, apply_absolute_value=True), drain=True)
                p.op("vector", lambda e, racc=racc: e.tensor_tensor(racc[:], racc[:], rtmp[:], ALU.add),
                     inc=vQz, drain=True)
        p.wait("tensor", vQz)
        p.op("tensor", lambda e: e.matmul(P6[5][0:1, 0:1], onesf[:], racc1[:], start=True, stop=True))
        p.op("tensor", lambda e: e.matmul(P6[5][0:1, 1:2], onesf[:], racc2[:], start=True, stop=True), inc=tQz)
        p.wait("scalar", tQz)
        p.op("scalar", lambda e: e.copy(s12[0:1, 0:2], PQm), inc=sQz)
        p.wait("gpsimd", sQz)
        p.dma("gpsimd", ar_in[:, :], s12[:], dQg)
        p.dma("gpsimd", fence_in[:, :], fin_sb[:], dQg)
        p.wait("gpsimd", dQg)
        p.op("gpsimd", lambda e: e.collective_compute("AllGather", ALU.bypass, replica_groups=RG,
                                                      ins=[ar_in[:, :]], outs=[ar_out[:, :]]), inc=gQz)
        p.op("gpsimd", lambda e: e.collective_compute("AllGather", ALU.bypass, replica_groups=RG,
                                                      ins=[fence_in[:, :]], outs=[fence_out[:, :]]), inc=gQz)

        attention(1)
        p.wait("gpsimd", dGq)
        p.wait("sync", dSY, dSY.n)
        p.barrier("a1")

        p.dma("gpsimd", ar8[:], ar_out[:, :], dQg)
        p.wait("tensor", dQg)
        p.op("tensor", lambda e: e.matmul(P6[5][0:1, 4:6], ones8[:], ar8[:, 0:2], start=True, stop=True), inc=tQz)
        p.wait("scalar", tQz)
        p.op("scalar", lambda e: e.copy(arb[:], P6[5][0:1, 4:6]), inc=sQz)
        p.wait("vector", sQz)
        p.op("vector", lambda e: e.tensor_single_scalar(arb[:], arb[:], 1.0 / (DFF * DM), ALU.mult), drain=True)
        p.op("vector", lambda e: e.tensor_single_scalar(thb[:], arb[:], C71, ALU.mult), drain=True)
        p.op("vector", lambda e: e.scalar_tensor_tensor(thb[:], arb[:], C72, thb[:], ALU.mult, ALU.add), drain=True)
        p.op("vector", lambda e: e.tensor_single_scalar(nthb[:], thb[:], -1.0, ALU.mult), inc=vQz, drain=True)
        if taps:
            p.wait("gpsimd", vQz)
            p.dma("gpsimd", dbgq[0:1, 0:2], s12[0:1, 0:2], dG)
            p.dma("gpsimd", dbgq[0:1, 2:4], arb[:], dG)
            p.dma("gpsimd", dbgq[0:1, 4:6], thb[:], dG)
            p.dma("gpsimd", ar8tap[:, :], ar8[:], dG)
        p.wait("tensor", vQz)
        p.op("tensor", lambda e: e.matmul(P6[4][:, 0:2], ones1f[:], thb[:], start=True, stop=True))
        p.op("tensor", lambda e: e.matmul(P6[4][:, 2:4], ones1f[:], nthb[:], start=True, stop=True), inc=tQz)
        p.wait("scalar", tQz)
        p.op("scalar", lambda e: e.copy(thbc[:], P6[4][:, 0:2]))
        p.op("scalar", lambda e: e.copy(nthbc[:], P6[4][:, 2:4]), inc=sQz)
        for mat, npc, qdst, col in ((w1s, npc1, q1_in, 0), (w2s, npc2, q2_in, 1)):
            for i in range(npc):
                p.wait("scalar", vQz)
                p.wait("scalar", dG, dG.n)
                p.dma("scalar", wslc[:], wpiece(mat, None, i), dQz)
                p.wait("vector", dQz)
                p.wait("vector", sQz)
                p.op("vector", lambda e, col=col: e.tensor_single_scalar(ppb[:], wslc[:], thbc[:, col:col + 1], ALU.is_ge))
                p.op("vector", lambda e, col=col: e.tensor_single_scalar(nnb[:], wslc[:], nthbc[:, col:col + 1], ALU.is_le), drain=True)
                p.op("vector", lambda e: e.tensor_tensor(qtb[:], ppb[:], nnb[:], ALU.subtract), inc=vQz, drain=True)
                p.wait("gpsimd", vQz)
                p.dma("gpsimd", wpiece(qdst, None, i), qtb[:], dG)
                if taps and qdst is q1_in:
                    p.dma("gpsimd", wpiece(q1tap, None, i), qtb[:], dG)
        p.wait("gpsimd", dG, dG.n)
        p.op("gpsimd", lambda e: e.collective_compute("AllGather", ALU.bypass, replica_groups=RG,
                                                      ins=[q1_in[:, :]], outs=[q1_out[:, :]]), inc=gQz)
        p.op("gpsimd", lambda e: e.collective_compute("AllGather", ALU.bypass, replica_groups=RG,
                                                      ins=[q2_in[:, :]], outs=[q2_out[:, :]]), inc=gQz)
        p.op("gpsimd", lambda e: e.collective_compute("AllGather", ALU.bypass, replica_groups=RG,
                                                      ins=[fence_in[:, :]], outs=[fence_out[:, :]]), inc=gQz)
        gQzFinal = gQz.n

        attention(1)
        p.wait("gpsimd", dG, dG.n)
        p.wait("sync", dSY, dSY.n)
        p.barrier("a1")
        stQ.close()
        stBp.close()
        stB.close()

        ccA = S("ccA")
        stB3 = contextlib.ExitStack()
        wosr = sb("wosr", [128, 2, DM], F32, stB3)
        woshh = sb("woshh", [128, 2, DM], BF16, stB3)
        wosll = sb("wosll", [128, 2, DM], BF16, stB3)
        gh3 = [sb(f"gh3{i}", [128, 2, QB], BF16, stB3) for i in range(2)]
        gl3 = [sb(f"gl3{i}", [128, 2, QB], BF16, stB3) for i in range(2)]
        pev = [sb(f"pev{i}", [128, QB], F32, stB3) for i in range(4)]
        PWps = [[P6[0], P6[1]], [P6[2], P6[3]]]
        dB3 = S("dB3"); vB3 = S("vB3"); tB3 = S("tB3"); sB3 = S("sB3")
        dB3p = [S("dB30"), S("dB31")]
        dGrs = [S("dGrs0"), S("dGrs1")]
        p.dma("scalar", wosr[:, 0, :], wos_d[0:128, :], dB3)
        p.dma("scalar", wosr[:, 1, :], wos_d[128:256, :], dB3)
        p.wait("vector", dB3)
        p.op("vector", lambda e: e.tensor_copy(woshh[:], wosr[:]), drain=True)
        p.op("vector", lambda e: e.tensor_tensor(wosll[:], wosr[:], woshh[:], ALU.subtract), inc=vB3, drain=True)
        tB3_h = []
        sB3_h = []
        for k in range(NCORES):
            for tt in range(nqb):
                QT = slice(QB * tt, QB * (tt + 1))
                pb3 = tt % 2
                PWp = PWps[pb3]
                it3 = k * nqb + tt
                if it3 >= 2:
                    p.wait("scalar", tB3, tB3_h[it3 - 2])
                for hh in range(2):
                    p.dma("scalar", gh3[pb3][:, hh, :], gsph[HD * hh:HD * (hh + 1), QT], dB3p[pb3])
                    p.dma("scalar", gl3[pb3][:, hh, :], gspl[HD * hh:HD * (hh + 1), QT], dB3p[pb3])
                if it3 == 0:
                    p.wait("tensor", vB3)
                p.wait("tensor", dB3p[pb3])
                if it3 == 0:
                    p.wait("tensor", dB3)
                if it3 >= 2:
                    p.wait("tensor", sB3, sB3_h[it3 - 2])
                for d in range(2):
                    dout0 = 256 * k + 128 * d
                    for hc in range(2):
                        st_ = hc == 0
                        sp_ = hc == 1
                        p.op("tensor", lambda e, d=d, hc=hc, dout0=dout0, pb3=pb3, st_=st_, PWp=PWp:
                             e.matmul(PWp[d][:], woshh[:, hc, dout0:dout0 + 128], gh3[pb3][:, hc, :], start=st_, stop=False))
                        p.op("tensor", lambda e, d=d, hc=hc, dout0=dout0, pb3=pb3, PWp=PWp:
                             e.matmul(PWp[d][:], woshh[:, hc, dout0:dout0 + 128], gl3[pb3][:, hc, :], start=False, stop=False))
                        p.op("tensor", lambda e, d=d, hc=hc, dout0=dout0, pb3=pb3, sp_=sp_, PWp=PWp:
                             e.matmul(PWp[d][:], wosll[:, hc, dout0:dout0 + 128], gh3[pb3][:, hc, :], start=False, stop=sp_),
                             inc=(tB3 if (d == 1 and hc == 1) else None))
                tB3_h.append(tB3.n)
                p.wait("scalar", tB3)
                p.wait("scalar", dGrs[pb3])
                p.op("scalar", lambda e, pb3=pb3, PWp=PWp: e.copy(pev[2 * pb3][:], PWp[0][:]))
                p.op("scalar", lambda e, pb3=pb3, PWp=PWp: e.copy(pev[2 * pb3 + 1][:], PWp[1][:]), inc=sB3)
                sB3_h.append(sB3.n)
                p.wait("sync", sB3)
                jj = (QB * tt) // rows
                col = (QB * tt) % rows
                p.dma("sync", rs_in[k][jj, 0:128, col:col + QB], pev[2 * pb3][:], dGrs[pb3])
                p.dma("sync", rs_in[k][jj, 128:256, col:col + QB], pev[2 * pb3 + 1][:], dGrs[pb3])
                if taps and k == 0:
                    p.dma("sync", rstap[jj, 0:128, col:col + QB], pev[2 * pb3][:], dGrs[pb3])
                    p.dma("sync", rstap[jj, 128:256, col:col + QB], pev[2 * pb3 + 1][:], dGrs[pb3])
            p.wait("gpsimd", dGrs[0])
            p.wait("gpsimd", dGrs[1])
            p.op("gpsimd", lambda e, k=k: e.collective_compute(
                "ReduceScatter", ALU.add, replica_groups=RG,
                ins=[rs_in[k][:, :, :]],
                outs=[rs_out[k][:, :]]), inc=ccA)
        p.op("gpsimd", lambda e: e.collective_compute("AllGather", ALU.bypass, replica_groups=RG,
                                                      ins=[fence_in[:, :]], outs=[fence_out[:, :]]), inc=ccA)
        p.wait("gpsimd", ccA)
        for k in range(NCORES):
            p.dma("gpsimd", rs_cp[k][:, :], rs_out[k][:, :], dG)
        if taps:
            p.dma("gpsimd", rsotap[:, :], rs_out[0][:, :], dG)
        stB3.close()
        dGcopy_mark = dG.n

        stC = contextlib.ExitStack()
        x2Tb = sb("x2Tb", [128, FC, rows], F32, stC)
        ln2h = sb("ln2h", [128, FC, rows], BF16, stC)
        ln2l = sb("ln2l", [128, FC, rows], BF16, stC)
        rst = [sb(f"rst{i}", [128, rows], F32, stC) for i in range(2)]
        sq2 = [sb(f"sq2{i}", [128, QB], F32, stC) for i in range(2)]
        ev1 = sb("ev1", [1, QB], F32, stC)
        ev2 = sb("ev2", [1, QB], F32, stC)
        mrow = sb("mrow", [1, QB], F32, stC)
        nmrow = sb("nmrow", [1, QB], F32, stC)
        msrow = sb("msrow", [1, QB], F32, stC)
        varow = sb("varow", [1, QB], F32, stC)
        rsrow = sb("rsrow", [1, QB], F32, stC)
        nbrow = sb("nbrow", [1, QB], F32, stC)
        bcr = sb("bcr", [128, QB], F32, stC)
        bcb = sb("bcb", [128, QB], F32, stC)
        lf2 = sb("lf2", [128, QB], F32, stC)
        stCp = contextlib.ExitStack()
        PW = [ps(f"PW{i}", [128, QB], stCp) for i in range(8)]

        dC = S("dC"); vCx = S("vCx")
        dCx = S("dCx")
        dCr = [S("dCr0"), S("dCr1")]
        sCq = S("sCq"); tCm = S("tCm"); sCe = S("sCe"); vCst = S("vCst")
        sCsq = S("sCsq"); vCrs = S("vCrs"); tCb = S("tCb"); sCb = S("sCb"); vCln = S("vCln")

        for fc in range(FC):
            p.dma("scalar", x2Tb[:, fc, :], xT[128 * fc:128 * (fc + 1), :], dCx)
        for fc in range(FC):
            rb = fc % 2
            k, dsub = divmod(fc, 2)
            p.wait("gpsimd", ccA)
            p.wait("gpsimd", dG, dGcopy_mark)
            if fc >= 2:
                p.wait("gpsimd", vCx)
            p.dma("gpsimd", rst[rb][:], rs_cp[k][128 * dsub:128 * (dsub + 1), :], dCr[rb])
            if fc == 0:
                p.wait("vector", dCx)
            p.wait("vector", dCr[rb])
            p.op("vector", lambda e, fc=fc, rb=rb:
                 e.tensor_tensor(x2Tb[:, fc, :], x2Tb[:, fc, :], rst[rb][:], ALU.add), inc=vCx, drain=True)

        for tb in range(tbn):
            TB = slice(QB * tb, QB * (tb + 1))
            if taps:
                p.wait("gpsimd", vCx)
                for fc in range(FC):
                    p.dma("gpsimd", x2tap[128 * fc:128 * (fc + 1), TB], x2Tb[:, fc, TB], dG)
            for fc in range(FC):
                qb2 = fc % 2
                p.wait("scalar", vCx)
                if fc >= 2:
                    p.wait("scalar", tCm)
                p.op("scalar", lambda e, fc=fc, qb2=qb2, TB=TB: e.activation(sq2[qb2][:], x2Tb[:, fc, TB], AF.Square), inc=sCq)
                p.wait("tensor", sCq)
                p.op("tensor", lambda e, fc=fc, TB=TB:
                     e.matmul(PW[0][0:1, :], onesf[:], x2Tb[:, fc, TB], start=(fc == 0), stop=(fc == FC - 1)))
                p.op("tensor", lambda e, fc=fc, qb2=qb2:
                     e.matmul(PW[1][0:1, :], onesf[:], sq2[qb2][:], start=(fc == 0), stop=(fc == FC - 1)),
                     inc=tCm)
            p.wait("scalar", tCm)
            p.op("scalar", lambda e: e.copy(ev1[:], PW[0][0:1, :]))
            p.op("scalar", lambda e: e.copy(ev2[:], PW[1][0:1, :]), inc=sCe)
            p.wait("vector", sCe)
            p.op("vector", lambda e: e.tensor_single_scalar(mrow[:], ev1[:], 1.0 / DM, ALU.mult), drain=True)
            p.op("vector", lambda e: e.tensor_single_scalar(nmrow[:], mrow[:], -1.0, ALU.mult), drain=True)
            p.op("vector", lambda e: e.tensor_single_scalar(msrow[:], ev2[:], 1.0 / DM, ALU.mult), drain=True)
            p.op("vector", lambda e: e.tensor_tensor(varow[:], mrow[:], nmrow[:], ALU.mult), drain=True)
            p.op("vector", lambda e: e.tensor_tensor(varow[:], msrow[:], varow[:], ALU.add), drain=True)
            p.op("vector", lambda e: e.tensor_scalar(varow[:], varow[:], E1, E2, ALU.add, ALU.add), inc=vCst, drain=True)
            p.wait("scalar", vCst)
            p.op("scalar", lambda e: e.activation(varow[:], varow[:], AF.Sqrt), inc=sCsq)
            p.wait("vector", sCsq)
            p.op("vector", lambda e: e.reciprocal(rsrow[:], varow[:]), drain=True)
            p.op("vector", lambda e: e.tensor_tensor(nbrow[:], nmrow[:], rsrow[:], ALU.mult), inc=vCrs, drain=True)
            p.wait("tensor", vCrs)
            p.op("tensor", lambda e: e.matmul(PW[2][:], ones1f[:], rsrow[:], start=True, stop=True))
            p.op("tensor", lambda e: e.matmul(PW[3][:], ones1f[:], nbrow[:], start=True, stop=True), inc=tCb)
            p.wait("scalar", tCb)
            p.op("scalar", lambda e: e.copy(bcr[:], PW[2][:]))
            p.op("scalar", lambda e: e.copy(bcb[:], PW[3][:]), inc=sCb)
            p.wait("vector", sCb)
            for fc in range(FC):
                p.op("vector", lambda e, fc=fc, TB=TB: e.tensor_tensor(lf2[:], x2Tb[:, fc, TB], bcr[:], ALU.mult), drain=True)
                p.op("vector", lambda e, fc=fc: e.tensor_tensor(lf2[:], lf2[:], bcb[:], ALU.add), drain=True)
                p.op("vector", lambda e, fc=fc, TB=TB: e.tensor_copy(ln2h[:, fc, TB], lf2[:]), drain=True)
                p.op("vector", lambda e, fc=fc, TB=TB: e.tensor_tensor(ln2l[:, fc, TB], lf2[:], ln2h[:, fc, TB], ALU.subtract),
                     inc=vCln, drain=True)

        p.barrier("C")
        stCp.close()

        stD = contextlib.ExitStack()
        ident = sb("ident", [128, 128], F32, stD)
        gI = S("gI")
        p.op("gpsimd", lambda e: e.iota(ident[:], pattern=[[1, 128]], base=0,
                                        channel_multiplier=-1, allow_small_or_imprecise_dtypes=True), inc=gI)
        p.wait("vector", gI)
        p.op("vector", lambda e: e.tensor_single_scalar(ident[:], ident[:], 0.0, ALU.is_equal), inc=vZ, drain=True)
        identReady = vZ.n
        w1t = [sb(f"w1t{i}", [128, FC, 128], BF16, stD) for i in range(2)]
        hfb = [sb(f"hfb{i}", [128, rows], F32, stD) for i in range(2)]
        hhb = [sb(f"hhb{i}", [128, rows], BF16, stD) for i in range(2)]
        hlb = [sb(f"hlb{i}", [128, rows], BF16, stD) for i in range(2)]
        stD1p = contextlib.ExitStack()
        PH = [ps(f"PH{i}", [128, rows], stD1p) for i in range(2)]

        dD = S("dD"); tDh = S("tDh"); sDg = S("sDg"); vDs = S("vDs")
        dD1p = [S("dD10"), S("dD11")]
        tDh_h = []; sDg_h = []; vDs_h = []
        p.wait("scalar", gQz, gQzFinal)
        if taps:
            p.wait("gpsimd", gQz, gQzFinal)
            p.dma("gpsimd", q1otap[:, :], q1_out[:, :], dG)
        for fo in range(nfo):
            fb = fo % 2
            if fo >= 2:
                p.wait("scalar", tDh, tDh_h[fo - 2])
            p.dma("scalar", w1t[fb][:, :, :],
                  q1_out[:, 128 * fo:128 * (fo + 1)].rearrange("(a c) d -> c a d", c=128), dD1p[fb])
            p.wait("tensor", dD1p[fb])
            if fo == 0:
                p.wait("tensor", vCln, vCln.n)
            if fo >= 2:
                p.wait("tensor", sDg, sDg_h[fo - 2])
            for fc in range(FC):
                for tb in range(tbn):
                    TB = slice(QB * tb, QB * (tb + 1))
                    p.op("tensor", lambda e, fb=fb, fc=fc, TB=TB:
                         e.matmul(PH[fb][:, TB], w1t[fb][:, fc], ln2h[:, fc, TB], start=(fc == 0), stop=False))
                    p.op("tensor", lambda e, fb=fb, fc=fc, TB=TB:
                         e.matmul(PH[fb][:, TB], w1t[fb][:, fc], ln2l[:, fc, TB], start=False, stop=(fc == FC - 1)),
                         inc=(tDh if (fc == FC - 1 and tb == tbn - 1) else None))
            tDh_h.append(tDh.n)
            p.wait("scalar", tDh)
            if fo >= 2:
                p.wait("scalar", vDs, vDs_h[fo - 2])
            p.op("scalar", lambda e, fb=fb, fo=fo:
                 e.activation(hfb[fb][:], PH[fb][:], AF.Gelu, bias=b1c[:, fo:fo + 1]), inc=sDg)
            sDg_h.append(sDg.n)
            p.wait("vector", sDg)
            p.wait("vector", dG, dG.n)
            p.op("vector", lambda e, fb=fb: e.tensor_copy(hhb[fb][:], hfb[fb][:]), drain=True)
            p.op("vector", lambda e, fb=fb: e.tensor_tensor(hlb[fb][:], hfb[fb][:], hhb[fb][:], ALU.subtract),
                 inc=vDs, drain=True)
            vDs_h.append(vDs.n)
            p.wait("gpsimd", vDs)
            p.dma("gpsimd", hsp[0, fo, :, :], hhb[fb][:], dG)
            p.dma("gpsimd", hsp[1, fo, :, :], hlb[fb][:], dG)

        p.wait("gpsimd", dG, dG.n)
        p.barrier("D1")
        stD1p.close()
        w2t = [sb(f"w2t{i}", [128, 1024], BF16, stD) for i in range(2)]
        hht = [sb(f"hht{i}", [128, QB], BF16, stD) for i in range(2)]
        hlt = [sb(f"hlt{i}", [128, QB], BF16, stD) for i in range(2)]
        tfb = sb("tfb", [128, QB], F32, stD)
        ytt = sb("ytt", [128, QB], F32, stD)
        yot = [sb(f"yot{i}", [128, 128], F32, stD) for i in range(2)]
        dout_hist = []
        stD2p = contextlib.ExitStack()
        PF = [ps(f"PF{i}", [128, QB], stD2p) for i in range(8)]

        tDf = S("tDf"); sDe = S("sDe"); vDy = S("vDy"); tDt = S("tDt"); sDt = S("sDt"); dOut = S("dOut")
        dDhp = [S("dDh0"), S("dDh1")]
        dDwp = [S("dDw0"), S("dDw1")]
        dOutp = [S("dOut0"), S("dOut1")]
        for tb2 in range(tbn):
            TB = slice(QB * tb2, QB * (tb2 + 1))
            for dh in range(2):
                tDf_h = []
                for fo in range(nfo):
                    hb = fo % 2
                    if fo >= 2:
                        p.wait("gpsimd", tDf, tDf_h[fo - 2])
                    p.dma("gpsimd", hht[hb][:], hsp[0, fo, :, TB], dDhp[hb])
                    p.dma("gpsimd", hlt[hb][:], hsp[1, fo, :, TB], dDhp[hb])
                    if fo >= 2:
                        p.wait("scalar", tDf, tDf_h[fo - 2])
                    p.dma("scalar", w2t[hb][:], q2_out[128 * fo:128 * (fo + 1), 1024 * dh:1024 * (dh + 1)], dDwp[hb])
                    p.wait("tensor", dDhp[hb])
                    p.wait("tensor", dDwp[hb])
                    if fo == 0:
                        p.wait("tensor", sDt, sDt.n)
                    for d in range(8):
                        p.op("tensor", lambda e, d=d, hb=hb, fo=fo:
                             e.matmul(PF[d][:], w2t[hb][:, 128 * d:128 * (d + 1)], hht[hb][:], start=(fo == 0), stop=False))
                        p.op("tensor", lambda e, d=d, hb=hb, fo=fo:
                             e.matmul(PF[d][:], w2t[hb][:, 128 * d:128 * (d + 1)], hlt[hb][:], start=False, stop=(fo == nfo - 1)),
                             inc=(tDf if d == 7 else None))
                    tDf_h.append(tDf.n)
                for d in range(8):
                    dd = 8 * dh + d
                    p.wait("vector", tDf)
                    p.op("vector", lambda e, d=d, dd=dd, TB=TB:
                         e.scalar_tensor_tensor(ytt[:], PF[d][:], b2c[:, dd:dd + 1], x2Tb[:, dd, TB], ALU.add, ALU.add),
                         inc=vDy, drain=True)
                    for s4 in range(QB // 128):
                        yb = len(dout_hist) % 2
                        p.wait("tensor", vDy)
                        if not dout_hist:
                            p.wait("tensor", vZ, identReady)
                        if dout_hist:
                            p.wait("tensor", sDt)
                        p.op("tensor", lambda e, d=d, s4=s4:
                             e.transpose(PF[d][:, 0:128], ytt[:, 128 * s4:128 * (s4 + 1)], ident[:]), inc=tDt)
                        p.wait("scalar", tDt)
                        p.wait("scalar", dOutp[yb])
                        p.op("scalar", lambda e, d=d, yb=yb: e.copy(yot[yb][:], PF[d][:, 0:128]), inc=sDt)
                        p.wait("gpsimd", sDt)
                        p.dma("gpsimd", y[QB * tb2 + 128 * s4:QB * tb2 + 128 * (s4 + 1), 128 * dd:128 * (dd + 1)],
                              yot[yb][:], dOutp[yb])
                        dout_hist.append(0)
        p.wait("gpsimd", dOutp[0])
        p.wait("gpsimd", dOutp[1])
        stD2p.close()
        stD.close()
        stC.close()

        p.emit(block)
    return nc


def _rope_tables(seq):
    d = np.arange(128)
    i = d % 64
    om = np.exp(-np.log(10000.0) * i / 64.0)
    omp = om * np.where(d < 64, -1.0, 1.0)
    omh_ = omp.astype(np.float32)
    oml_ = (omp - omh_.astype(np.float64)).astype(np.float32)
    t = np.arange(seq, dtype=np.float64)
    phase = omp[:, None] * t[None, :]
    k2s_ = np.round(phase / np.pi)
    sgs_ = np.where(np.abs(k2s_) % 2 == 1, -1.0, 1.0).astype(np.float32)
    k2c_ = np.round((phase + np.pi / 2) / np.pi)
    sgc_ = np.where(np.abs(k2c_) % 2 == 1, -1.0, 1.0).astype(np.float32)
    return (k2s_.astype(np.float32), sgs_, k2c_.astype(np.float32), sgc_,
            omh_.reshape(128, 1), oml_.reshape(128, 1))


def kernel(L0_attn_hidden_states, L0_attn_wq, L0_attn_wk, L0_attn_wv, L0_attn_wg,
           L0_attn_wo, L0_attn_norm_w, L0_ffn_w1, L0_ffn_b1, L0_ffn_w2, L0_ffn_b2):
    seq = L0_attn_hidden_states.shape[1]
    rows = seq // NCORES
    xfull = np.ascontiguousarray(np.asarray(L0_attn_hidden_states, dtype=np.float32).reshape(seq, DM))
    xTf = np.ascontiguousarray(xfull.T)
    wq = np.ascontiguousarray(np.asarray(L0_attn_wq, np.float32).T)
    wk = np.ascontiguousarray(np.asarray(L0_attn_wk, np.float32).T)
    wv = np.ascontiguousarray(np.asarray(L0_attn_wv, np.float32).T)
    wg = np.ascontiguousarray(np.asarray(L0_attn_wg, np.float32).T)
    woT = np.ascontiguousarray(np.asarray(L0_attn_wo, np.float32).T)
    w1T = np.ascontiguousarray(np.asarray(L0_ffn_w1, np.float32).T)
    w2T = np.ascontiguousarray(np.asarray(L0_ffn_w2, np.float32).T)
    b1v = np.ascontiguousarray(np.asarray(L0_ffn_b1, np.float32).reshape(DFF // 128, 128).T)
    b2v = np.ascontiguousarray(np.asarray(L0_ffn_b2, np.float32).reshape(FC, 128).T)
    nwv = np.ascontiguousarray(np.asarray(L0_attn_norm_w, np.float32).reshape(HD, 1))
    k2s_, sgs_, k2c_, sgc_, omh_, oml_ = _rope_tables(seq)

    nc = build(seq=seq)
    in_maps = []
    for c in range(NCORES):
        h0 = HPC * c
        wp = np.empty((HPC, 6, DM, HD), np.float32)
        for hh in range(HPC):
            sl = slice(HD * (h0 + hh), HD * (h0 + hh + 1))
            wp[hh, 0] = wq[:, sl]
            wp[hh, 1] = wk[:, sl]
            wp[hh, 2] = wv[:, sl]
            wp[hh, 3] = wg[:, sl]
            wp[hh, 4] = np.concatenate([wp[hh, 0][:, 64:128], wp[hh, 0][:, 0:64]], axis=1)
            wp[hh, 5] = np.concatenate([wp[hh, 1][:, 64:128], wp[hh, 1][:, 0:64]], axis=1)
        in_maps.append({
            "x": xfull,
            "xT": np.ascontiguousarray(xTf[:, rows * c:rows * (c + 1)]),
            "wproj": np.ascontiguousarray(wp),
            "wos_d": np.ascontiguousarray(woT[2 * HD * c:2 * HD * (c + 1), :]),
            "w1s": np.ascontiguousarray(w1T[(DM // NCORES) * c:(DM // NCORES) * (c + 1), :]),
            "w2s": np.ascontiguousarray(w2T[(DFF // NCORES) * c:(DFF // NCORES) * (c + 1), :]),
            "b1": b1v, "b2": b2v, "nw": nwv,
            "k2s": k2s_, "sgs": sgs_, "k2c": k2c_, "sgc": sgc_,
            "omh": omh_, "oml": oml_,
        })
    res = run_bass_kernel_spmd(nc, in_maps, core_ids=list(range(NCORES)))
    out = np.empty((1, seq, DM), np.float32)
    for c in range(NCORES):
        out[0, rows * c:rows * (c + 1), :] = res.results[c]["y"]
    return out


# version 05 (final)
# speedup vs baseline: 1.5935x; 1.0052x over previous
import contextlib
import numpy as np
import concourse.bass as bass
import concourse.mybir as mybir
from concourse.bass_utils import run_bass_kernel_spmd

F32 = mybir.dt.float32
BF16 = mybir.dt.bfloat16
AF = mybir.ActivationFunctionType
ALU = mybir.AluOpType

DM = 2048
HD = 128
NH = 16
HPC = 2
NCORES = 8
DFF = 8192
FC = DM // 128
TS = 256
QB = 512
EPS = 1e-5


def _split2(v):
    a = np.float64(np.float32(v))
    b = float(np.float32(np.float64(v) - a))
    return float(a), b


def _split3(v):
    a = np.float64(np.float32(v))
    r = np.float64(v) - a
    b = np.float64(np.float32(r))
    c = float(np.float32(r - b))
    return float(a), float(b), c


PI1, PI2, PI3 = _split3(np.pi)
H1, H2 = _split2(np.pi / 2)
E1, E2 = _split2(EPS)
S1, S2 = _split2(1.0 / np.sqrt(128.0))
C71, C72 = _split2(0.7)

ENGS = ("sync", "scalar", "vector", "tensor", "gpsimd")


class Sem:
    __slots__ = ("h", "n")

    def __init__(self, h):
        self.h = h
        self.n = 0


class P:

    def __init__(self, nc, es):
        self.nc = nc
        self.es = es
        self.q = {e: [] for e in ENGS}
        self._sems = {}

    def sem(self, name):
        if name.startswith("bar_"):
            name = "bar"
        elif not name.startswith("d") and name != "ccA":
            name = name[0] + "P"
        if name not in self._sems:
            self._sems[name] = Sem(self.es.enter_context(self.nc.semaphore(name)))
        return self._sems[name]

    def wait(self, eng, sem, v=None):
        val = sem.n if v is None else v
        self.q[eng].append(lambda e, s=sem.h, w=val: e.wait_ge(s, w))

    def op(self, eng, fn, inc=None, incn=1, drain=False):
        if inc is not None:
            inc.n += incn

        def run(e, fn=fn, inc=inc, incn=incn, drain=drain):
            instr = fn(e)
            if inc is not None:
                instr.then_inc(inc.h, incn)
            if drain:
                e.drain()

        self.q[eng].append(run)

    def dma(self, eng, out, in_, inc, transpose=False):
        inc.n += 16

        def run(e, out=out, in_=in_, inc=inc, transpose=transpose):
            if transpose:
                e.dma_start(out=out, in_=in_, transpose=True).then_inc(inc.h, 16)
            else:
                e.dma_start(out=out, in_=in_).then_inc(inc.h, 16)

        self.q[eng].append(run)

    def barrier(self, name):
        b = self.sem("bar_" + name)
        target = b.n + len(ENGS)
        for e in ENGS:
            self.op(e, lambda eng: eng.nop(), inc=b, incn=1)
        for e in ENGS:
            self.wait(e, b, target)

    def emit(self, block):
        for name in ENGS:
            fns = self.q[name]

            def body(eng, fns=fns):
                for f in fns:
                    f(eng)

            getattr(block, name)(body)


def build(seq=8192, taps=False):
    nts = seq // TS
    nqb = seq // QB
    rows = seq // NCORES
    tbn = rows // QB
    nkc = seq // 128
    nA = seq // 128
    w1r = DM // NCORES
    w2r = DFF // NCORES
    nfo = DFF // 128

    nc = bass.Bass("TRN2", target_bir_lowering=False)
    dram = lambda n, s, d, k: nc.dram_tensor(n, s, d, kind=k).ap()
    drams = lambda n, s, d: nc.dram_tensor(n, s, d, kind="Internal", addr_space="Shared").ap()
    IK = "ExternalOutput" if taps else "Internal"
    x = dram("x", [seq, DM], F32, "ExternalInput")
    xT = dram("xT", [DM, rows], F32, "ExternalInput")
    wproj = dram("wproj", [HPC, 6, DM, HD], F32, "ExternalInput")
    wos_d = dram("wos_d", [2 * HD, DM], F32, "ExternalInput")
    w1s = dram("w1s", [w1r, DFF], F32, "ExternalInput")
    w2s = dram("w2s", [w2r, DM], F32, "ExternalInput")
    b1 = dram("b1", [128, DFF // 128], F32, "ExternalInput")
    b2 = dram("b2", [128, FC], F32, "ExternalInput")
    nw = dram("nw", [HD, 1], F32, "ExternalInput")
    k2s = dram("k2s", [HD, seq], F32, "ExternalInput")
    sgs = dram("sgs", [HD, seq], F32, "ExternalInput")
    k2c = dram("k2c", [HD, seq], F32, "ExternalInput")
    sgc = dram("sgc", [HD, seq], F32, "ExternalInput")
    omh = dram("omh", [HD, 1], F32, "ExternalInput")
    oml = dram("oml", [HD, 1], F32, "ExternalInput")
    y = dram("y", [rows, DM], F32, "ExternalOutput")

    lnsp = dram("lnsp", [2, seq, DM], BF16, IK)
    qsp = dram("qsp", [HPC, 2, HD, seq], BF16, IK)
    csp = dram("csp", [HPC, HD, seq], F32, IK)
    ctab = dram("ctab", [HD, seq], F32, "Internal")
    stab = dram("stab", [HD, seq], F32, "Internal")
    hsp = dram("hsp", [2, nfo, 128, rows], BF16, IK)
    gsph = dram("gsph", [HPC * HD, seq], BF16, IK)
    gspl = dram("gspl", [HPC * HD, seq], BF16, IK)
    rs_in = [dram(f"rs_in{k}", [NCORES, 2 * HD, rows], F32, "Internal") for k in range(NCORES)]
    rs_out = [dram(f"rs_out{k}", [2 * HD, rows], F32, "Internal") for k in range(NCORES)]
    q1_in = dram("q1_in", [w1r, DFF], BF16, "Internal")
    q1tap = dram("q1tap", [w1r, DFF], BF16, IK) if taps else None
    q1otap = dram("q1otap", [DM, DFF], BF16, IK) if taps else None
    q1_out = drams("q1_out", [DM, DFF], BF16)
    q2_in = dram("q2_in", [w2r, DM], BF16, "Internal")
    q2_out = drams("q2_out", [DFF, DM], BF16)
    x2tap = dram("x2tap", [DM, rows], F32, IK) if taps else None
    dbgq = dram("dbgq", [1, 8], F32, IK) if taps else None
    ar8tap = dram("ar8tap", [8, 128], F32, IK) if taps else None
    rstap = dram("rstap", [NCORES, 2 * HD, rows], F32, IK) if taps else None
    rsotap = dram("rsotap", [2 * HD, rows], F32, IK) if taps else None
    rsused = dram("rsused", [FC, 128, rows], F32, IK) if taps else None
    avtap = dram("avtap", [HD, seq], F32, IK) if taps else None
    fence_in = dram("fence_in", [1, 128], F32, "Internal")
    fence_out = dram("fence_out", [8, 128], F32, "Internal")
    ar_in = dram("ar_in", [1, 128], F32, "Internal")
    ar_out = dram("ar_out", [8, 128], F32, "Internal")
    RG = [list(range(NCORES))]

    es = contextlib.ExitStack()
    with es:
        def sb(name, shape, dt=F32, st=None):
            stk = st if st is not None else es
            return stk.enter_context(nc.sbuf_tensor(name, shape, dt))

        def ps(name, shape, st=None):
            stk = st if st is not None else es
            return stk.enter_context(nc.psum_tensor(name, shape, F32))

        p = P(nc, es)
        S = p.sem

        onesb = sb("onesb", [128, 1], BF16)
        onesf = sb("onesf", [128, 1], F32)
        ones1f = sb("ones1f", [1, 128], F32)
        maskb = sb("maskb", [128, 896], BF16)
        nwc = sb("nwc", [HD, 1], F32)
        omhc = sb("omhc", [HD, 1], F32)
        omlc = sb("omlc", [HD, 1], F32)
        b2c = sb("b2c", [128, FC], F32)
        c20 = sb("c20", [128, 1], F32)
        b1c = sb("b1c", [128, DFF // 128], F32)

        block = es.enter_context(nc.Block())

        dW = S("dW")
        dX = S("dX")
        dLT = S("dLT")
        dSW = S("dSW")
        dG = S("dG")
        dSY = S("dSY")
        vZ = S("vZ")


        st00 = contextlib.ExitStack()
        mkf = sb("mkf", [128, 896], F32, st00)
        p.op("gpsimd", lambda e: e.iota(mkf[:], pattern=[[1, 896]], base=-384,
                                        channel_multiplier=-1, allow_small_or_imprecise_dtypes=True), inc=vZ)
        p.op("vector", lambda e: e.memset(onesb[:], 1.0))
        p.op("vector", lambda e: e.memset(onesf[:], 1.0))
        p.op("vector", lambda e: e.memset(ones1f[:], 1.0))
        p.op("vector", lambda e: e.memset(c20[:], -20.0))
        p.wait("vector", vZ)
        p.op("vector", lambda e: e.tensor_single_scalar(maskb[:], mkf[:], 0.0, ALU.is_ge), inc=vZ, drain=True)
        st00.close()

        p.dma("scalar", nwc[:], nw[:, :], dW)
        p.dma("scalar", omhc[:], omh[:, :], dW)
        p.dma("scalar", omlc[:], oml[:, :], dW)
        p.dma("scalar", b2c[:], b2[:, :], dW)
        p.dma("scalar", b1c[:], b1[:, :], dW)

        vWs = S("vWs")

        stA = contextlib.ExitStack()
        xa = [sb(f"xa{i}", [128, DM], F32, stA) for i in range(2)]
        lnhb = [sb(f"lnhb{i}", [128, DM], BF16, stA) for i in range(2)]
        lnlb = [sb(f"lnlb{i}", [128, DM], BF16, stA) for i in range(2)]
        lfb = sb("lfb", [128, DM], F32, stA)
        sqd = sb("sqd", [128, DM], F32, stA)
        s1c = sb("s1c", [128, 1], F32, stA); s2c = sb("s2c", [128, 1], F32, stA)
        mcl = sb("mcl", [128, 1], F32, stA); nmc = sb("nmc", [128, 1], F32, stA)
        msc = sb("msc", [128, 1], F32, stA); vvc = sb("vvc", [128, 1], F32, stA)
        sdc = sb("sdc", [128, 1], F32, stA); rsc = sb("rsc", [128, 1], F32, stA)
        nbc = sb("nbc", [128, 1], F32, stA)
        sqA = S("sqA"); vstA = S("vstA"); sdA = S("sdA"); vrsA = S("vrsA")
        shiA = S("shiA"); vfinA = S("vfinA"); vaffA = S("vaffA")
        dXp = [S("dX0"), S("dX1")]
        dGp = [S("dGa0"), S("dGa1")]

        p.wait("sync", vZ)
        for i in range(nA):
            b = i % 2
            if i >= 2:
                p.wait("sync", vfinA)
                p.wait("sync", dGp[b])
            p.dma("sync", xa[b][:], x[128 * i:128 * (i + 1), :], dXp[b])
            p.wait("scalar", dXp[b])
            p.op("scalar", lambda e, b=b: e.activation(sqd[:], xa[b][:], AF.Square, accum_out=s2c[:]), inc=sqA)
            p.wait("vector", dXp[b])
            p.op("vector", lambda e, b=b: e.tensor_reduce(s1c[:], xa[b][:], axis=mybir.AxisListType.X, op=ALU.add), drain=True)
            p.wait("vector", sqA)
            p.wait("vector", shiA)
            p.op("vector", lambda e: e.tensor_single_scalar(mcl[:], s1c[:], 1.0 / DM, ALU.mult), drain=True)
            p.op("vector", lambda e: e.tensor_single_scalar(nmc[:], mcl[:], -1.0, ALU.mult), drain=True)
            p.op("vector", lambda e: e.tensor_single_scalar(msc[:], s2c[:], 1.0 / DM, ALU.mult), drain=True)
            p.op("vector", lambda e: e.scalar_tensor_tensor(vvc[:], mcl[:], nmc[:, 0:1], msc[:], ALU.mult, ALU.add), drain=True)
            p.op("vector", lambda e: e.tensor_scalar(vvc[:], vvc[:], E1, E2, ALU.add, ALU.add), inc=vstA, drain=True)
            p.wait("scalar", vstA)
            p.op("scalar", lambda e: e.activation(sdc[:], vvc[:], AF.Sqrt), inc=sdA)
            p.wait("vector", sdA)
            p.op("vector", lambda e: e.reciprocal(rsc[:], sdc[:]), drain=True)
            p.op("vector", lambda e: e.tensor_tensor(nbc[:], nmc[:], rsc[:], ALU.mult), inc=vrsA, drain=True)
            p.wait("vector", vrsA, vrsA.n)
            p.op("vector", lambda e, b=b: e.tensor_scalar(lfb[:], xa[b][:], rsc[:, 0:1], nbc[:, 0:1],
                                                          ALU.mult, ALU.add), inc=vaffA, drain=True)
            p.wait("scalar", vaffA)
            p.op("scalar", lambda e, b=b: e.copy(lnhb[b][:], lfb[:]), inc=shiA)
            p.wait("vector", shiA)
            p.op("vector", lambda e, b=b: e.tensor_tensor(lnlb[b][:], lfb[:], lnhb[b][:], ALU.subtract),
                 inc=vfinA, drain=True)
            p.wait("gpsimd", vfinA)
            p.dma("gpsimd", lnsp[0, 128 * i:128 * (i + 1), :], lnhb[b][:], dGp[b])
            p.dma("gpsimd", lnsp[1, 128 * i:128 * (i + 1), :], lnlb[b][:], dGp[b])

        p.wait("gpsimd", dGp[0])
        p.wait("gpsimd", dGp[1])
        p.barrier("A")
        stA.close()

        stB = contextlib.ExitStack()
        wqkvg = [[sb(f"w{t}{i}", [128, FC, 128], BF16, stB) for i in range(2)]
                 for t in ("q", "k", "v", "g", "qp", "kp")]
        Kr = [sb(f"Kr{i}", [128, nkc, 128], BF16, stB) for i in range(2)]
        Vr = [sb(f"Vr{i}", [128, nkc, 128], BF16, stB) for i in range(2)]
        qhiA = [sb(f"qhiA{i}", [128, QB], BF16, stB) for i in range(2)]
        qloA = [sb(f"qloA{i}", [128, QB], BF16, stB) for i in range(2)]
        cmbA = [sb(f"cmbA{i}", [128, QB], F32, stB) for i in range(2)]
        pexp = [sb(f"pexp{i}", [128, QB], F32, stB) for i in range(2)]
        phi = [sb(f"phi{i}", [128, QB], BF16, stB) for i in range(2)]
        plo = [sb(f"plo{i}", [128, QB], BF16, stB) for i in range(2)]
        zrow = sb("zrow", [1, QB], F32, stB)
        rz = sb("rz", [1, QB], F32, stB)
        bc1 = sb("bc1", [128, QB], F32, stB)
        avn = sb("avn", [128, QB], F32, stB)
        sqb = sb("sqb", [128, QB], F32, stB)
        msqr = sb("msqr", [1, QB], F32, stB)
        rr_ = sb("rr_", [1, QB], F32, stB)
        bc2 = bc1
        gat = sqb
        gth = sb("gth", [128, QB], BF16, stB)
        gtl = sb("gtl", [128, QB], BF16, stB)
        stSw = contextlib.ExitStack()
        lnT = sb("lnT", [128, 2, 2, FC, TS], BF16, stSw)
        wraw = sb("wraw", [128, FC, 128], F32, stSw)
        cosQ = sb("cosQ", [128, TS], F32, stSw)
        sinQ = sb("sinQ", [128, TS], F32, stSw)
        tio = sb("tio", [128, TS], F32, stSw)
        ph_ = sb("ph_", [128, TS], F32, stSw)
        rA = sb("rA", [128, TS], F32, stSw)
        rB = sb("rB", [128, TS], F32, stSw)
        snb = sb("snb", [128, TS], F32, stSw)
        costile = sb("costile", [128, TS], F32, stSw)
        sintile = sb("sintile", [128, TS], F32, stSw)
        k2sT = sb("k2sT", [128, TS], F32, stSw); sgsT = sb("sgsT", [128, TS], F32, stSw)
        k2cT = sb("k2cT", [128, TS], F32, stSw); sgcT = sb("sgcT", [128, TS], F32, stSw)
        qf = sb("qf", [128, TS], F32, stSw)
        qsw = sb("qsw", [128, TS], F32, stSw)
        qr1 = sb("qr1", [128, TS], F32, stSw)
        qr2 = snb
        qhiT = sb("qhiT", [128, TS], BF16, stSw)
        qloT = sb("qloT", [128, TS], BF16, stSw)
        kf = sb("kf", [128, TS], F32, stSw)
        ksw = sb("ksw", [128, TS], F32, stSw)
        gsb = qr1
        cmb = snb
        stBp = contextlib.ExitStack()
        P6 = [ps(f"P6_{i}", [128, QB], stBp) for i in range(6)]
        P7 = ps("P7", [128, QB], stBp)
        P8 = ps("P8", [128, QB], stBp)
        PQs = [P6[0][:, 0:TS], P6[4][:, 0:TS]]
        PKs = [P6[1][:, 0:TS], P6[5][:, 0:TS]]
        PGs = [P6[2][:, 0:TS], P7[:, 0:TS]]
        PVs = [P6[3][:, 0:TS], P8[:, 0:TS]]

        dR = S("dR")
        dLTp = [S("dLT0"), S("dLT1")]
        dSWq = S("dSWq"); dSWk = S("dSWk")
        dGq = S("dGq")
        vTab = S("vTab"); sTab = S("sTab")
        tQd = S("tQd"); tKd = S("tKd"); tGd = S("tGd"); tVd = S("tVd")
        sQe = S("sQe"); sKe = S("sKe"); sGe = S("sGe")
        vQe = S("vQe"); vKe = S("vKe"); vGe = S("vGe"); vVe = S("vVe")


        tSc = S("tSc"); sEx = S("sEx"); vPs = S("vPs"); tPe = S("tPe")
        dQph = [[S("dQA0"), S("dQA1")], [S("dQB0"), S("dQB1")]]
        sZe = S("sZe"); vRz = S("vRz"); tB1 = S("tB1"); sB1 = S("sB1")
        vAvn = S("vAvn"); sSq = S("sSq"); tM = S("tM"); sMs = S("sMs")
        vM1 = S("vM1"); sS2 = S("sS2"); vRr = S("vRr"); tB2 = S("tB2"); sB2 = S("sB2")
        vGa = S("vGa"); dQA = S("dQA")

        def load_weights(h):
            for pj in range(6):
                p.wait("scalar", vWs, vWs.n)
                p.dma("scalar", wraw[:, :, :],
                      wproj[h, pj, :, :].rearrange("(a c) d -> c a d", c=128), dW)
                p.wait("vector", dW)
                dhi, dlo = wqkvg[pj]
                p.op("vector", lambda e, d=dhi: e.tensor_copy(d[:], wraw[:]), drain=True)
                p.op("vector", lambda e, dh_=dhi, dl=dlo:
                     e.tensor_tensor(dl[:], wraw[:], dh_[:], ALU.subtract), inc=vWs, drain=True)

        def sweep(h):
            load_weights(h)
            tQd_h = []; tVd_h = []; tKd_h2 = []; sQe_h = []; sKe_h = []; sGe_h = []; vVe_h = []
            for tt in range(nts):
                b = tt % 2
                T = slice(TS * tt, TS * (tt + 1))
                PQ = PQs[b]; PK = PKs[b]; PG = PGs[b]; PV = PVs[b]
                PVfull = (P6[3] if b == 0 else P8)
                if tt >= 2:
                    p.wait("sync", tKd, tKd_h2[tt - 2])
                    p.wait("sync", tVd, tVd_h[tt - 2])
                if tt >= 2:
                    p.wait("scalar", tKd, tKd_h2[tt - 2])
                    p.wait("scalar", tVd, tVd_h[tt - 2])
                for half in range(2):
                    for fc in range(FC):
                        p.dma("sync", lnT[:, b, half, fc, :],
                              lnsp[half, T, 128 * fc:128 * (fc + 1)], dLTp[b], transpose=True)
                if h == 0:
                    p.wait("gpsimd", vTab)
                    p.wait("gpsimd", dGq)
                    p.dma("gpsimd", k2sT[:], k2s[:, T], dR)
                    p.dma("gpsimd", sgsT[:], sgs[:, T], dR)
                    p.dma("gpsimd", k2cT[:], k2c[:, T], dR)
                    p.dma("gpsimd", sgcT[:], sgc[:, T], dR)
                    gIo = S("gIo")
                    p.op("gpsimd", lambda e, tt=tt: e.iota(tio[:], pattern=[[1, TS]], base=TS * tt,
                                                           channel_multiplier=0,
                                                           allow_small_or_imprecise_dtypes=True), inc=gIo)
                    p.wait("vector", dR)
                    p.wait("vector", gIo)
                    V_ = "vector"
                    p.op(V_, lambda e: e.tensor_single_scalar(ph_[:], tio[:], omhc[:, 0:1], ALU.mult), drain=True)
                    p.op(V_, lambda e: e.scalar_tensor_tensor(ph_[:], tio[:], omlc[:, 0:1], ph_[:], ALU.mult, ALU.add), drain=True)
                    p.op(V_, lambda e: e.scalar_tensor_tensor(rA[:], k2sT[:], -PI1, ph_[:], ALU.mult, ALU.add), drain=True)
                    p.op(V_, lambda e: e.scalar_tensor_tensor(rB[:], k2sT[:], -PI2, rA[:], ALU.mult, ALU.add), drain=True)
                    p.op(V_, lambda e: e.scalar_tensor_tensor(rA[:], k2sT[:], -PI3, rB[:], ALU.mult, ALU.add), inc=vTab, drain=True)
                    p.wait("scalar", vTab)
                    p.wait("scalar", dGq)
                    p.op("scalar", lambda e: e.activation(snb[:], rA[:], AF.Sin), inc=sTab)
                    p.wait(V_, sTab)
                    p.op(V_, lambda e: e.tensor_tensor(sintile[:], snb[:], sgsT[:], ALU.mult), inc=vTab, drain=True)
                    p.op(V_, lambda e: e.tensor_scalar(rB[:], ph_[:], H1, H2, ALU.add, ALU.add), drain=True)
                    p.op(V_, lambda e: e.scalar_tensor_tensor(rA[:], k2cT[:], -PI1, rB[:], ALU.mult, ALU.add), drain=True)
                    p.op(V_, lambda e: e.scalar_tensor_tensor(rB[:], k2cT[:], -PI2, rA[:], ALU.mult, ALU.add), drain=True)
                    p.op(V_, lambda e: e.scalar_tensor_tensor(rA[:], k2cT[:], -PI3, rB[:], ALU.mult, ALU.add), inc=vTab, drain=True)
                    p.wait("scalar", vTab)
                    p.op("scalar", lambda e: e.activation(snb[:], rA[:], AF.Sin), inc=sTab)
                    p.wait(V_, sTab)
                    p.op(V_, lambda e: e.tensor_tensor(costile[:], snb[:], sgcT[:], ALU.mult), drain=True)
                    p.op(V_, lambda e: e.tensor_single_scalar(cosQ[:], costile[:], S1, ALU.mult), drain=True)
                    p.op(V_, lambda e: e.scalar_tensor_tensor(cosQ[:], costile[:], S2, cosQ[:], ALU.mult, ALU.add), drain=True)
                    p.op(V_, lambda e: e.tensor_single_scalar(sinQ[:], sintile[:], S1, ALU.mult), drain=True)
                    p.op(V_, lambda e: e.scalar_tensor_tensor(sinQ[:], sintile[:], S2, sinQ[:], ALU.mult, ALU.add), inc=vTab, drain=True)
                    p.wait("gpsimd", vTab)
                    p.dma("gpsimd", ctab[:, T], costile[:], dGq)
                    p.dma("gpsimd", stab[:, T], sintile[:], dGq)
                else:
                    p.wait("gpsimd", vQe)
                    p.wait("gpsimd", vKe)
                    p.wait("gpsimd", dGq)
                    p.dma("gpsimd", costile[:], ctab[:, T], dR)
                    p.dma("gpsimd", sintile[:], stab[:, T], dR)
                    p.wait("vector", dR)
                    p.op("vector", lambda e: e.tensor_single_scalar(cosQ[:], costile[:], S1, ALU.mult), drain=True)
                    p.op("vector", lambda e: e.scalar_tensor_tensor(cosQ[:], costile[:], S2, cosQ[:], ALU.mult, ALU.add), drain=True)
                    p.op("vector", lambda e: e.tensor_single_scalar(sinQ[:], sintile[:], S1, ALU.mult), drain=True)
                    p.op("vector", lambda e: e.scalar_tensor_tensor(sinQ[:], sintile[:], S2, sinQ[:], ALU.mult, ALU.add), inc=vTab, drain=True)

                p.wait("tensor", dLTp[b])
                if tt == 0:
                    p.wait("tensor", vWs)
                if tt >= 2:
                    p.wait("tensor", sQe, sQe_h[tt - 2]); p.wait("tensor", sKe, sKe_h[tt - 2])
                    p.wait("tensor", sGe, sGe_h[tt - 2]); p.wait("tensor", vVe, vVe_h[tt - 2])
                wqh, wql = wqkvg[0]; wkh, wkl = wqkvg[1]
                wvh, wvl = wqkvg[2]; wgh, wgl = wqkvg[3]
                wqph, wqpl = wqkvg[4]; wkph, wkpl = wqkvg[5]
                for (PP, whi_, wlo_, tsem) in ((PQ, wqh, wql, tQd), (PK, wkh, wkl, tKd), (PG, wgh, wgl, tGd)):
                    for fc in range(FC):
                        p.op("tensor", lambda e, PP=PP, w=whi_, fc=fc, b=b:
                             e.matmul(PP, w[:, fc], lnT[:, b, 0, fc], start=(fc == 0), stop=False))
                        p.op("tensor", lambda e, PP=PP, w=whi_, fc=fc, b=b:
                             e.matmul(PP, w[:, fc], lnT[:, b, 1, fc], start=False, stop=False))
                        p.op("tensor", lambda e, PP=PP, w=wlo_, fc=fc, b=b:
                             e.matmul(PP, w[:, fc], lnT[:, b, 0, fc], start=False, stop=(fc == FC - 1)),
                             inc=(tsem if fc == FC - 1 else None))
                for s4 in range(TS // 128):
                    for fc in range(FC):
                        sl = slice(128 * s4, 128 * (s4 + 1))
                        p.op("tensor", lambda e, PVf=PVfull, fc=fc, b=b, sl=sl:
                             e.matmul(PVf[:, sl], lnT[:, b, 0, fc, sl], wvh[:, fc], start=(fc == 0), stop=False))
                        p.op("tensor", lambda e, PVf=PVfull, fc=fc, b=b, sl=sl:
                             e.matmul(PVf[:, sl], lnT[:, b, 0, fc, sl], wvl[:, fc], start=False, stop=False))
                        p.op("tensor", lambda e, PVf=PVfull, fc=fc, b=b, sl=sl:
                             e.matmul(PVf[:, sl], lnT[:, b, 1, fc, sl], wvh[:, fc], start=False, stop=(fc == FC - 1)),
                             inc=(tVd if (fc == FC - 1 and s4 == TS // 128 - 1) else None))

                p.wait("scalar", tQd)
                p.wait("scalar", vQe, vQe.n)
                p.op("scalar", lambda e, PQ=PQ: e.copy(qf[:], PQ), inc=sQe)
                p.wait("scalar", tKd)
                p.wait("scalar", vKe, vKe.n)
                p.op("scalar", lambda e, PK=PK: e.copy(kf[:], PK), inc=sKe)
                p.wait("tensor", sQe)
                p.wait("tensor", sKe)
                for (PP, wph, wpl, tsem) in ((PQ, wqph, wqpl, tQd), (PK, wkph, wkpl, tKd)):
                    for fc in range(FC):
                        p.op("tensor", lambda e, PP=PP, w=wph, fc=fc, b=b:
                             e.matmul(PP, w[:, fc], lnT[:, b, 0, fc], start=(fc == 0), stop=False))
                        p.op("tensor", lambda e, PP=PP, w=wph, fc=fc, b=b:
                             e.matmul(PP, w[:, fc], lnT[:, b, 1, fc], start=False, stop=False))
                        p.op("tensor", lambda e, PP=PP, w=wpl, fc=fc, b=b:
                             e.matmul(PP, w[:, fc], lnT[:, b, 0, fc], start=False, stop=(fc == FC - 1)),
                             inc=(tsem if fc == FC - 1 else None))
                tQd_h.append(tQd.n)
                tKd_h2.append(tKd.n)
                p.wait("scalar", tQd)
                p.op("scalar", lambda e, PQ=PQ: e.copy(qsw[:], PQ), inc=sQe)
                p.wait("scalar", tKd)
                p.op("scalar", lambda e, PK=PK: e.copy(ksw[:], PK), inc=sKe)
                p.wait("vector", sQe)
                p.wait("vector", vTab)
                p.wait("vector", dGq)
                p.op("vector", lambda e: e.tensor_tensor(qr1[:], qf[:], cosQ[:], ALU.mult))
                p.op("vector", lambda e: e.tensor_tensor(qr2[:], qsw[:], sinQ[:], ALU.mult), drain=True)
                p.op("vector", lambda e: e.tensor_tensor(qr1[:], qr1[:], qr2[:], ALU.add), drain=True)
                p.op("vector", lambda e: e.tensor_copy(qhiT[:], qr1[:]), drain=True)
                p.op("vector", lambda e: e.tensor_tensor(qloT[:], qr1[:], qhiT[:], ALU.subtract), inc=vQe, drain=True)
                p.wait("gpsimd", vQe)
                p.dma("gpsimd", qsp[h, 0, :, T], qhiT[:], dGq)
                p.dma("gpsimd", qsp[h, 1, :, T], qloT[:], dGq)

                p.wait("vector", sKe)
                kcs = slice(2 * tt, 2 * (tt + 1))
                p.op("vector", lambda e: e.tensor_tensor(qr1[:], kf[:], costile[:], ALU.mult))
                p.op("vector", lambda e: e.tensor_tensor(qr2[:], ksw[:], sintile[:], ALU.mult), drain=True)
                p.op("vector", lambda e: e.tensor_tensor(qr1[:], qr1[:], qr2[:], ALU.add), drain=True)
                p.op("vector", lambda e, kcs=kcs: e.tensor_copy(Kr[0][:, kcs, :], qr1[:]), drain=True)
                p.op("vector", lambda e, kcs=kcs: e.tensor_tensor(Kr[1][:, kcs, :], qr1[:], Kr[0][:, kcs, :], ALU.subtract), inc=vKe, drain=True)

                p.wait("scalar", tGd)
                p.wait("scalar", vGe, vGe.n)
                p.wait("scalar", vKe)
                p.op("scalar", lambda e, PG=PG: e.activation(gsb[:], PG, AF.Silu), inc=sGe)
                p.wait("vector", sGe)
                p.op("vector", lambda e: e.tensor_single_scalar(cmb[:], gsb[:], nwc[:, 0:1], ALU.mult), inc=vGe, drain=True)
                p.wait("gpsimd", vGe)
                p.dma("gpsimd", csp[h, :, T], cmb[:], dGq)

                p.wait("vector", tVd)
                p.op("vector", lambda e, kcs=kcs, PV=PV: e.tensor_copy(Vr[0][:, kcs, :], PV), drain=True)
                p.op("vector", lambda e, kcs=kcs, PV=PV: e.tensor_tensor(Vr[1][:, kcs, :], PV, Vr[0][:, kcs, :], ALU.subtract), inc=vVe, drain=True)
                sQe_h.append(sQe.n); sKe_h.append(sKe.n); sGe_h.append(sGe.n)
                vVe_h.append(vVe.n); tVd_h.append(tVd.n)

        def attention(h):
            qld = "gpsimd" if h == 0 else "scalar"
            qst = "gpsimd" if h == 0 else "sync"
            dST = dG if h == 0 else dSY
            p.wait(qld, dGq)
            vPs_h = []
            tPe_h = []
            tSc_h = []
            for qb in range(nqb):
                ab = qb % 2
                dQ = dQph[h][ab]
                QT = slice(QB * qb, QB * (qb + 1))
                if qb >= 2:
                    p.wait(qld, tSc)
                    p.wait(qld, vGa, vGa.n)
                p.dma(qld, qhiA[ab][:], qsp[h, 0, :, QT], dQ)
                p.dma(qld, qloA[ab][:], qsp[h, 1, :, QT], dQ)
                p.dma(qld, cmbA[ab][:], csp[h, :, QT], dQ)
                nkq = 4 * (qb + 1)
                base = len(vPs_h)

                def emit_scores(kc, qb=qb, ab=ab, base=base, dQ=dQ):
                    pb = (base + kc) % 2
                    PSb = PS2[pb]
                    if kc == 0:
                        p.wait("tensor", dQ)
                    u = base + kc
                    if u >= 2:
                        p.wait("tensor", vPs, vPs_h[u - 2])
                    p.op("tensor", lambda e, PSb=PSb, kc=kc, ab=ab:
                         e.matmul(PSb[:], Kr[0][:, kc, :], qhiA[ab][:], start=True, stop=False))
                    p.op("tensor", lambda e, PSb=PSb, kc=kc, ab=ab:
                         e.matmul(PSb[:], Kr[0][:, kc, :], qloA[ab][:], start=False, stop=False))
                    p.op("tensor", lambda e, PSb=PSb, kc=kc, ab=ab:
                         e.matmul(PSb[:], Kr[1][:, kc, :], qhiA[ab][:], start=False, stop=True), inc=tSc)
                    tSc_h.append(tSc.n)

                emit_scores(0)
                if nkq > 1:
                    emit_scores(1)
                for kc in range(nkq):
                    u = base + kc
                    pb = u % 2
                    PSb = PS2[pb]
                    diag = kc >= 4 * qb
                    p.wait("scalar", tSc, tSc_h[u])
                    if u >= 2:
                        p.wait("scalar", vPs, vPs_h[u - 2])
                        p.wait("scalar", tPe, tPe_h[u - 2])
                    p.op("scalar", lambda e, pb=pb, PSb=PSb:
                         e.activation(pexp[pb][:], PSb[:], AF.Exp, bias=c20[:, 0:1]), inc=sEx)
                    if diag:
                        p.wait("vector", sEx)
                        p.op("vector", lambda e, pb=pb, off=384 - 128 * (kc - 4 * qb):
                             e.tensor_tensor(pexp[pb][:], pexp[pb][:], maskb[:, off:off + QB], ALU.mult), drain=True)
                        if u >= 2:
                            p.wait("vector", tPe, tPe_h[u - 2])
                        p.op("vector", lambda e, pb=pb: e.tensor_copy(phi[pb][:], pexp[pb][:]), drain=True)
                        p.op("vector", lambda e, pb=pb: e.tensor_tensor(plo[pb][:], pexp[pb][:], phi[pb][:], ALU.subtract),
                             inc=vPs, drain=True)
                    else:
                        p.op("scalar", lambda e: e.drain())
                        p.op("scalar", lambda e, pb=pb: e.copy(phi[pb][:], pexp[pb][:]), inc=sEx)
                        p.wait("vector", sEx)
                        if u >= 2:
                            p.wait("vector", tPe, tPe_h[u - 2])
                        p.op("vector", lambda e, pb=pb: e.tensor_tensor(plo[pb][:], pexp[pb][:], phi[pb][:], ALU.subtract),
                             inc=vPs, drain=True)
                    vPs_h.append(vPs.n)
                    p.wait("tensor", vPs)
                    last = kc == nkq - 1
                    p.op("tensor", lambda e, kc=kc, pb=pb, qb=qb:
                         e.matmul(PAV[:], Vr[0][:, kc, :], phi[pb][:], start=(kc == 0), stop=False))
                    p.op("tensor", lambda e, kc=kc, pb=pb:
                         e.matmul(PAV[:], Vr[0][:, kc, :], plo[pb][:], start=False, stop=False))
                    p.op("tensor", lambda e, kc=kc, pb=pb, last=last:
                         e.matmul(PAV[:], Vr[1][:, kc, :], phi[pb][:], start=False, stop=last))
                    p.op("tensor", lambda e, kc=kc, pb=pb:
                         e.matmul(PZ, onesb[:], phi[pb][:], start=(kc == 0), stop=False))
                    p.op("tensor", lambda e, kc=kc, pb=pb, last=last:
                         e.matmul(PZ, onesb[:], plo[pb][:], start=False, stop=last), inc=tPe)
                    tPe_h.append(tPe.n)
                    if kc + 2 < nkq:
                        emit_scores(kc + 2)
                p.wait("scalar", tPe)
                p.op("scalar", lambda e: e.copy(zrow[:], PZ), inc=sZe)
                p.wait("vector", sZe)
                p.op("vector", lambda e: e.reciprocal(rz[:], zrow[:]), inc=vRz, drain=True)
                p.wait("tensor", vRz)
                p.op("tensor", lambda e: e.matmul(PB1[:], ones1f[:], rz[:], start=True, stop=True), inc=tB1)
                p.wait("scalar", tB1)
                p.op("scalar", lambda e: e.copy(bc1[:], PB1[:]), inc=sB1)
                p.wait("vector", sB1)
                p.op("vector", lambda e: e.tensor_tensor(avn[:], PAV[:], bc1[:], ALU.mult), inc=vAvn, drain=True)
                if taps and h == 0:
                    p.wait("gpsimd" if h == 0 else "sync", vAvn)
                    p.dma("gpsimd" if h == 0 else "sync", avtap[:, QT], avn[:], dST)
                p.wait("scalar", vAvn)
                p.op("scalar", lambda e: e.activation(sqb[:], avn[:], AF.Square), inc=sSq)
                p.wait("tensor", sSq)
                p.wait("tensor", vAvn)
                p.op("tensor", lambda e: e.matmul(PM, onesf[:], sqb[:], start=True, stop=True), inc=tM)
                p.wait("scalar", tM)
                p.op("scalar", lambda e: e.copy(msqr[:], PM), inc=sMs)
                p.wait("vector", sMs)
                p.op("vector", lambda e: e.tensor_scalar(msqr[:], msqr[:], 1.0 / 128.0, E1, ALU.mult, ALU.add), drain=True)
                p.op("vector", lambda e: e.tensor_single_scalar(msqr[:], msqr[:], E2, ALU.add), inc=vM1, drain=True)
                p.wait("scalar", vM1)
                p.op("scalar", lambda e: e.activation(msqr[:], msqr[:], AF.Sqrt), inc=sS2)
                p.wait("vector", sS2)
                p.op("vector", lambda e: e.reciprocal(rr_[:], msqr[:]), inc=vRr, drain=True)
                p.wait("tensor", vRr)
                p.op("tensor", lambda e: e.matmul(PB2[:], ones1f[:], rr_[:], start=True, stop=True), inc=tB2)
                p.wait("scalar", tB2)
                p.op("scalar", lambda e: e.copy(bc2[:], PB2[:]), inc=sB2)
                p.wait("vector", sB2)
                p.wait("vector", dQ)
                p.op("vector", lambda e: e.tensor_tensor(avn[:], avn[:], bc2[:], ALU.mult), drain=True)
                p.wait("vector", dST, dST.n)
                p.op("vector", lambda e, ab=ab: e.tensor_tensor(gat[:], avn[:], cmbA[ab][:], ALU.mult), drain=True)
                p.op("vector", lambda e: e.tensor_copy(gth[:], gat[:]), drain=True)
                p.op("vector", lambda e: e.tensor_tensor(gtl[:], gat[:], gth[:], ALU.subtract), inc=vGa, drain=True)
                p.wait(qst, vGa)
                p.dma(qst, gsph[HD * h:HD * (h + 1), QT], gth[:], dST)
                p.dma(qst, gspl[HD * h:HD * (h + 1), QT], gtl[:], dST)

        PS2 = [P6[0], P6[1]]
        PAV = P6[2]
        PZ = P6[3][0:1, :]
        PB1 = P6[4]
        PM = P6[5][0:1, :]
        PB2 = P6[4]

        sweep(0)
        p.wait("gpsimd", dGq)
        p.wait("gpsimd", dG, dG.n)
        p.barrier("s0")
        attention(0)
        p.barrier("a0")
        sweep(1)
        p.wait("gpsimd", dGq)
        p.wait("gpsimd", dG, dG.n)
        p.barrier("s1")
        stSw.close()

        stQ = contextlib.ExitStack()
        wslc = sb("wslc", [128, 1024], F32, stQ)
        racc1 = sb("racc1", [128, 1], F32, stQ)
        racc2 = sb("racc2", [128, 1], F32, stQ)
        rtmp = sb("rtmp", [128, 1], F32, stQ)
        ppb = sb("ppb", [128, 1024], F32, stQ)
        nnb = sb("nnb", [128, 1024], F32, stQ)
        qtb = sb("qtb", [128, 1024], BF16, stQ)
        s12 = sb("s12", [1, 128], F32, stQ)
        ar8 = sb("ar8", [8, 128], F32, stQ)
        ones8 = sb("ones8", [8, 1], F32, stQ)
        fin_sb = sb("fin_sb", [1, 128], F32, stQ)
        arb = sb("arb", [1, 2], F32, stQ)
        thb = sb("thb", [1, 2], F32, stQ)
        nthb = sb("nthb", [1, 2], F32, stQ)
        thbc = sb("thbc", [128, 2], F32, stQ)
        nthbc = sb("nthbc", [128, 2], F32, stQ)
        PQm = P6[5][0:1, 0:2]
        dQz = S("dQz"); dQg = S("dQg"); vQz = S("vQz"); tQz = S("tQz"); sQz = S("sQz"); gQz = S("gQz")

        p.op("vector", lambda e: e.memset(ones8[:], 1.0))
        p.op("vector", lambda e: e.memset(s12[:], 0.0))
        p.op("vector", lambda e: e.memset(fin_sb[:], 0.0), drain=True)
        p.op("vector", lambda e: e.memset(racc1[:], 0.0))
        p.op("vector", lambda e: e.memset(racc2[:], 0.0), drain=True)
        npc1 = (w1r * DFF) // (128 * 1024)
        npc2 = (w2r * DM) // (128 * 1024)
        def wpiece(mat, rowsz, i):
            wch = int(mat.tensor.shape[1]) // 1024
            rj, cj = divmod(i, wch)
            return mat[128 * rj:128 * (rj + 1), 1024 * cj:1024 * (cj + 1)]
        for mat, npc, racc in ((w1s, npc1, racc1), (w2s, npc2, racc2)):
            for i in range(npc):
                p.wait("scalar", vQz)
                p.dma("scalar", wslc[:], wpiece(mat, None, i), dQz)
                p.wait("vector", dQz)
                p.op("vector", lambda e: e.tensor_reduce(rtmp[:], wslc[:], axis=mybir.AxisListType.X,
                                                         op=ALU.add, apply_absolute_value=True), drain=True)
                p.op("vector", lambda e, racc=racc: e.tensor_tensor(racc[:], racc[:], rtmp[:], ALU.add),
                     inc=vQz, drain=True)
        p.wait("tensor", vQz)
        p.op("tensor", lambda e: e.matmul(P6[5][0:1, 0:1], onesf[:], racc1[:], start=True, stop=True))
        p.op("tensor", lambda e: e.matmul(P6[5][0:1, 1:2], onesf[:], racc2[:], start=True, stop=True), inc=tQz)
        p.wait("scalar", tQz)
        p.op("scalar", lambda e: e.copy(s12[0:1, 0:2], PQm), inc=sQz)
        p.wait("gpsimd", sQz)
        p.dma("gpsimd", ar_in[:, :], s12[:], dQg)
        p.dma("gpsimd", fence_in[:, :], fin_sb[:], dQg)
        p.wait("gpsimd", dQg)
        p.op("gpsimd", lambda e: e.collective_compute("AllGather", ALU.bypass, replica_groups=RG,
                                                      ins=[ar_in[:, :]], outs=[ar_out[:, :]]), inc=gQz)
        p.op("gpsimd", lambda e: e.collective_compute("AllGather", ALU.bypass, replica_groups=RG,
                                                      ins=[fence_in[:, :]], outs=[fence_out[:, :]]), inc=gQz)

        attention(1)
        p.wait("gpsimd", dGq)
        p.wait("sync", dSY, dSY.n)
        p.barrier("a1")

        p.dma("gpsimd", ar8[:], ar_out[:, :], dQg)
        p.wait("tensor", dQg)
        p.op("tensor", lambda e: e.matmul(P6[5][0:1, 4:6], ones8[:], ar8[:, 0:2], start=True, stop=True), inc=tQz)
        p.wait("scalar", tQz)
        p.op("scalar", lambda e: e.copy(arb[:], P6[5][0:1, 4:6]), inc=sQz)
        p.wait("vector", sQz)
        p.op("vector", lambda e: e.tensor_single_scalar(arb[:], arb[:], 1.0 / (DFF * DM), ALU.mult), drain=True)
        p.op("vector", lambda e: e.tensor_single_scalar(thb[:], arb[:], C71, ALU.mult), drain=True)
        p.op("vector", lambda e: e.scalar_tensor_tensor(thb[:], arb[:], C72, thb[:], ALU.mult, ALU.add), drain=True)
        p.op("vector", lambda e: e.tensor_single_scalar(nthb[:], thb[:], -1.0, ALU.mult), inc=vQz, drain=True)
        if taps:
            p.wait("gpsimd", vQz)
            p.dma("gpsimd", dbgq[0:1, 0:2], s12[0:1, 0:2], dG)
            p.dma("gpsimd", dbgq[0:1, 2:4], arb[:], dG)
            p.dma("gpsimd", dbgq[0:1, 4:6], thb[:], dG)
            p.dma("gpsimd", ar8tap[:, :], ar8[:], dG)
        p.wait("tensor", vQz)
        p.op("tensor", lambda e: e.matmul(P6[4][:, 0:2], ones1f[:], thb[:], start=True, stop=True))
        p.op("tensor", lambda e: e.matmul(P6[4][:, 2:4], ones1f[:], nthb[:], start=True, stop=True), inc=tQz)
        p.wait("scalar", tQz)
        p.op("scalar", lambda e: e.copy(thbc[:], P6[4][:, 0:2]))
        p.op("scalar", lambda e: e.copy(nthbc[:], P6[4][:, 2:4]), inc=sQz)
        for mat, npc, qdst, col in ((w1s, npc1, q1_in, 0), (w2s, npc2, q2_in, 1)):
            for i in range(npc):
                p.wait("scalar", vQz)
                p.wait("scalar", dG, dG.n)
                p.dma("scalar", wslc[:], wpiece(mat, None, i), dQz)
                p.wait("vector", dQz)
                p.wait("vector", sQz)
                p.op("vector", lambda e, col=col: e.tensor_single_scalar(ppb[:], wslc[:], thbc[:, col:col + 1], ALU.is_ge))
                p.op("vector", lambda e, col=col: e.tensor_single_scalar(nnb[:], wslc[:], nthbc[:, col:col + 1], ALU.is_le), drain=True)
                p.op("vector", lambda e: e.tensor_tensor(qtb[:], ppb[:], nnb[:], ALU.subtract), inc=vQz, drain=True)
                p.wait("gpsimd", vQz)
                p.dma("gpsimd", wpiece(qdst, None, i), qtb[:], dG)
                if taps and qdst is q1_in:
                    p.dma("gpsimd", wpiece(q1tap, None, i), qtb[:], dG)
        p.wait("gpsimd", dG, dG.n)
        p.op("gpsimd", lambda e: e.collective_compute("AllGather", ALU.bypass, replica_groups=RG,
                                                      ins=[q1_in[:, :]], outs=[q1_out[:, :]]), inc=gQz)
        p.op("gpsimd", lambda e: e.collective_compute("AllGather", ALU.bypass, replica_groups=RG,
                                                      ins=[q2_in[:, :]], outs=[q2_out[:, :]]), inc=gQz)
        p.op("gpsimd", lambda e: e.collective_compute("AllGather", ALU.bypass, replica_groups=RG,
                                                      ins=[fence_in[:, :]], outs=[fence_out[:, :]]), inc=gQz)
        gQzFinal = gQz.n

        attention(1)
        p.wait("gpsimd", dG, dG.n)
        p.wait("sync", dSY, dSY.n)
        p.barrier("a1")
        stQ.close()
        stBp.close()
        stB.close()

        ccA = S("ccA")
        stB3 = contextlib.ExitStack()
        wosr = sb("wosr", [128, 2, DM], F32, stB3)
        woshh = sb("woshh", [128, 2, DM], BF16, stB3)
        wosll = sb("wosll", [128, 2, DM], BF16, stB3)
        gh3 = [sb(f"gh3{i}", [128, 2, QB], BF16, stB3) for i in range(2)]
        gl3 = [sb(f"gl3{i}", [128, 2, QB], BF16, stB3) for i in range(2)]
        pev = [sb(f"pev{i}", [128, QB], F32, stB3) for i in range(4)]
        PWps = [[P6[0], P6[1]], [P6[2], P6[3]]]
        dB3 = S("dB3"); vB3 = S("vB3"); tB3 = S("tB3"); sB3 = S("sB3")
        dB3p = [S("dB30"), S("dB31")]
        dGrs = [S("dGrs0"), S("dGrs1")]
        p.dma("scalar", wosr[:, 0, :], wos_d[0:128, :], dB3)
        p.dma("scalar", wosr[:, 1, :], wos_d[128:256, :], dB3)
        p.wait("vector", dB3)
        p.op("vector", lambda e: e.tensor_copy(woshh[:], wosr[:]), drain=True)
        p.op("vector", lambda e: e.tensor_tensor(wosll[:], wosr[:], woshh[:], ALU.subtract), inc=vB3, drain=True)
        tB3_h = []
        sB3_h = []
        for k in range(NCORES):
            for tt in range(nqb):
                QT = slice(QB * tt, QB * (tt + 1))
                pb3 = tt % 2
                PWp = PWps[pb3]
                it3 = k * nqb + tt
                if it3 >= 2:
                    p.wait("scalar", tB3, tB3_h[it3 - 2])
                for hh in range(2):
                    p.dma("scalar", gh3[pb3][:, hh, :], gsph[HD * hh:HD * (hh + 1), QT], dB3p[pb3])
                    p.dma("scalar", gl3[pb3][:, hh, :], gspl[HD * hh:HD * (hh + 1), QT], dB3p[pb3])
                if it3 == 0:
                    p.wait("tensor", vB3)
                p.wait("tensor", dB3p[pb3])
                if it3 == 0:
                    p.wait("tensor", dB3)
                if it3 >= 2:
                    p.wait("tensor", sB3, sB3_h[it3 - 2])
                for d in range(2):
                    dout0 = 256 * k + 128 * d
                    for hc in range(2):
                        st_ = hc == 0
                        sp_ = hc == 1
                        p.op("tensor", lambda e, d=d, hc=hc, dout0=dout0, pb3=pb3, st_=st_, PWp=PWp:
                             e.matmul(PWp[d][:], woshh[:, hc, dout0:dout0 + 128], gh3[pb3][:, hc, :], start=st_, stop=False))
                        p.op("tensor", lambda e, d=d, hc=hc, dout0=dout0, pb3=pb3, PWp=PWp:
                             e.matmul(PWp[d][:], woshh[:, hc, dout0:dout0 + 128], gl3[pb3][:, hc, :], start=False, stop=False))
                        p.op("tensor", lambda e, d=d, hc=hc, dout0=dout0, pb3=pb3, sp_=sp_, PWp=PWp:
                             e.matmul(PWp[d][:], wosll[:, hc, dout0:dout0 + 128], gh3[pb3][:, hc, :], start=False, stop=sp_),
                             inc=(tB3 if (d == 1 and hc == 1) else None))
                tB3_h.append(tB3.n)
                p.wait("scalar", tB3)
                p.wait("scalar", dGrs[pb3])
                p.op("scalar", lambda e, pb3=pb3, PWp=PWp: e.copy(pev[2 * pb3][:], PWp[0][:]))
                p.op("scalar", lambda e, pb3=pb3, PWp=PWp: e.copy(pev[2 * pb3 + 1][:], PWp[1][:]), inc=sB3)
                sB3_h.append(sB3.n)
                p.wait("sync", sB3)
                jj = (QB * tt) // rows
                col = (QB * tt) % rows
                p.dma("sync", rs_in[k][jj, 0:128, col:col + QB], pev[2 * pb3][:], dGrs[pb3])
                p.dma("sync", rs_in[k][jj, 128:256, col:col + QB], pev[2 * pb3 + 1][:], dGrs[pb3])
                if taps and k == 0:
                    p.dma("sync", rstap[jj, 0:128, col:col + QB], pev[2 * pb3][:], dGrs[pb3])
                    p.dma("sync", rstap[jj, 128:256, col:col + QB], pev[2 * pb3 + 1][:], dGrs[pb3])
            p.wait("gpsimd", dGrs[0])
            p.wait("gpsimd", dGrs[1])
            p.op("gpsimd", lambda e, k=k: e.collective_compute(
                "ReduceScatter", ALU.add, replica_groups=RG,
                ins=[rs_in[k][:, :, :]],
                outs=[rs_out[k][:, :]]), inc=ccA)
        p.op("gpsimd", lambda e: e.collective_compute("AllGather", ALU.bypass, replica_groups=RG,
                                                      ins=[fence_in[:, :]], outs=[fence_out[:, :]]), inc=ccA)
        if taps:
            p.wait("gpsimd", ccA)
            p.dma("gpsimd", rsotap[:, :], rs_out[0][:, :], dG)
        stB3.close()
        dGcopy_mark = dG.n

        stC = contextlib.ExitStack()
        x2Tb = sb("x2Tb", [128, FC, rows], F32, stC)
        ln2h = sb("ln2h", [128, FC, rows], BF16, stC)
        ln2l = sb("ln2l", [128, FC, rows], BF16, stC)
        rst = [sb(f"rst{i}", [128, rows], F32, stC) for i in range(2)]
        sq2 = [sb(f"sq2{i}", [128, QB], F32, stC) for i in range(2)]
        ev1 = sb("ev1", [1, QB], F32, stC)
        ev2 = sb("ev2", [1, QB], F32, stC)
        mrow = sb("mrow", [1, QB], F32, stC)
        nmrow = sb("nmrow", [1, QB], F32, stC)
        msrow = sb("msrow", [1, QB], F32, stC)
        varow = sb("varow", [1, QB], F32, stC)
        rsrow = sb("rsrow", [1, QB], F32, stC)
        nbrow = sb("nbrow", [1, QB], F32, stC)
        bcr = sb("bcr", [128, QB], F32, stC)
        bcb = sb("bcb", [128, QB], F32, stC)
        lf2 = sb("lf2", [128, QB], F32, stC)
        stCp = contextlib.ExitStack()
        PW = [ps(f"PW{i}", [128, QB], stCp) for i in range(8)]

        dC = S("dC"); vCx = S("vCx")
        dCx = S("dCx")
        dCr = [S("dCr0"), S("dCr1")]
        sCq = S("sCq"); tCm = S("tCm"); sCe = S("sCe"); vCst = S("vCst")
        sCsq = S("sCsq"); vCrs = S("vCrs"); tCb = S("tCb"); sCb = S("sCb"); vCln = S("vCln")

        for fc in range(FC):
            p.dma("scalar", x2Tb[:, fc, :], xT[128 * fc:128 * (fc + 1), :], dCx)
        for fc in range(FC):
            rb = fc % 2
            k, dsub = divmod(fc, 2)
            p.wait("gpsimd", ccA)
            p.wait("gpsimd", dG, dGcopy_mark)
            if fc >= 2:
                p.wait("gpsimd", vCx)
            p.dma("gpsimd", rst[rb][:], rs_out[k][128 * dsub:128 * (dsub + 1), :], dCr[rb])
            if fc == 0:
                p.wait("vector", dCx)
            p.wait("vector", dCr[rb])
            p.op("vector", lambda e, fc=fc, rb=rb:
                 e.tensor_tensor(x2Tb[:, fc, :], x2Tb[:, fc, :], rst[rb][:], ALU.add), inc=vCx, drain=True)

        for tb in range(tbn):
            TB = slice(QB * tb, QB * (tb + 1))
            if taps:
                p.wait("gpsimd", vCx)
                for fc in range(FC):
                    p.dma("gpsimd", x2tap[128 * fc:128 * (fc + 1), TB], x2Tb[:, fc, TB], dG)
            for fc in range(FC):
                qb2 = fc % 2
                p.wait("scalar", vCx)
                if fc >= 2:
                    p.wait("scalar", tCm)
                p.op("scalar", lambda e, fc=fc, qb2=qb2, TB=TB: e.activation(sq2[qb2][:], x2Tb[:, fc, TB], AF.Square), inc=sCq)
                p.wait("tensor", sCq)
                p.op("tensor", lambda e, fc=fc, TB=TB:
                     e.matmul(PW[0][0:1, :], onesf[:], x2Tb[:, fc, TB], start=(fc == 0), stop=(fc == FC - 1)))
                p.op("tensor", lambda e, fc=fc, qb2=qb2:
                     e.matmul(PW[1][0:1, :], onesf[:], sq2[qb2][:], start=(fc == 0), stop=(fc == FC - 1)),
                     inc=tCm)
            p.wait("scalar", tCm)
            p.op("scalar", lambda e: e.copy(ev1[:], PW[0][0:1, :]))
            p.op("scalar", lambda e: e.copy(ev2[:], PW[1][0:1, :]), inc=sCe)
            p.wait("vector", sCe)
            p.op("vector", lambda e: e.tensor_single_scalar(mrow[:], ev1[:], 1.0 / DM, ALU.mult), drain=True)
            p.op("vector", lambda e: e.tensor_single_scalar(nmrow[:], mrow[:], -1.0, ALU.mult), drain=True)
            p.op("vector", lambda e: e.tensor_single_scalar(msrow[:], ev2[:], 1.0 / DM, ALU.mult), drain=True)
            p.op("vector", lambda e: e.tensor_tensor(varow[:], mrow[:], nmrow[:], ALU.mult), drain=True)
            p.op("vector", lambda e: e.tensor_tensor(varow[:], msrow[:], varow[:], ALU.add), drain=True)
            p.op("vector", lambda e: e.tensor_scalar(varow[:], varow[:], E1, E2, ALU.add, ALU.add), inc=vCst, drain=True)
            p.wait("scalar", vCst)
            p.op("scalar", lambda e: e.activation(varow[:], varow[:], AF.Sqrt), inc=sCsq)
            p.wait("vector", sCsq)
            p.op("vector", lambda e: e.reciprocal(rsrow[:], varow[:]), drain=True)
            p.op("vector", lambda e: e.tensor_tensor(nbrow[:], nmrow[:], rsrow[:], ALU.mult), inc=vCrs, drain=True)
            p.wait("tensor", vCrs)
            p.op("tensor", lambda e: e.matmul(PW[2][:], ones1f[:], rsrow[:], start=True, stop=True))
            p.op("tensor", lambda e: e.matmul(PW[3][:], ones1f[:], nbrow[:], start=True, stop=True), inc=tCb)
            p.wait("scalar", tCb)
            p.op("scalar", lambda e: e.copy(bcr[:], PW[2][:]))
            p.op("scalar", lambda e: e.copy(bcb[:], PW[3][:]), inc=sCb)
            p.wait("vector", sCb)
            for fc in range(FC):
                p.op("vector", lambda e, fc=fc, TB=TB: e.tensor_tensor(lf2[:], x2Tb[:, fc, TB], bcr[:], ALU.mult), drain=True)
                p.op("vector", lambda e, fc=fc: e.tensor_tensor(lf2[:], lf2[:], bcb[:], ALU.add), drain=True)
                p.op("vector", lambda e, fc=fc, TB=TB: e.tensor_copy(ln2h[:, fc, TB], lf2[:]), drain=True)
                p.op("vector", lambda e, fc=fc, TB=TB: e.tensor_tensor(ln2l[:, fc, TB], lf2[:], ln2h[:, fc, TB], ALU.subtract),
                     inc=vCln, drain=True)

        p.barrier("C")
        stCp.close()

        stD = contextlib.ExitStack()
        ident = sb("ident", [128, 128], F32, stD)
        gI = S("gI")
        p.op("gpsimd", lambda e: e.iota(ident[:], pattern=[[1, 128]], base=0,
                                        channel_multiplier=-1, allow_small_or_imprecise_dtypes=True), inc=gI)
        p.wait("vector", gI)
        p.op("vector", lambda e: e.tensor_single_scalar(ident[:], ident[:], 0.0, ALU.is_equal), inc=vZ, drain=True)
        identReady = vZ.n
        w1t = [sb(f"w1t{i}", [128, FC, 128], BF16, stD) for i in range(2)]
        hfb = [sb(f"hfb{i}", [128, rows], F32, stD) for i in range(2)]
        hhb = [sb(f"hhb{i}", [128, rows], BF16, stD) for i in range(2)]
        hlb = [sb(f"hlb{i}", [128, rows], BF16, stD) for i in range(2)]
        stD1p = contextlib.ExitStack()
        PH = [ps(f"PH{i}", [128, rows], stD1p) for i in range(2)]

        dD = S("dD"); tDh = S("tDh"); sDg = S("sDg"); vDs = S("vDs")
        dD1p = [S("dD10"), S("dD11")]
        tDh_h = []; sDg_h = []; vDs_h = []
        p.wait("scalar", gQz, gQzFinal)
        if taps:
            p.wait("gpsimd", gQz, gQzFinal)
            p.dma("gpsimd", q1otap[:, :], q1_out[:, :], dG)
        for fo in range(nfo):
            fb = fo % 2
            if fo >= 2:
                p.wait("scalar", tDh, tDh_h[fo - 2])
            p.dma("scalar", w1t[fb][:, :, :],
                  q1_out[:, 128 * fo:128 * (fo + 1)].rearrange("(a c) d -> c a d", c=128), dD1p[fb])
            p.wait("tensor", dD1p[fb])
            if fo == 0:
                p.wait("tensor", vCln, vCln.n)
            if fo >= 2:
                p.wait("tensor", sDg, sDg_h[fo - 2])
            for fc in range(FC):
                for tb in range(tbn):
                    TB = slice(QB * tb, QB * (tb + 1))
                    p.op("tensor", lambda e, fb=fb, fc=fc, TB=TB:
                         e.matmul(PH[fb][:, TB], w1t[fb][:, fc], ln2h[:, fc, TB], start=(fc == 0), stop=False))
                    p.op("tensor", lambda e, fb=fb, fc=fc, TB=TB:
                         e.matmul(PH[fb][:, TB], w1t[fb][:, fc], ln2l[:, fc, TB], start=False, stop=(fc == FC - 1)),
                         inc=(tDh if (fc == FC - 1 and tb == tbn - 1) else None))
            tDh_h.append(tDh.n)
            p.wait("scalar", tDh)
            if fo >= 2:
                p.wait("scalar", vDs, vDs_h[fo - 2])
            p.op("scalar", lambda e, fb=fb, fo=fo:
                 e.activation(hfb[fb][:], PH[fb][:], AF.Gelu, bias=b1c[:, fo:fo + 1]), inc=sDg)
            sDg_h.append(sDg.n)
            p.wait("vector", sDg)
            p.wait("vector", dG, dG.n)
            p.op("vector", lambda e, fb=fb: e.tensor_copy(hhb[fb][:], hfb[fb][:]), drain=True)
            p.op("vector", lambda e, fb=fb: e.tensor_tensor(hlb[fb][:], hfb[fb][:], hhb[fb][:], ALU.subtract),
                 inc=vDs, drain=True)
            vDs_h.append(vDs.n)
            p.wait("gpsimd", vDs)
            p.dma("gpsimd", hsp[0, fo, :, :], hhb[fb][:], dG)
            p.dma("gpsimd", hsp[1, fo, :, :], hlb[fb][:], dG)

        p.wait("gpsimd", dG, dG.n)
        p.barrier("D1")
        stD1p.close()
        w2t = [sb(f"w2t{i}", [128, 1024], BF16, stD) for i in range(2)]
        hht = [sb(f"hht{i}", [128, QB], BF16, stD) for i in range(2)]
        hlt = [sb(f"hlt{i}", [128, QB], BF16, stD) for i in range(2)]
        tfb = sb("tfb", [128, QB], F32, stD)
        ytt = sb("ytt", [128, QB], F32, stD)
        yot = [sb(f"yot{i}", [128, 128], F32, stD) for i in range(2)]
        dout_hist = []
        stD2p = contextlib.ExitStack()
        PF = [ps(f"PF{i}", [128, QB], stD2p) for i in range(8)]

        tDf = S("tDf"); sDe = S("sDe"); vDy = S("vDy"); tDt = S("tDt"); sDt = S("sDt"); dOut = S("dOut")
        dDhp = [S("dDh0"), S("dDh1")]
        dDwp = [S("dDw0"), S("dDw1")]
        dOutp = [S("dOut0"), S("dOut1")]
        for tb2 in range(tbn):
            TB = slice(QB * tb2, QB * (tb2 + 1))
            for dh in range(2):
                tDf_h = []
                for fo in range(nfo):
                    hb = fo % 2
                    if fo >= 2:
                        p.wait("gpsimd", tDf, tDf_h[fo - 2])
                    p.dma("gpsimd", hht[hb][:], hsp[0, fo, :, TB], dDhp[hb])
                    p.dma("gpsimd", hlt[hb][:], hsp[1, fo, :, TB], dDhp[hb])
                    if fo >= 2:
                        p.wait("scalar", tDf, tDf_h[fo - 2])
                    p.dma("scalar", w2t[hb][:], q2_out[128 * fo:128 * (fo + 1), 1024 * dh:1024 * (dh + 1)], dDwp[hb])
                    p.wait("tensor", dDhp[hb])
                    p.wait("tensor", dDwp[hb])
                    if fo == 0:
                        p.wait("tensor", sDt, sDt.n)
                    for d in range(8):
                        p.op("tensor", lambda e, d=d, hb=hb, fo=fo:
                             e.matmul(PF[d][:], w2t[hb][:, 128 * d:128 * (d + 1)], hht[hb][:], start=(fo == 0), stop=False))
                        p.op("tensor", lambda e, d=d, hb=hb, fo=fo:
                             e.matmul(PF[d][:], w2t[hb][:, 128 * d:128 * (d + 1)], hlt[hb][:], start=False, stop=(fo == nfo - 1)),
                             inc=(tDf if d == 7 else None))
                    tDf_h.append(tDf.n)
                for d in range(8):
                    dd = 8 * dh + d
                    p.wait("vector", tDf)
                    p.op("vector", lambda e, d=d, dd=dd, TB=TB:
                         e.scalar_tensor_tensor(ytt[:], PF[d][:], b2c[:, dd:dd + 1], x2Tb[:, dd, TB], ALU.add, ALU.add),
                         inc=vDy, drain=True)
                    for s4 in range(QB // 128):
                        yb = len(dout_hist) % 2
                        p.wait("tensor", vDy)
                        if not dout_hist:
                            p.wait("tensor", vZ, identReady)
                        if dout_hist:
                            p.wait("tensor", sDt)
                        p.op("tensor", lambda e, d=d, s4=s4:
                             e.transpose(PF[d][:, 0:128], ytt[:, 128 * s4:128 * (s4 + 1)], ident[:]), inc=tDt)
                        p.wait("scalar", tDt)
                        p.wait("scalar", dOutp[yb])
                        p.op("scalar", lambda e, d=d, yb=yb: e.copy(yot[yb][:], PF[d][:, 0:128]), inc=sDt)
                        p.wait("gpsimd", sDt)
                        p.dma("gpsimd", y[QB * tb2 + 128 * s4:QB * tb2 + 128 * (s4 + 1), 128 * dd:128 * (dd + 1)],
                              yot[yb][:], dOutp[yb])
                        dout_hist.append(0)
        p.wait("gpsimd", dOutp[0])
        p.wait("gpsimd", dOutp[1])
        stD2p.close()
        stD.close()
        stC.close()

        p.emit(block)
    return nc


def _rope_tables(seq):
    d = np.arange(128)
    i = d % 64
    om = np.exp(-np.log(10000.0) * i / 64.0)
    omp = om * np.where(d < 64, -1.0, 1.0)
    omh_ = omp.astype(np.float32)
    oml_ = (omp - omh_.astype(np.float64)).astype(np.float32)
    t = np.arange(seq, dtype=np.float64)
    phase = omp[:, None] * t[None, :]
    k2s_ = np.round(phase / np.pi)
    sgs_ = np.where(np.abs(k2s_) % 2 == 1, -1.0, 1.0).astype(np.float32)
    k2c_ = np.round((phase + np.pi / 2) / np.pi)
    sgc_ = np.where(np.abs(k2c_) % 2 == 1, -1.0, 1.0).astype(np.float32)
    return (k2s_.astype(np.float32), sgs_, k2c_.astype(np.float32), sgc_,
            omh_.reshape(128, 1), oml_.reshape(128, 1))


def kernel(L0_attn_hidden_states, L0_attn_wq, L0_attn_wk, L0_attn_wv, L0_attn_wg,
           L0_attn_wo, L0_attn_norm_w, L0_ffn_w1, L0_ffn_b1, L0_ffn_w2, L0_ffn_b2):
    seq = L0_attn_hidden_states.shape[1]
    rows = seq // NCORES
    xfull = np.ascontiguousarray(np.asarray(L0_attn_hidden_states, dtype=np.float32).reshape(seq, DM))
    xTf = np.ascontiguousarray(xfull.T)
    wq = np.ascontiguousarray(np.asarray(L0_attn_wq, np.float32).T)
    wk = np.ascontiguousarray(np.asarray(L0_attn_wk, np.float32).T)
    wv = np.ascontiguousarray(np.asarray(L0_attn_wv, np.float32).T)
    wg = np.ascontiguousarray(np.asarray(L0_attn_wg, np.float32).T)
    woT = np.ascontiguousarray(np.asarray(L0_attn_wo, np.float32).T)
    w1T = np.ascontiguousarray(np.asarray(L0_ffn_w1, np.float32).T)
    w2T = np.ascontiguousarray(np.asarray(L0_ffn_w2, np.float32).T)
    b1v = np.ascontiguousarray(np.asarray(L0_ffn_b1, np.float32).reshape(DFF // 128, 128).T)
    b2v = np.ascontiguousarray(np.asarray(L0_ffn_b2, np.float32).reshape(FC, 128).T)
    nwv = np.ascontiguousarray(np.asarray(L0_attn_norm_w, np.float32).reshape(HD, 1))
    k2s_, sgs_, k2c_, sgc_, omh_, oml_ = _rope_tables(seq)

    nc = build(seq=seq)
    in_maps = []
    for c in range(NCORES):
        h0 = HPC * c
        wp = np.empty((HPC, 6, DM, HD), np.float32)
        for hh in range(HPC):
            sl = slice(HD * (h0 + hh), HD * (h0 + hh + 1))
            wp[hh, 0] = wq[:, sl]
            wp[hh, 1] = wk[:, sl]
            wp[hh, 2] = wv[:, sl]
            wp[hh, 3] = wg[:, sl]
            wp[hh, 4] = np.concatenate([wp[hh, 0][:, 64:128], wp[hh, 0][:, 0:64]], axis=1)
            wp[hh, 5] = np.concatenate([wp[hh, 1][:, 64:128], wp[hh, 1][:, 0:64]], axis=1)
        in_maps.append({
            "x": xfull,
            "xT": np.ascontiguousarray(xTf[:, rows * c:rows * (c + 1)]),
            "wproj": np.ascontiguousarray(wp),
            "wos_d": np.ascontiguousarray(woT[2 * HD * c:2 * HD * (c + 1), :]),
            "w1s": np.ascontiguousarray(w1T[(DM // NCORES) * c:(DM // NCORES) * (c + 1), :]),
            "w2s": np.ascontiguousarray(w2T[(DFF // NCORES) * c:(DFF // NCORES) * (c + 1), :]),
            "b1": b1v, "b2": b2v, "nw": nwv,
            "k2s": k2s_, "sgs": sgs_, "k2c": k2c_, "sgc": sgc_,
            "omh": omh_, "oml": oml_,
        })
    res = run_bass_kernel_spmd(nc, in_maps, core_ids=list(range(NCORES)))
    out = np.empty((1, seq, DM), np.float32)
    for c in range(NCORES):
        out[0, rows * c:rows * (c + 1), :] = res.results[c]["y"]
    return out
